# version 09
# baseline (speedup 1.0000x reference)
import math
from contextlib import ExitStack

import numpy as np

import concourse.bass as bass
import concourse.mybir as mybir
from concourse.bass_utils import run_bass_kernel_spmd
from concourse.ap import AP
from concourse.bass_core_rust import IndirectOffsetOnAxis

F32 = mybir.dt.float32
BF16 = mybir.dt.bfloat16
U32 = mybir.dt.uint32
A = mybir.AluOpType
AF = mybir.ActivationFunctionType
AX = mybir.AxisListType

T = 2048
D = 1024
NTT = T // 128
NFC = D // 128
NEG = -1.0e30

ENGINES = ("sync", "scalar", "vector", "tensor", "gpsimd")


class Tok(tuple):
    __slots__ = ()


class Prog:

    def __init__(self):
        self.ops = []
        self.cnt = {}

    def emit(self, eng, fn, after=(), inc=()):
        waits = [t for t in after if t is not None]
        incs = list(inc)
        self.ops.append((eng, fn, waits, incs))
        tok = None
        for s, amt in incs:
            self.cnt[s] = self.cnt.get(s, 0) + amt
            if tok is None:
                tok = Tok((s, self.cnt[s]))
        return tok

    def dma(self, queue, out, in_, after=(), sem="dma", **kw):
        return self.emit(queue, lambda e: e.dma_start(out=out, in_=in_, **kw),
                         after=after, inc=[(sem, 16)])

    def ten(self, fn, after=(), sem="ts"):
        return self.emit("tensor", fn, after=after, inc=[(sem, 1)])

    def vec(self, fn, after=(), sem="vs"):
        return self.emit("vector", fn, after=after, inc=[(sem, 1)])

    def act(self, fn, after=(), sem="ss"):
        return self.emit("scalar", fn, after=after, inc=[(sem, 1)])

    def gp(self, fn, after=(), sem="gs"):
        return self.emit("gpsimd", fn, after=after, inc=[(sem, 1)])

    def barrier_tok(self, sem):
        return Tok((sem, self.cnt.get(sem, 0)))

    def replay(self, nc, block, sems):
        per = {e: [] for e in ENGINES}
        for op in self.ops:
            per[op[0]].append(op)

        def mk(ename):
            def body(eng):
                seen = {}
                for _, fn, waits, incs in per[ename]:
                    for s, v in waits:
                        if seen.get(s, -1) < v:
                            eng.wait_ge(sems[s], v)
                            seen[s] = v
                    inst = fn(eng)
                    for s, amt in incs:
                        inst = inst.then_inc(sems[s], amt)
            return body

        for ename in ENGINES:
            getattr(block, ename)(mk(ename))



def bf16x3_mm(p, psum_ap, lhs_hi, lhs_lo, rhs_hi, rhs_lo, after=(), start=True,
              stop=True, sem="ts", tile_position=None):
    p.emit("tensor", lambda e: e.matmul(psum_ap, lhs_hi, rhs_hi, start=start,
                                        stop=False, tile_position=tile_position),
           after=after, inc=[])
    p.emit("tensor", lambda e: e.matmul(psum_ap, lhs_hi, rhs_lo, start=False,
                                        stop=False, tile_position=tile_position),
           inc=[])
    return p.emit("tensor", lambda e: e.matmul(psum_ap, lhs_lo, rhs_hi,
                                               start=False, stop=stop,
                                               tile_position=tile_position),
                  inc=[(sem, 1)])


def round_half_even(p, v_out, v_in, tmp_rem, tmp_bit, tmp_par, nbits=8,
                    after=(), out_dtype_note=None):
    half = float(2 ** (nbits - 1))
    t = p.vec(lambda e: e.tensor_single_scalar(v_out, v_in, half, A.add), after=after)
    t = p.vec(lambda e: e.tensor_copy(tmp_rem, v_out), after=[t])
    for i in range(nbits):
        b = float(2 ** (nbits - 1 - i))
        t = p.vec(lambda e, b=b: e.tensor_single_scalar(tmp_bit, tmp_rem, b, A.is_ge), after=[t])
        t = p.vec(lambda e, b=b: e.scalar_tensor_tensor(tmp_rem, tmp_bit, -b, tmp_rem, A.mult, A.add), after=[t])
    t = p.vec(lambda e: e.scalar_tensor_tensor(tmp_par, tmp_rem, 0.5, tmp_bit, A.is_equal, A.mult), after=[t])
    t = p.vec(lambda e: e.tensor_single_scalar(tmp_bit, tmp_rem, 0.5, A.is_gt), after=[t])
    t = p.vec(lambda e: e.tensor_add(tmp_par, tmp_par, tmp_bit), after=[t])
    t = p.vec(lambda e: e.tensor_sub(v_out, v_out, tmp_rem), after=[t])
    t = p.vec(lambda e: e.tensor_add(v_out, v_out, tmp_par), after=[t])
    t = p.vec(lambda e: e.tensor_single_scalar(v_out, v_out, -half, A.add), after=[t])
    return t



SEM_NAMES = ("dma",
             "dmb",
             "dmg",
             "dmo",
             "ts",
             "vs",
             "ss",
             "gs",
             )


class Ctx:
    def __init__(self, nc, prog, tiles, sems, psum, dram=None):
        self.nc = nc
        self.p = prog
        self.t = tiles
        self.sems = sems
        self.psum = psum
        self.dram = dram or {}

    def bank32(self, b, cols=512, off=0):
        return self.psum[:, b * 512 + off:b * 512 + off + cols]

    def bank16(self, b, cols=1024, off=0):
        full = self.psum[:, b * 512:(b + 1) * 512].bitcast(BF16)
        return full[:, off:off + cols]


def alloc_tiles(nc, es):
    t = {}

    def sb(name, shape, dtype):
        t[name] = es.enter_context(nc.sbuf_tensor(name, shape, dtype))
        return t[name]

    sb("ft_hi", [128, 8 * T], BF16)
    sb("ft_lo", [128, 8 * T], BF16)
    sb("idn", [128, 128], BF16)
    sb("idn32", [128, 128], F32)
    sb("cols", [128, 512], F32)
    sb("icol", [128, 16], U32)
    sb("xa", [128, 1024], F32)
    sb("xb", [128, 1024], F32)
    sb("s0", [128, 1024], F32)
    sb("d0", [128, 1024], F32)
    sb("n0", [128, 1024], F32)
    sb("nh", [128, 1024], BF16)
    sb("nl", [128, 1024], BF16)
    sb("sqd", [128, 1024], F32)
    sb("yst", [128, 1024], F32)
    sb("AR32", [128, 6144], F32)
    sb("AR16", [128, 41984], BF16)
    return t


def tok_rows(page_ap, tt, w=1024):
    return page_ap[tt * 128:(tt + 1) * 128, :]


def ft_chunk(tile, c, c0=0, w=T):
    return tile[:, c * T + c0:c * T + c0 + w]


def make_ident(ctx):
    p = ctx.p
    for nm in ("idn", "idn32"):
        tile = ctx.t[nm]
        tk = p.gp(lambda e, tile=tile: e.memset(tile[:, :], 0.0))
        p.gp(lambda e, tile=tile: e.affine_select(
            out=tile[:, :], in_=tile[:, :], compare_op=A.not_equal, fill=1.0,
            base=0, pattern=[[-1, 128]], channel_multiplier=1), after=[tk])
    ctx.ones_row = ctx.t["cols"][0:1, 100:228]
    ctx.ones_row32 = ctx.t["cols"][32:33, 100:228]
    p.vec(lambda e: e.memset(ctx.ones_row, 1.0))
    p.vec(lambda e: e.memset(ctx.ones_row32, 1.0))
    ctx.ident_tok = p.barrier_tok("gs")
    ctx.ones_tok = p.barrier_tok("vs")
    return ctx.ident_tok



class EpiPipe:

    def __init__(self, ctx, x_page, out_page, ft_pair, banks=(6, 7), gb=None,
                 eps=1e-5, skip_ln=False):
        self.c = ctx
        self.x_page = x_page
        self.out_page = out_page
        self.ft = ft_pair
        self.banks = banks
        self.gb = gb
        self.eps = eps
        self.skip_ln = skip_ln
        self.war_x = {0: None, 1: None}
        self.war_s = None
        self.war_n = []
        self.war_split = None
        self.war_bank = None
        self.out_toks = []
        self.ft_toks = []

    def tile(self, tt, y_src, y_tok, y_is_add_of=None):
        c, p = self.c, self.c.p
        tl = c.t
        s0, d0, n0 = tl["s0"][:, :], tl["d0"][:, :], tl["n0"][:, :]
        cols = tl["cols"]
        m_col, v_col, r_col = cols[:, 0:1], cols[:, 1:2], cols[:, 2:3]
        par = tt % 2
        if y_is_add_of is None:
            xs = (tl["xa"] if par == 0 else tl["xb"])[:, :]
            lx = p.dma("sync", xs, self.x_page[tt * 128:(tt + 1) * 128, :],
                       after=[self.war_x[par]], sem="xlA" if par == 0 else "xlB")
            t = p.vec(lambda e: e.tensor_add(s0, y_src, xs),
                      after=[y_tok, lx, self.war_s])
            self.war_x[par] = t
        else:
            t = p.vec(lambda e: e.tensor_add(s0, y_src, y_is_add_of),
                      after=[y_tok, self.war_s])
            self.war_x[par] = t
        if self.skip_ln:
            tn = p.vec(lambda e: e.tensor_copy(n0, s0), after=[t] + self.war_n)
        else:
            t = p.vec(lambda e: e.reduce_sum(m_col, s0, axis=AX.X, negate=True), after=[t])
            t = p.vec(lambda e: e.tensor_single_scalar(m_col, m_col, 1.0 / 1024.0, A.mult), after=[t])
            t = p.vec(lambda e: e.tensor_single_scalar(d0, s0, m_col, A.add), after=[t])
            tsq = p.act(lambda e: e.activation(tl["sqd"][:, :], d0, AF.Square,
                                               accum_out=v_col), after=[t])
            tv = p.vec(lambda e: e.tensor_scalar(r_col, v_col, 1.0 / 1024.0,
                                                 self.eps, A.mult, A.add), after=[tsq])
            tv = p.act(lambda e: e.activation(r_col, r_col, AF.Sqrt), after=[tv])
            tv = p.vec(lambda e: e.reciprocal(r_col, r_col), after=[tv])
            if self.gb is None:
                tn = p.vec(lambda e: e.tensor_single_scalar(n0, d0, r_col, A.mult),
                           after=[tv] + self.war_n)
            else:
                g_b, b_b = self.gb
                tn = p.vec(lambda e: e.scalar_tensor_tensor(n0, d0, r_col, g_b,
                                                            A.mult, A.mult),
                           after=[tv] + self.war_n)
                tn = p.vec(lambda e: e.tensor_add(n0, n0, b_b), after=[tn])
        self.war_s = tn
        st = p.dma("scalar", self.out_page[tt * 128:(tt + 1) * 128, :], n0,
                   after=[tn], sem="dmo")
        self.out_toks.append(st)
        n_free = [st]
        if self.ft is not None:
            nh, nl = tl["nh"][:, :], tl["nl"][:, :]
            t_hi = p.vec(lambda e: e.tensor_copy(nh, n0), after=[tn, self.war_split])
            t_lo = p.vec(lambda e: e.tensor_sub(nl, n0, nh), after=[t_hi])
            n_free.append(t_lo)
            tks = []
            for (tile, bank, rdy) in ((nh, self.banks[0], t_hi),
                                      (nl, self.banks[1], t_lo)):
                tk = None
                for cc in range(8):
                    tk = p.ten(lambda e, tile=tile, cc=cc, bank=bank: e.transpose(
                        c.bank16(bank, 128, cc * 128),
                        tile[:, cc * 128:(cc + 1) * 128], tl["idn"][:, :]),
                        after=[rdy, self.war_bank, c.ident_tok] if cc == 0 else ())
                tks.append(tk)
            self.war_split = tks[1]
            evs = []
            for (bank, dst, tk) in ((self.banks[0], self.ft[0], tks[0]),
                                    (self.banks[1], self.ft[1], tks[1])):
                dst_ap = AP(dst.tensor, dst.offset + tt * 128,
                            [list(dst.ap[0]), [T, 8], [1, 128]])
                evs.append(p.act(lambda e, bank=bank, dst_ap=dst_ap: e.copy(
                    dst_ap, c.bank16(bank, 1024)), after=[tk]))
            self.war_bank = evs[-1]
            self.ft_toks.append(evs)
        self.war_n = n_free
        return st




def emit_bcast_row(ctx, dst_ap, src_row_ap, after=(), bank=4):
    p = ctx.p
    n = src_row_ap.shape[-1]
    tm = p.ten(lambda e: e.matmul(ctx.bank32(bank, n), ctx.ones_row,
                                  src_row_ap, start=True, stop=True),
               after=list(after) + [ctx.ones_tok])
    return p.vec(lambda e: e.tensor_copy(dst_ap, ctx.bank32(bank, n)), after=[tm])


def emit_clip(p, out_ap, in_ap, lo, hi, after=()):
    return p.vec(lambda e: e.tensor_scalar(out_ap, in_ap, float(lo), float(hi),
                                           A.max, A.min), after=after)



def emit_l0_ffn(ctx, x_page, wT_dram, out_page, ft_out, banks_tr=(6, 7)):
    p = ctx.p
    tl = ctx.t
    AR32, AR16 = tl["AR32"], tl["AR16"]
    cols = tl["cols"]
    wst = AR32[:, 0:1024]
    zt = AR32[:, 1024:2048]
    rt = AR32[:, 2048:3072]
    q0 = AR32[:, 3072:4096]
    q1 = AR32[:, 4096:5120]
    q2 = AR32[:, 5120:6144]
    wq = [AR16[:, c * 1024:(c + 1) * 1024] for c in range(8)]
    aq = AR16[:, 8192:9216]
    aqT = AR16[:, 9216:10240]

    tsum = None
    ld_prev = None
    for c in range(8):
        ld = p.dma("sync", wst, wT_dram[c * 128:(c + 1) * 128, :],
                   after=[tsum], sem="l0fw")
        tsum = p.vec(lambda e, c=c: e.reduce_sum(cols[:, 8 + c:9 + c], wst,
                                                 axis=AX.X, apply_absolute_value=True),
                     after=[ld, tsum])
    tsum = p.vec(lambda e: e.reduce_sum(cols[:, 16:17], cols[:, 8:16], axis=AX.X),
                 after=[tsum])
    ones_col = cols[:, 17:18]
    t1 = p.vec(lambda e: e.memset(ones_col, 1.0), after=[])
    tm = p.ten(lambda e: e.matmul(ctx.bank32(4, 1)[0:1, :], ones_col, cols[:, 16:17],
                                  start=True, stop=True), after=[tsum, t1])
    sc = cols[:, 18:19]
    tm = p.vec(lambda e: e.tensor_scalar(sc[0:1, :], ctx.bank32(4, 1)[0:1, :],
                                         1.0 / (1024.0 * 1024.0), 1e-8, A.mult, A.add),
               after=[tm])
    tws = p.vec(lambda e: e.tensor_copy(cols[0:1, 19:20], sc[0:1, :]), after=[tm])
    tm = p.vec(lambda e: e.reciprocal(sc[0:1, :], sc[0:1, :]), after=[tws])
    tb = emit_bcast_row(ctx, sc, sc[0:1, :], after=[tm])
    tbw = emit_bcast_row(ctx, cols[:, 19:20], cols[0:1, 19:20], after=[tws, tb])

    wq_tok = None
    for c in range(8):
        ld = p.dma("sync", wst, wT_dram[c * 128:(c + 1) * 128, :],
                   after=[wq_tok if wq_tok is not None else tsum], sem="l0fw")
        t = p.vec(lambda e, c=c: e.tensor_single_scalar(zt, wst, sc, A.mult),
                  after=[ld, tb, wq_tok])
        t = emit_clip(p, zt, zt, -2.0, 2.0, after=[t])
        t = round_half_even(p, rt, zt, q0, q1, q2, nbits=3, after=[t])
        t = p.vec(lambda e, c=c: e.tensor_scalar(wq[c], rt, -1.0, 1.0,
                                                 A.max, A.min), after=[t])
        wq_tok = t

    epi = EpiPipe(ctx, x_page, out_page, ft_out, banks=banks_tr)
    war_aq = None
    war_aqT = None
    war_x = None
    war_y = None
    for tt in range(NTT):
        xq = (tl["xa"] if tt % 2 == 0 else tl["xb"])[:, :]
        lx = p.dma("sync", xq, x_page[tt * 128:(tt + 1) * 128, :],
                   after=[epi.war_x[tt % 2], war_x], sem="xlA" if tt % 2 == 0 else "xlB")
        asc = cols[:, 20:21]
        t = p.vec(lambda e, xq=xq: e.reduce_max(asc, xq, axis=AX.X,
                                                apply_absolute_value=True), after=[lx])
        t = p.vec(lambda e: e.tensor_single_scalar(asc, asc, 1e-8, A.add), after=[t])
        tg = p.vec(lambda e: e.tensor_copy(cols[:, 21:22], asc), after=[t])
        t = p.vec(lambda e: e.reciprocal(asc, asc), after=[tg])
        t = p.vec(lambda e: e.tensor_single_scalar(asc, asc, 127.0, A.mult), after=[t])
        t = p.vec(lambda e, xq=xq: e.tensor_single_scalar(zt, xq, asc, A.mult), after=[t, war_aq, wq_tok])
        war_x = t
        t = emit_clip(p, zt, zt, -128.0, 127.0, after=[t])
        t = round_half_even(p, rt, zt, q0, q1, q2, nbits=8, after=[t])
        taq = p.vec(lambda e: e.tensor_copy(aq, rt), after=[t])
        tk = None
        for cc in range(8):
            tk = p.ten(lambda e, cc=cc: e.transpose(
                ctx.bank16(5, 128, cc * 128), aq[:, cc * 128:(cc + 1) * 128],
                tl["idn"][:, :]), after=[taq, war_aqT, tm, ctx.ident_tok] if cc == 0 else ())
        war_aq = tk
        tev = p.act(lambda e: e.copy(aqT, ctx.bank16(5, 1024)), after=[tk])
        war_aqT = tev
        mmt = None
        for osup in range(2):
            for cc in range(8):
                mmt = p.ten(lambda e, osup=osup, cc=cc: e.matmul(
                    ctx.bank32(osup, 512),
                    aqT[:, cc * 128:(cc + 1) * 128],
                    wq[cc][:, osup * 512:(osup + 1) * 512],
                    start=(cc == 0), stop=(cc == 7)),
                    after=[tev, wq_tok, tbw, war_y] if (cc == 0 and osup == 0) else ())
        scol = cols[:, 22:23]
        t = p.vec(lambda e: e.tensor_single_scalar(scol, cols[:, 21:22], cols[:, 19:20], A.mult), after=[tg, mmt, tbw])
        t = p.vec(lambda e: e.tensor_single_scalar(scol, scol, 1.0 / 127.0, A.mult), after=[t])
        ty = p.vec(lambda e: e.tensor_single_scalar(tl["yst"][:, :], ctx.bank32(0, 1024), scol, A.mult), after=[t])
        war_y = ty
        epi.tile(tt, tl["yst"][:, :], ty, y_is_add_of=xq)
    return epi


def emit_masks(ctx):
    p = ctx.p
    cols = ctx.t["cols"]
    md = cols[:, 256:384]
    me = cols[:, 384:512]
    t0 = p.gp(lambda e: e.memset(md, 0.0))
    t0 = p.gp(lambda e: e.affine_select(out=md, in_=md, compare_op=A.is_ge,
                                        fill=NEG, base=0, pattern=[[-1, 128]],
                                        channel_multiplier=1), after=[t0])
    t1 = p.gp(lambda e: e.memset(me, 0.0), after=[t0])
    t1 = p.gp(lambda e: e.affine_select(out=me, in_=me, compare_op=A.is_ge,
                                        fill=NEG, base=-1, pattern=[[1, 128]],
                                        channel_multiplier=-1), after=[t1])
    ctx.mdiag = md
    ctx.medge = me
    ctx.mask_tok = p.barrier_tok("gs")


def emit_wsplit(ctx, pieces, after=(), war=None):
    p = ctx.p
    last = war
    for (st, dh, dl, sc) in pieces:
        if sc == 1.0:
            t = p.vec(lambda e, st=st, dh=dh: e.tensor_copy(dh, st),
                      after=list(after) + [last])
            t = p.vec(lambda e, st=st, dh=dh, dl=dl: e.tensor_sub(dl, st, dh),
                      after=[t])
        else:
            t = p.vec(lambda e, st=st, dh=dh, sc=sc: e.tensor_single_scalar(
                dh, st, sc, A.mult), after=list(after) + [last])
            t = p.vec(lambda e, st=st, dh=dh, dl=dl, sc=sc: e.scalar_tensor_tensor(
                dl, st, sc, dh, A.mult, A.subtract), after=[t])
        last = t
    return last


class A16:
    def __init__(self, ctx, ncols=41984):
        self.t = ctx.t["AR16"]
        self.off = 0
        self.ncols = ncols

    def take(self, cols):
        assert self.off + cols <= self.ncols, (self.off, cols)
        ap = self.t[:, self.off:self.off + cols]
        self.off += cols
        return ap


class A32:
    def __init__(self, ctx, ncols=6144):
        self.t = ctx.t["AR32"]
        self.off = 0
        self.ncols = ncols

    def take(self, cols):
        assert self.off + cols <= self.ncols, (self.off, cols)
        ap = self.t[:, self.off:self.off + cols]
        self.off += cols
        return ap




def emit_projT(ctx, ft, wsl_hi, wsl_lo, wc0, fw, dst_hi, dst_lo, banks,
               after=(), war=None, evac_scale=None, dst32=None):
    p = ctx.p
    ft_hi, ft_lo = ft
    evac_tok = {0: None, 1: None}
    done = None
    for ts in range(4):
        b = banks[ts % 2]
        mm = None
        for c in range(8):
            lh = wsl_hi[c][:, wc0:wc0 + fw]
            ll = wsl_lo[c][:, wc0:wc0 + fw]
            rh = ft_hi[:, c * T + ts * 512: c * T + ts * 512 + 512]
            rl = ft_lo[:, c * T + ts * 512: c * T + ts * 512 + 512]
            aft = (list(after) + [war, evac_tok[ts % 2]]) if c == 0 else ()
            p.emit("tensor", lambda e, lh=lh, rh=rh, b=b, c=c: e.matmul(
                ctx.bank32(b, 512)[0:fw, :], lh, rh, start=(c == 0), stop=False),
                after=aft, inc=[])
            p.emit("tensor", lambda e, lh=lh, rl=rl, b=b: e.matmul(
                ctx.bank32(b, 512)[0:fw, :], lh, rl, start=False, stop=False), inc=[])
            mm = p.emit("tensor", lambda e, ll=ll, rh=rh, b=b, c=c: e.matmul(
                ctx.bank32(b, 512)[0:fw, :], ll, rh, start=False, stop=(c == 7)),
                inc=[("ts", 1)])
        ps = ctx.bank32(b, 512)[0:fw, :]
        if dst32 is not None:
            d32 = dst32[0:fw, ts * 512:(ts + 1) * 512]
            tok = p.act(lambda e, d32=d32, ps=ps: e.copy(d32, ps), after=[mm])
        else:
            dh = dst_hi[0:fw, ts * 512:(ts + 1) * 512]
            dl = dst_lo[0:fw, ts * 512:(ts + 1) * 512]
            th = p.act(lambda e, dh=dh, ps=ps: e.copy(dh, ps), after=[mm])
            tok = p.vec(lambda e, dl=dl, ps=ps, dh=dh: e.tensor_sub(dl, ps, dh),
                        after=[th])
        evac_tok[ts % 2] = tok
        done = tok
    return done, (evac_tok[0], evac_tok[1])


def emit_l1_attn(ctx, ft, wqkv_dram, wqkr_dram, wvr_dram, wproj_dram,
                 x_page, out_page, ft_out):
    p = ctx.p
    tl = ctx.t
    a16 = A16(ctx)
    a32 = A32(ctx)

    qt_hi, qt_lo = a16.take(T), a16.take(T)
    kt_hi, kt_lo = a16.take(T), a16.take(T)
    vt_hi, vt_lo = a16.take(T), a16.take(T)
    v_hi, v_lo = a16.take(T), a16.take(T)
    at_h2 = (a16.take(1280), a16.take(1280))
    at_l2 = (a16.take(1280), a16.take(1280))
    ats = a16.take(2048)
    aos = a16.take(512)
    aorh = [a16.take(1024) for _ in range(2)]
    aorl = [a16.take(1024) for _ in range(2)]
    wsl_hi = [a16.take(384) for _ in range(8)]
    wsl_lo = [a16.take(384) for _ in range(8)]
    AR16 = tl["AR16"]
    wp_hi = [AR16[:, c * 1024:(c + 1) * 1024] for c in range(8)]
    wp_lo = [AR16[:, 8192 + c * 1024:8192 + (c + 1) * 1024] for c in range(8)]
    wstage = a32.take(1024)
    exps2 = (a32.take(1280), a32.take(1280))

    cols = tl["cols"]
    rmaxs = (cols[:, 3:4], cols[:, 8:9])
    rsums = (cols[:, 4:5], cols[:, 9:10])
    rrecs = (cols[:, 5:6], cols[:, 12:13])
    aoh, aol = ctx.dram["aoh"], ctx.dram["aol"]

    def s1(pr, nk_win, hd_q, qi, hx, slice_tok):
        u = {"qi": qi, "hx": hx, "par": (qi * 2 + hx) % 2}
        par = u["par"]
        ks0 = max(0, qi - (nk_win - 1))
        nk = qi - ks0 + 1
        kw = nk * 128
        u["ks0"], u["nk"] = ks0, nk
        hbase = hd_q[hx]
        u["hbase"] = hbase
        hslice = slice(hbase[0], hbase[0] + hbase[1])
        sb = (0, 1, 2) if par == 0 else (5, 6, 7)
        u["sb"] = sb
        mm = None
        first = True
        for g0 in range(0, kw, 512):
            gw = min(512, kw - g0)
            bidx = sb[g0 // 512]
            lhsT_h = qt_hi[hslice, qi * 128:(qi + 1) * 128]
            lhsT_l = qt_lo[hslice, qi * 128:(qi + 1) * 128]
            rh = kt_hi[hslice, ks0 * 128 + g0: ks0 * 128 + g0 + gw]
            rl = kt_lo[hslice, ks0 * 128 + g0: ks0 * 128 + g0 + gw]
            aft = [slice_tok, S["trio"][par]] if first else ()
            tp = (hbase[0], 0) if hbase[0] == 96 else None
            p.emit("tensor", lambda e, a=lhsT_h, b=rh, bidx=bidx, tp=tp: e.matmul(
                ctx.bank32(bidx, b.shape[-1]), a, b, start=True, stop=False,
                tile_position=tp), after=aft, inc=[])
            p.emit("tensor", lambda e, a=lhsT_h, b=rl, bidx=bidx, tp=tp: e.matmul(
                ctx.bank32(bidx, b.shape[-1]), a, b, start=False, stop=False,
                tile_position=tp), inc=[])
            mm = p.emit("tensor", lambda e, a=lhsT_l, b=rh, bidx=bidx, tp=tp: e.matmul(
                ctx.bank32(bidx, b.shape[-1]), a, b, start=False, stop=True,
                tile_position=tp), inc=[("ts", 1)])
            first = False

        def bankslice(c0, w):
            b = sb[c0 // 512]
            return ctx.bank32(b, w, c0 % 512)
        tmask = p.vec(lambda e, apx=bankslice((nk - 1) * 128, 128):
                      e.tensor_add(apx, apx, ctx.mdiag), after=[mm, ctx.mask_tok])
        if qi >= nk_win - 1 and nk_win > 1:
            tmask = p.vec(lambda e, apx=bankslice(0, 128):
                          e.tensor_add(apx, apx, ctx.medge), after=[tmask])
        sc_all = ctx.bank32(sb[0], kw)
        tmx = p.vec(lambda e, sc_all=sc_all, par=par: e.reduce_max(
            rmaxs[par], sc_all, axis=AX.X, negate=True), after=[tmask])
        texp = p.act(lambda e, sc_all=sc_all, kw=kw, par=par: e.activation(
            exps2[par][:, 0:kw], sc_all, AF.Exp, bias=rmaxs[par],
            accum_out=rsums[par]), after=[tmx, S["war_exps"][par]])
        S["trio"][par] = texp
        trc = p.vec(lambda e, par=par: e.reciprocal(rrecs[par], rsums[par]),
                    after=[texp])
        th = p.act(lambda e, kw=kw, par=par: e.activation(
            at_h2[par][:, 0:kw], exps2[par][:, 0:kw], AF.Copy,
            scale=rrecs[par]), after=[trc, S["war_at"][par]])
        tl_ = p.vec(lambda e, kw=kw, par=par: e.scalar_tensor_tensor(
            at_l2[par][:, 0:kw], exps2[par][:, 0:kw], rrecs[par],
            at_h2[par][:, 0:kw], A.mult, A.subtract), after=[th])
        S["war_exps"][par] = tl_
        u["tl_"] = tl_
        return u

    def s2(u, ao_chunk, war_aos):
        qi, hx, par = u["qi"], u["hx"], u["par"]
        ks0, nk = u["ks0"], u["nk"]
        hbase = u["hbase"]
        hx64 = hx * 64
        first_pv = True
        tpv = None
        last_t = None
        for gbase in range(0, nk, 4):
            n = min(4, nk - gbase)
            gpar = (gbase // 4) % 2
            t2 = None
            for j in range(n):
                kc = gbase + j
                aft = [u["tl_"]] + ([S["war_b4"], ctx.ident_tok] if j == 0 else [])
                t1 = p.ten(lambda e, kc=kc, j=j, par=par: e.transpose(
                    ctx.bank16(4, 128, j * 256),
                    at_h2[par][:, kc * 128:(kc + 1) * 128], tl["idn"][:, :]),
                    after=aft)
                t2 = p.ten(lambda e, kc=kc, j=j, par=par: e.transpose(
                    ctx.bank16(4, 128, j * 256 + 128),
                    at_l2[par][:, kc * 128:(kc + 1) * 128], tl["idn"][:, :]))
            last_t = t2
            te = p.act(lambda e, gpar=gpar, n=n: e.copy(
                ats[:, gpar * 1024:gpar * 1024 + n * 256],
                ctx.bank16(4, n * 256)), after=[t2, S["war_ats"][gpar]])
            S["war_b4"] = te
            for j in range(n):
                kc = gbase + j
                kcol = (ks0 + kc) * 128
                vsel = slice(hx64, hx64 + 64)
                vh = v_hi[:, kcol:kcol + 128][:, vsel]
                vl = v_lo[:, kcol:kcol + 128][:, vsel]
                ah = ats[:, gpar * 1024 + j * 256:gpar * 1024 + j * 256 + 128]
                al = ats[:, gpar * 1024 + j * 256 + 128:gpar * 1024 + j * 256 + 256]
                pv_first = first_pv
                pv_last = (kc == nk - 1)
                tp = (0, hx64)
                ops = ctx.bank32(3, 128)[hx64:hx64 + 64, :]
                aft = [te] + ([S["war_pv"]] if (pv_first and hx == 0) else [])
                p.emit("tensor", lambda e, vh=vh, ah=ah, tp=tp, ops=ops,
                       st=pv_first: e.matmul(ops, vh, ah, start=st,
                                             stop=False, tile_position=tp),
                       after=aft, inc=[])
                p.emit("tensor", lambda e, vl=vl, ah=ah, tp=tp, ops=ops: e.matmul(
                    ops, vl, ah, start=False, stop=False, tile_position=tp), inc=[])
                tpv = p.emit("tensor", lambda e, vh=vh, al=al, tp=tp, ops=ops,
                             sp=pv_last: e.matmul(ops, vh, al, start=False,
                                                  stop=sp, tile_position=tp),
                             inc=[("ts", 1)])
                first_pv = False
            S["war_ats"][gpar] = tpv
        S["war_at"][par] = last_t
        S["done_pv"] = tpv
        if hx == 1:
            slot = qi % 2
            sh = aos[:, slot * 256:slot * 256 + 128]
            sl = aos[:, slot * 256 + 128:slot * 256 + 256]
            th = p.act(lambda e, sh=sh: e.copy(sh, ctx.bank32(3, 128)),
                       after=[tpv, war_aos[slot]])
            tsub = p.vec(lambda e, sh=sh, sl=sl: e.tensor_sub(
                sl, ctx.bank32(3, 128), sh), after=[th])
            S["war_pv"] = tsub
            S["done_pv"] = tsub
            ro = qi * 1024 + ao_chunk * 128
            w1 = p.dma("gpsimd", AP(aoh, ro * 128, [[128, 128], [1, 128]]),
                       sh, after=[tsub], sem="aowA" if slot == 0 else "aowB")
            w2 = p.dma("gpsimd", AP(aol, ro * 128, [[128, 128], [1, 128]]),
                       sl, after=[tsub], sem="aowA" if slot == 0 else "aowB")
            war_aos[slot] = w2

    def attention(pr, nk_win, hd_q, ao_chunk, slice_tok, war_ao, war_aos):
        units = [(qi, hx) for qi in range(NTT) for hx in range(2)]
        ulist = [s1(pr, nk_win, hd_q, 0, 0, slice_tok)]
        for i in range(len(units)):
            if i + 1 < len(units):
                qi, hx = units[i + 1]
                ulist.append(s1(pr, nk_win, hd_q, qi, hx, slice_tok))
            s2(ulist[i], ao_chunk, war_aos)
        return S["done_pv"]

    slice_free = None
    war_proj = (None, None)
    war_ao = None
    war_aos = {0: None, 1: None}
    S = {"trio": {0: None, 1: None}, "war_exps": {0: None, 1: None},
         "war_at": {0: None, 1: None}, "war_ats": {0: None, 1: None},
         "war_b4": None, "war_pv": None, "done_pv": None}
    for pr in range(4):
        sl_tok = None
        for c in range(8):
            src = AP(wqkv_dram.tensor, c * 128 * 1536 + 64 * 2 * pr,
                     [[1536, 128], [512, 3], [1, 128]])
            ld = p.dma("sync", wstage[:, 0:384], src, after=[slice_free], sem="wsl")
            sl_tok = emit_wsplit(
                ctx, [(wstage[:, 0:128], wsl_hi[c][:, 0:128], wsl_lo[c][:, 0:128], 1.0),
                      (wstage[:, 128:256], wsl_hi[c][:, 128:256], wsl_lo[c][:, 128:256], 0.125),
                      (wstage[:, 256:384], wsl_hi[c][:, 256:384], wsl_lo[c][:, 256:384], 1.0)],
                after=[ld], war=sl_tok)
            slice_free = sl_tok
        tq, _ = emit_projT(ctx, ft, wsl_hi, wsl_lo, 0, 128, qt_hi, qt_lo, (0, 1),
                           after=[sl_tok], war=war_proj[0])
        tk, _ = emit_projT(ctx, ft, wsl_hi, wsl_lo, 128, 128, kt_hi, kt_lo, (0, 1),
                           after=[sl_tok], war=tq)
        tv, _ = emit_projT(ctx, ft, wsl_hi, wsl_lo, 256, 128, vt_hi, vt_lo, (0, 1),
                           after=[sl_tok], war=tk)
        slice_free = tv
        tev = None
        for tt in range(NTT):
            t1 = p.ten(lambda e, tt=tt: e.transpose(
                ctx.bank16(4, 128, 0), vt_hi[:, tt * 128:(tt + 1) * 128],
                tl["idn"][:, :]), after=[tv, tev, ctx.ident_tok])
            t2 = p.ten(lambda e, tt=tt: e.transpose(
                ctx.bank16(4, 128, 128), vt_lo[:, tt * 128:(tt + 1) * 128],
                tl["idn"][:, :]))
            ta = p.act(lambda e, tt=tt: e.copy(v_hi[:, tt * 128:(tt + 1) * 128],
                                               ctx.bank16(4, 128, 0)), after=[t2])
            tev = p.act(lambda e, tt=tt: e.copy(v_lo[:, tt * 128:(tt + 1) * 128],
                                                ctx.bank16(4, 128, 128)), after=[ta])
        war_ao = attention(pr, 9, ((0, 64), (64, 64)), pr, tev, war_ao, war_aos)
        war_proj = (war_ao, war_ao)

    for quad in range(2):
        sl_tok = None
        for c in range(8):
            src = AP(wqkr_dram.tensor, c * 128 * 512 + 128 * quad,
                     [[512, 128], [256, 2], [1, 128]])
            ld = p.dma("sync", wstage[:, 0:256], src, after=[slice_free], sem="wsl")
            sl_tok = emit_wsplit(
                ctx, [(wstage[:, 0:128], wsl_hi[c][:, 0:128], wsl_lo[c][:, 0:128],
                       1.0 / math.sqrt(32.0)),
                      (wstage[:, 128:256], wsl_hi[c][:, 128:256], wsl_lo[c][:, 128:256], 1.0)],
                after=[ld], war=sl_tok)
            slice_free = sl_tok
        tq, _ = emit_projT(ctx, ft, wsl_hi, wsl_lo, 0, 128, qt_hi, qt_lo, (0, 1),
                           after=[sl_tok], war=war_proj[0])
        tkk, _ = emit_projT(ctx, ft, wsl_hi, wsl_lo, 128, 128, kt_hi, kt_lo, (0, 1),
                            after=[sl_tok], war=tq)
        slice_free = tkk
        war_proj = (tkk, tkk)
        for sub in range(2):
            pr = quad * 2 + sub
            svtok = None
            for c in range(8):
                src = AP(wvr_dram.tensor, c * 128 * 512 + 128 * pr,
                         [[512, 128], [1, 128]])
                ld = p.dma("sync", wstage[:, 0:128], src, after=[slice_free], sem="wsl")
                svtok = emit_wsplit(
                    ctx, [(wstage[:, 0:128], wsl_hi[c][:, 256:384],
                           wsl_lo[c][:, 256:384], 1.0)], after=[ld], war=svtok)
                slice_free = svtok
            tv, _ = emit_projT(ctx, ft, wsl_hi, wsl_lo, 256, 128, vt_hi, vt_lo,
                               (0, 1), after=[svtok], war=war_proj[0])
            slice_free = tv
            tev = None
            for tt in range(NTT):
                t1 = p.ten(lambda e, tt=tt: e.transpose(
                    ctx.bank16(4, 128, 0), vt_hi[:, tt * 128:(tt + 1) * 128],
                    tl["idn"][:, :]), after=[tv, tev, ctx.ident_tok])
                t2 = p.ten(lambda e, tt=tt: e.transpose(
                    ctx.bank16(4, 128, 128), vt_lo[:, tt * 128:(tt + 1) * 128],
                    tl["idn"][:, :]))
                ta = p.act(lambda e, tt=tt: e.copy(v_hi[:, tt * 128:(tt + 1) * 128],
                                                   ctx.bank16(4, 128, 0)), after=[t2])
                tev = p.act(lambda e, tt=tt: e.copy(v_lo[:, tt * 128:(tt + 1) * 128],
                                                    ctx.bank16(4, 128, 128)), after=[ta])
            hA = (sub * 64, 32)
            hB = (sub * 64 + 32, 32)
            war_ao = attention(pr, 3, (hA, hB), 4 + pr, tev, war_ao, war_aos)
            war_proj = (war_ao, war_proj[1])

    wp_tok, wst_free = None, None
    for c in range(8):
        ld = p.dma("sync", wstage, wproj_dram[c * 128:(c + 1) * 128, :],
                   after=[wst_free, war_ao if c == 0 else None], sem="wsl")
        wp_tok = emit_wsplit(ctx, [(wstage, wp_hi[c], wp_lo[c], 1.0)],
                             after=[ld, war_ao if c == 0 else None], war=wp_tok)
        wst_free = wp_tok

    ao_barrier = (p.barrier_tok("aowA"), p.barrier_tok("aowB"))
    epi = EpiPipe(ctx, x_page, out_page, ft_out, banks=(6, 7))
    war_y = None
    war_aor = {0: None, 1: None}
    for tt in range(NTT):
        par = tt % 2
        lh = p.dma("sync", aorh[par], AP(aoh,
                                         tt * 1024 * 128,
                                         [[128, 128], [16384, 8], [1, 128]]),
                   after=[war_aor[par], ao_barrier[0], ao_barrier[1]],
                   sem="aorA" if par == 0 else "aorB")
        ll = p.dma("scalar", aorl[par], AP(aol,
                                           tt * 1024 * 128,
                                           [[128, 128], [16384, 8], [1, 128]]),
                   after=[war_aor[par], ao_barrier[0], ao_barrier[1]],
                   sem="aosA" if par == 0 else "aosB")
        mm = None
        for osup in range(2):
            b = osup
            for c in range(8):
                ah = aorh[par][:, c * 128:(c + 1) * 128]
                al = aorl[par][:, c * 128:(c + 1) * 128]
                rh = wp_hi[c][:, osup * 512:(osup + 1) * 512]
                rl = wp_lo[c][:, osup * 512:(osup + 1) * 512]
                aft = [lh, ll, wp_tok, war_y] if (c == 0 and osup == 0) else ()
                p.emit("tensor", lambda e, ah=ah, rh=rh, b=b, c=c: e.matmul(
                    ctx.bank32(b, 512), ah, rh, start=(c == 0), stop=False),
                    after=aft, inc=[])
                p.emit("tensor", lambda e, ah=ah, rl=rl, b=b: e.matmul(
                    ctx.bank32(b, 512), ah, rl, start=False, stop=False), inc=[])
                mm = p.emit("tensor", lambda e, al=al, rh=rh, b=b, c=c: e.matmul(
                    ctx.bank32(b, 512), al, rh, start=False, stop=(c == 7)),
                    inc=[("ts", 1)])
        war_aor[par] = mm
        epi.tile(tt, ctx.bank32(0, 1024), mm)
        war_y = epi.war_s
    return epi


def emit_page_to_ft(ctx, page, ft_pair, banks=(6, 7)):
    p = ctx.p
    tl = ctx.t
    war = {0: None, 1: None}
    war_split = None
    war_bank = None
    done = None
    for tt in range(NTT):
        par = tt % 2
        xs = (tl["xa"] if par == 0 else tl["xb"])[:, :]
        lx = p.dma("sync", xs, page[tt * 128:(tt + 1) * 128, :],
                   after=[war[par]], sem="xlA" if par == 0 else "xlB")
        nh, nl = tl["nh"][:, :], tl["nl"][:, :]
        t_hi = p.vec(lambda e, xs=xs: e.tensor_copy(nh, xs), after=[lx, war_split])
        t_lo = p.vec(lambda e, xs=xs: e.tensor_sub(nl, xs, nh), after=[t_hi])
        war[par] = t_lo
        tks = []
        for (tile, bank, rdy) in ((nh, banks[0], t_hi), (nl, banks[1], t_lo)):
            tk = None
            for cc in range(8):
                tk = p.ten(lambda e, tile=tile, cc=cc, bank=bank: e.transpose(
                    ctx.bank16(bank, 128, cc * 128),
                    tile[:, cc * 128:(cc + 1) * 128], tl["idn"][:, :]),
                    after=[rdy, war_bank, ctx.ident_tok] if cc == 0 else ())
            tks.append(tk)
        war_split = tks[1]
        evs = []
        for (bank, dst, tk) in ((banks[0], ft_pair[0], tks[0]),
                                (banks[1], ft_pair[1], tks[1])):
            dst_ap = AP(dst.tensor, dst.offset + tt * 128,
                        [list(dst.ap[0]), [T, 8], [1, 128]])
            evs.append(p.act(lambda e, bank=bank, dst_ap=dst_ap: e.copy(
                dst_ap, ctx.bank16(bank, 1024)), after=[tk]))
        war_bank = evs[-1]
        done = evs[-1]
    return done


def emit_page_to_ft(ctx, page, ft_pair, banks=(6, 7)):
    p = ctx.p
    tl = ctx.t
    war = {0: None, 1: None}
    war_split = None
    war_bank = None
    done = None
    for tt in range(NTT):
        par = tt % 2
        xs = (tl["xa"] if par == 0 else tl["xb"])[:, :]
        lx = p.dma("sync", xs, page[tt * 128:(tt + 1) * 128, :],
                   after=[war[par]], sem="xlA" if par == 0 else "xlB")
        nh, nl = tl["nh"][:, :], tl["nl"][:, :]
        t_hi = p.vec(lambda e, xs=xs: e.tensor_copy(nh, xs), after=[lx, war_split])
        t_lo = p.vec(lambda e, xs=xs: e.tensor_sub(nl, xs, nh), after=[t_hi])
        war[par] = t_lo
        tks = []
        for (tile, bank, rdy) in ((nh, banks[0], t_hi), (nl, banks[1], t_lo)):
            tk = None
            for cc in range(8):
                tk = p.ten(lambda e, tile=tile, cc=cc, bank=bank: e.transpose(
                    ctx.bank16(bank, 128, cc * 128),
                    tile[:, cc * 128:(cc + 1) * 128], tl["idn"][:, :]),
                    after=[rdy, war_bank, ctx.ident_tok] if cc == 0 else ())
            tks.append(tk)
        war_split = tks[1]
        evs = []
        for (bank, dst, tk) in ((banks[0], ft_pair[0], tks[0]),
                                (banks[1], ft_pair[1], tks[1])):
            dst_ap = AP(dst.tensor, dst.offset + tt * 128,
                        [list(dst.ap[0]), [T, 8], [1, 128]])
            evs.append(p.act(lambda e, bank=bank, dst_ap=dst_ap: e.copy(
                dst_ap, ctx.bank16(bank, 1024)), after=[tk]))
        war_bank = evs[-1]
        done = evs[-1]
    return done


def emit_l0_attn(ctx, ft, wqT, wkT, wvT, woT, pek_dram, pev_dram,
                 x_page, out_page, ft_out):
    p = ctx.p
    tl = ctx.t
    a16 = A16(ctx)
    a32 = A32(ctx)

    qt_hi, qt_lo = a16.take(T), a16.take(T)
    kt_hi, kt_lo = a16.take(T), a16.take(T)
    vtf32 = a16.take(4096).bitcast(F32)
    v32 = a16.take(4096).bitcast(F32)
    ats = a16.take(2048).bitcast(F32)
    aos = a16.take(512)
    R = a16.take(4096)
    aorh = [R[:, 0:1024], R[:, 1024:2048]]
    aorl = [R[:, 2048:3072], R[:, 3072:4096]]
    scsb2 = R.bitcast(F32)
    qpsb2 = a16.take(2304).bitcast(F32)
    relmask = a16.take(512).bitcast(F32)
    A32t = a16.take(2304).bitcast(F32)
    wsl_hi = [a16.take(384) for _ in range(8)]
    wsl_lo = [a16.take(384) for _ in range(8)]
    pekT_hi, pekT_lo = a16.take(1152), a16.take(1152)
    pev_hi, pev_lo = a16.take(9 * 64), a16.take(9 * 64)
    AR16 = tl["AR16"]
    wo_hi = [AR16[:, c * 1024:(c + 1) * 1024] for c in range(8)]
    wo_lo = [AR16[:, 8192 + c * 1024:8192 + (c + 1) * 1024] for c in range(8)]

    scsb = a32.take(2048)
    exps = a32.take(2048)
    qpsb = a32.take(1152)
    b8sb = a32.take(256)
    wstage = a32.take(512)

    cols = tl["cols"]
    rmaxs = (cols[:, 3:4], cols[:, 8:9])
    rsums = (cols[:, 4:5], cols[:, 9:10])
    rrecs = (cols[:, 5:6], cols[:, 10:11])
    qp0s = (cols[:, 6:7], cols[:, 11:12])
    far = cols[:, 7:8]
    scsbs = (scsb, scsb2)
    qpsbs = (qpsb, qpsb2)
    aoh, aol = ctx.dram["aoh"], ctx.dram["aol"]
    qpd = [ctx.dram["qp0"], ctx.dram["qp1"]]
    bndd = [ctx.dram["bnd0"], ctx.dram["bnd1"]]
    b8d = [ctx.dram["b80"], ctx.dram["b81"]]
    mk = p.vec(lambda e: e.memset(relmask, 1.0))
    mk = p.gp(lambda e: e.affine_select(
        out=relmask[:, 0:128], in_=relmask[:, 0:128], compare_op=A.is_ge,
        fill=0.0, base=-128, pattern=[[1, 128]], channel_multiplier=1),
        after=[mk])
    mk_tok = p.gp(lambda e: e.affine_select(
        out=relmask[:, 128:256], in_=relmask[:, 128:256], compare_op=A.is_ge,
        fill=0.0, base=0, pattern=[[1, 128]], channel_multiplier=1),
        after=[mk])
    mask1 = relmask[:, 0:128]
    mask0 = relmask[:, 128:256]

    tzb = p.vec(lambda e: e.memset(scsb[:, 0:1152], 0.0))
    tza = p.vec(lambda e: e.memset(A32t[:, 1025:1152], 0.0))
    zb0 = p.dma("scalar", AP(bndd[0], 0, [[1152, 128], [1, 1152]]),
                scsb[:, 0:1152], after=[tzb], sem="bwA")
    zb1 = p.dma("scalar", AP(bndd[1], 0, [[1152, 128], [1, 1152]]),
                scsb[:, 0:1152], after=[tzb], sem="bwB")

    pvt = None
    pk_tok = None
    for cc in range(9):
        st = wstage[:, 0:64]
        ld = p.dma("sync", st, pek_dram[cc * 128:(cc + 1) * 128, :],
                   after=[pk_tok], sem="pkl")
        t1 = p.ten(lambda e, st=st: e.transpose(
            ctx.bank32(5, 128)[0:64, :], st, tl["idn32"][:, :]),
            after=[ld, pvt, ctx.ident_tok])
        t2 = p.vec(lambda e, cc=cc: e.tensor_copy(
            scsb[0:64, cc * 128:(cc + 1) * 128], ctx.bank32(5, 128)[0:64, :]),
            after=[t1, zb0, zb1])
        pvt = t2
        pk_tok = t2
    tdup = p.dma("sync", scsb[64:128, 0:1152], scsb[0:64, 0:1152],
                 after=[pvt], sem="pkd")
    tz = p.vec(lambda e: e.memset(scsb[:, 1025:1152], 0.0), after=[tdup])
    tk_pek = emit_wsplit(ctx, [(scsb[:, 0:1152], pekT_hi, pekT_lo, 0.125)],
                         after=[tz], war=None)
    pv_tok = tk_pek
    for cc in range(9):
        st = wstage[:, 0:64]
        if cc == 8:
            tz8 = p.vec(lambda e: e.memset(st, 0.0), after=[pv_tok])
            ld = p.dma("sync", st[0:1, :], pev_dram[1024:1025, :],
                       after=[tz8], sem="pkl")
        else:
            ld = p.dma("sync", st, pev_dram[cc * 128:(cc + 1) * 128, :],
                       after=[pv_tok], sem="pkl")
        pv_tok = emit_wsplit(ctx, [(st, pev_hi[:, cc * 64:(cc + 1) * 64],
                                    pev_lo[:, cc * 64:(cc + 1) * 64], 1.0)],
                             after=[ld], war=pv_tok)
    pev_tok = pv_tok
    pev32 = a16.take(1152).bitcast(F32)
    pev_tok = p.vec(lambda e: e.tensor_add(pev32, pev_hi, pev_lo), after=[pev_tok])

    def s1a(pr, qi, hx, slice_tok, war_hbuf):
        u = {"qi": qi, "hx": hx, "par": (qi * 2 + hx) % 2}
        par = u["par"]
        kw = (qi + 1) * 128
        gmax = min(qi, 7)
        hbase = hx * 64
        hslice = slice(hbase, hbase + 64)
        u["hbase"] = hbase
        mmq = None
        for g0 in range(0, 1152, 512):
            gw = min(512, 1152 - g0)
            bidx = g0 // 512
            lh = qt_hi[hslice, qi * 128:(qi + 1) * 128]
            llo = qt_lo[hslice, qi * 128:(qi + 1) * 128]
            rh = pekT_hi[hslice, g0:g0 + gw]
            rl = pekT_lo[hslice, g0:g0 + gw]
            aft = ([slice_tok, S["bank03"], tk_pek] + war_hbuf[par]) if g0 == 0 else ()
            p.emit("tensor", lambda e, lh=lh, rh=rh, bidx=bidx: e.matmul(
                ctx.bank32(bidx, rh.shape[-1]), lh, rh, start=True, stop=False),
                after=aft, inc=[])
            p.emit("tensor", lambda e, lh=lh, rl=rl, bidx=bidx: e.matmul(
                ctx.bank32(bidx, rl.shape[-1]), lh, rl, start=False, stop=False), inc=[])
            mmq = p.emit("tensor", lambda e, llo=llo, rh=rh, bidx=bidx: e.matmul(
                ctx.bank32(bidx, rh.shape[-1]), llo, rh, start=False, stop=True),
                inc=[("ts", 1)])
        tev = p.act(lambda e, par=par: e.copy(qpsbs[par], ctx.bank32(0, 1152)),
                    after=[mmq] + S["war_qpsb"][par])
        tq0 = p.vec(lambda e, par=par: e.tensor_copy(qp0s[par], qpsbs[par][:, 0:1]),
                    after=[tev])
        u["tq0"] = tq0
        wqp = p.dma("sync", AP(qpd[par], 0, [[1152, 128], [1, 1152]]),
                    qpsbs[par], after=[tev] + war_hbuf[par],
                    sem="qpwA" if par == 0 else "qpwB")
        u["wqp"] = wqp
        mm = None
        for g0 in range(0, kw, 512):
            gw = min(512, kw - g0)
            bidx = g0 // 512
            lh = qt_hi[hslice, qi * 128:(qi + 1) * 128]
            llo = qt_lo[hslice, qi * 128:(qi + 1) * 128]
            rh = kt_hi[hslice, g0:g0 + gw]
            rl = kt_lo[hslice, g0:g0 + gw]
            aft = [tev] if g0 == 0 else ()
            p.emit("tensor", lambda e, lh=lh, rh=rh, bidx=bidx: e.matmul(
                ctx.bank32(bidx, rh.shape[-1]), lh, rh, start=True, stop=False),
                after=aft, inc=[])
            p.emit("tensor", lambda e, lh=lh, rl=rl, bidx=bidx: e.matmul(
                ctx.bank32(bidx, rl.shape[-1]), lh, rl, start=False, stop=False), inc=[])
            mm = p.emit("tensor", lambda e, llo=llo, rh=rh, bidx=bidx: e.matmul(
                ctx.bank32(bidx, rh.shape[-1]), llo, rh, start=False, stop=True),
                inc=[("ts", 1)])
        tsc = p.act(lambda e, kw=kw, par=par: e.copy(
            scsbs[par][:, 0:kw], ctx.bank32(0, kw)),
            after=[mm] + S["war_scsb"][par])
        S["bank03"] = tsc
        u["tsc"] = tsc
        base = 1024 - 128 * gmax
        slab = AP(qpd[par], base, [[1151, 128], [128, gmax + 1], [1, 128]])
        gacc = p.dma("gpsimd", scsbs[par][:, (qi - gmax) * 128:(qi + 1) * 128],
                     slab, after=[wqp, tsc],
                     sem="gacA" if par == 0 else "gacB", accum_op=A.add)
        u["toks_bias"] = [gacc]
        u["gacc"] = gacc
        u["bacc"] = None
        if qi >= 8:
            tb = p.vec(lambda e: e.memset(b8sb[:, 0:129], 0.0),
                       after=[tq0, S["war_b8sb"]] + war_hbuf[par])
            tb = p.vec(lambda e, par=par: e.tensor_single_scalar(
                b8sb[:, 0:129], b8sb[:, 0:129], qp0s[par], A.add), after=[tb])
            tb = p.vec(lambda e, par=par: e.tensor_copy(
                b8sb[:, 129:256], qpsbs[par][:, 1:128]), after=[tb, tev])
            u["tbcp"] = tb
            wb8 = p.dma("scalar", AP(b8d[par], 0, [[256, 128], [1, 256]]),
                        b8sb, after=[tb], sem="b8wA" if par == 0 else "b8wB")
            S["war_b8sb"] = wb8
            b8r = AP(b8d[par], 128, [[255, 128], [1, 128]])
            bacc = p.dma("gpsimd", scsbs[par][:, (qi - 8) * 128:(qi - 7) * 128],
                         b8r, after=[wb8, tsc],
                         sem="b8rA" if par == 0 else "b8rB", accum_op=A.add)
            u["toks_bias"].append(bacc)
            u["bacc"] = bacc
        return u

    def s1b(u, war_hbuf):
        qi, par = u["qi"], u["par"]
        kw = (qi + 1) * 128
        toks = list(u["toks_bias"])
        if qi >= 9:
            tfar = p.vec(lambda e, qi=qi, par=par: e.tensor_single_scalar(
                scsbs[par][:, 0:(qi - 8) * 128], scsbs[par][:, 0:(qi - 8) * 128],
                qp0s[par], A.add), after=[u["tq0"], u["tsc"]])
            toks.append(tfar)
        tmask = p.vec(lambda e, qi=qi, par=par: e.tensor_add(
            scsbs[par][:, qi * 128:(qi + 1) * 128],
            scsbs[par][:, qi * 128:(qi + 1) * 128],
            ctx.mdiag), after=toks + [ctx.mask_tok])
        tmx = p.vec(lambda e, kw=kw, par=par: e.reduce_max(
            rmaxs[par], scsbs[par][:, 0:kw], axis=AX.X, negate=True), after=[tmask])
        texp = p.act(lambda e, kw=kw, par=par: e.activation(
            exps[:, 0:kw], scsbs[par][:, 0:kw], AF.Exp, bias=rmaxs[par],
            accum_out=rsums[par]), after=[tmx])
        trc = p.vec(lambda e, par=par: e.reciprocal(rrecs[par], rsums[par]),
                    after=[texp])
        tat = p.vec(lambda e, kw=kw, par=par: e.tensor_single_scalar(
            scsbs[par][:, 0:kw], exps[:, 0:kw], rrecs[par], A.mult), after=[trc])
        u["tat"] = tat
        gmw = min(qi, 8)
        bw = (gmw + 1) * 128
        wofs = (8 - gmw) * 128
        wbd = p.dma("scalar",
                    AP(bndd[par], wofs, [[1152, 128], [1, bw]]),
                    scsbs[par][:, (qi - gmw) * 128:(qi + 1) * 128],
                    after=[tat] + war_hbuf[par],
                    sem="bwA" if par == 0 else "bwB")
        u["wbd"] = wbd

    def s2(u, war_aos, war_hbuf):
        qi, hx, par, hbase = u["qi"], u["hx"], u["par"], u["hbase"]
        ar = p.dma("sync", A32t[:, 1:1025],
                   AP(bndd[par], 1, [[1153, 128], [1, 1024]]),
                   after=[u["wbd"], S["war_A"]], sem="arA" if par == 0 else "arB")
        war_hbuf[par] = [u["gacc"], ar] + ([u["bacc"]] if u["bacc"] else [])
        tfs = p.vec(lambda e: e.reduce_sum(far, A32t[:, 1:1025], axis=AX.X,
                                           negate=True), after=[ar])
        tfs = p.vec(lambda e: e.tensor_single_scalar(far, far, 1.0, A.add),
                    after=[tfs])
        tfs = p.vec(lambda e: e.tensor_copy(A32t[:, 0:1], far), after=[tfs])
        for ucm, msk in ((7 - qi, mask1), (8 - qi, mask0)):
            if 0 <= ucm <= 8:
                tfs = p.vec(lambda e, ucm=ucm, msk=msk: e.tensor_mul(
                    A32t[:, ucm * 128:(ucm + 1) * 128],
                    A32t[:, ucm * 128:(ucm + 1) * 128], msk),
                    after=[tfs, mk_tok])
        uc0 = max(0, 7 - qi)
        chunks = [("s", kc) for kc in range(qi + 1)] + \
                 [("a", uc) for uc in range(uc0, 9)]
        first_pv = True
        tpv = None
        last_t1_attn = None
        for gbase in range(0, len(chunks), 4):
            grp = chunks[gbase:gbase + 4]
            gpar = (gbase // 4) % 2
            vb = (5, 6)[gpar]
            t1 = None
            for j, (knd, idx) in enumerate(grp):
                src = scsbs[par][:, idx * 128:(idx + 1) * 128] if knd == "s" \
                    else A32t[:, idx * 128:(idx + 1) * 128]
                stok = u["tat"] if knd == "s" else tfs
                aft = [stok] + ([S["war_bank56"][gpar], ctx.ident_tok] if j == 0 else [])
                t1 = p.ten(lambda e, src=src, j=j, vb=vb: e.transpose(
                    ctx.bank32(vb, 128, j * 128), src, tl["idn32"][:, :]),
                    after=aft)
                if knd == "s":
                    last_t1_attn = t1
            te = p.act(lambda e, vb=vb, gpar=gpar, n=len(grp): e.copy(
                ats[:, gpar * 512:gpar * 512 + n * 128],
                ctx.bank32(vb, n * 128)), after=[t1, S["war_ats"][gpar]])
            S["war_bank56"][gpar] = te
            for j, (knd, idx) in enumerate(grp):
                vsl = v32[:, idx * 128:(idx + 1) * 128][:, hbase:hbase + 64] \
                    if knd == "s" else pev32[:, idx * 64:(idx + 1) * 64]
                ah = ats[:, gpar * 512 + j * 128:gpar * 512 + (j + 1) * 128]
                ops = ctx.bank32(4, 128)[hbase:hbase + 64, :]
                last = (knd == "a" and idx == 8)
                aft = [te] + ([pev_tok] if knd == "a" else []) + \
                      ([S["war_pv"]] if (first_pv and hx == 0) else [])
                tpv = p.emit("tensor", lambda e, vsl=vsl, ah=ah, ops=ops,
                             st=first_pv, sp=last, tp=(0, hbase): e.matmul(
                    ops, vsl, ah, start=st, stop=sp, tile_position=tp),
                    after=aft, inc=[("ts", 1)])
                first_pv = False
            S["war_ats"][gpar] = tpv
            if any(k == "a" for k, _ in grp):
                S["war_A"] = t1
        S["war_scsb"][par] = [u["wbd"], last_t1_attn]
        S["war_qpsb"][par] = [u["wqp"]] + ([u["tbcp"]] if "tbcp" in u else [])
        S["done_pv"] = tpv
        if hx == 1:
            slot = qi % 2
            sh = aos[:, slot * 256:slot * 256 + 128]
            sl = aos[:, slot * 256 + 128:slot * 256 + 256]
            th = p.act(lambda e, sh=sh: e.copy(sh, ctx.bank32(4, 128)),
                       after=[tpv, war_aos[slot]])
            tsub = p.vec(lambda e, sh=sh, sl=sl: e.tensor_sub(
                sl, ctx.bank32(4, 128), sh), after=[th])
            S["war_pv"] = tsub
            S["done_pv"] = tsub
            ro = qi * 1024 + u["pr"] * 128
            w1 = p.dma("gpsimd", AP(aoh, ro * 128, [[128, 128], [1, 128]]),
                       sh, after=[tsub], sem="aowA" if slot == 0 else "aowB")
            w2 = p.dma("gpsimd", AP(aol, ro * 128, [[128, 128], [1, 128]]),
                       sl, after=[tsub], sem="aowA" if slot == 0 else "aowB")
            war_aos[slot] = w2

    def attention(pr, slice_tok, war_ao, war_aos, war_hbuf):
        units = [(qi, hx) for qi in range(NTT) for hx in range(2)]
        ulist = []
        u0 = s1a(pr, 0, 0, slice_tok, war_hbuf)
        u0["pr"] = pr
        s1b(u0, war_hbuf)
        ulist.append(u0)
        for i in range(len(units)):
            if i + 1 < len(units):
                qi, hx = units[i + 1]
                un = s1a(pr, qi, hx, slice_tok, war_hbuf)
                un["pr"] = pr
                ulist.append(un)
            s2(ulist[i], war_aos, war_hbuf)
            if i + 1 < len(units):
                s1b(ulist[i + 1], war_hbuf)
        return S["done_pv"]

    slice_free = pv_tok
    war_proj = (None, None)
    war_ao = None
    war_aos = {0: None, 1: None}
    war_hbuf = {0: [zb0], 1: [zb1]}
    S = {"bank03": None, "war_pv": None, "war_b8sb": None,
         "war_A": tza, "done_pv": None,
         "war_scsb": {0: [zb0, zb1, tk_pek], 1: []},
         "war_qpsb": {0: [], 1: []},
         "war_ats": {0: None, 1: None},
         "war_bank56": {0: None, 1: None}}
    for pr in range(8):
        sl_tok = None
        for c in range(8):
            for j, wsrc in enumerate((wqT, wkT, wvT)):
                src = AP(wsrc.tensor, c * 128 * 1024 + 128 * pr,
                         [[1024, 128], [1, 128]])
                ld = p.dma("sync", wstage[:, 0:128], src, after=[slice_free], sem="wsl")
                sl_tok = emit_wsplit(
                    ctx, [(wstage[:, 0:128], wsl_hi[c][:, j * 128:(j + 1) * 128],
                           wsl_lo[c][:, j * 128:(j + 1) * 128],
                           0.125 if j == 1 else 1.0)],
                    after=[ld], war=sl_tok)
                slice_free = sl_tok
        tq, _ = emit_projT(ctx, ft, wsl_hi, wsl_lo, 0, 128, qt_hi, qt_lo, (0, 1),
                           after=[sl_tok], war=war_proj[0])
        tkk, _ = emit_projT(ctx, ft, wsl_hi, wsl_lo, 128, 128, kt_hi, kt_lo, (0, 1),
                            after=[sl_tok], war=tq)
        tv, _ = emit_projT(ctx, ft, wsl_hi, wsl_lo, 256, 128, None, None, (0, 1),
                           after=[sl_tok], war=tkk, dst32=vtf32)
        slice_free = tv
        tev = None
        tev2 = {0: None, 1: None}
        for tt in range(NTT):
            slot = tt % 2
            vb = (5, 3)[slot]
            t1 = p.ten(lambda e, tt=tt, vb=vb: e.transpose(
                ctx.bank32(vb, 128), vtf32[:, tt * 128:(tt + 1) * 128],
                tl["idn32"][:, :]), after=[tv, tev2[slot], ctx.ident_tok])
            tev = p.act(lambda e, tt=tt, vb=vb: e.copy(
                v32[:, tt * 128:(tt + 1) * 128], ctx.bank32(vb, 128)),
                after=[t1])
            tev2[slot] = tev
        war_ao = attention(pr, tev, war_ao, war_aos, war_hbuf)
        war_proj = (war_ao, war_ao)

    wo_tok, wst_free = None, None
    for c in range(8):
        for j in range(2):
            ld = p.dma("sync", wstage[:, 0:512],
                       woT[c * 128:(c + 1) * 128, j * 512:(j + 1) * 512],
                       after=[wst_free, war_ao if (c == 0 and j == 0) else None],
                       sem="wsl")
            wo_tok = emit_wsplit(ctx, [(wstage[:, 0:512], wo_hi[c][:, j * 512:(j + 1) * 512],
                                        wo_lo[c][:, j * 512:(j + 1) * 512], 1.0)],
                                 after=[ld, war_ao if (c == 0 and j == 0) else None],
                                 war=wo_tok)
            wst_free = wo_tok

    ao_barrier = (p.barrier_tok("aowA"), p.barrier_tok("aowB"))
    epi = EpiPipe(ctx, x_page, out_page, ft_out, banks=(6, 7))
    war_y = None
    war_aor = {0: None, 1: None}
    for tt in range(NTT):
        par = tt % 2
        lh = p.dma("sync", aorh[par], AP(aoh, tt * 1024 * 128,
                                         [[128, 128], [16384, 8], [1, 128]]),
                   after=[war_aor[par], ao_barrier[0], ao_barrier[1]],
                   sem="aorA" if par == 0 else "aorB")
        ll = p.dma("scalar", aorl[par], AP(aol, tt * 1024 * 128,
                                           [[128, 128], [16384, 8], [1, 128]]),
                   after=[war_aor[par], ao_barrier[0], ao_barrier[1]],
                   sem="aosA" if par == 0 else "aosB")
        mm = None
        for osup in range(2):
            b = osup
            for c in range(8):
                ah = aorh[par][:, c * 128:(c + 1) * 128]
                al = aorl[par][:, c * 128:(c + 1) * 128]
                rh = wo_hi[c][:, osup * 512:(osup + 1) * 512]
                rl = wo_lo[c][:, osup * 512:(osup + 1) * 512]
                aft = [lh, ll, wo_tok, war_y] if (c == 0 and osup == 0) else ()
                p.emit("tensor", lambda e, ah=ah, rh=rh, b=b, c=c: e.matmul(
                    ctx.bank32(b, 512), ah, rh, start=(c == 0), stop=False),
                    after=aft, inc=[])
                p.emit("tensor", lambda e, ah=ah, rl=rl, b=b: e.matmul(
                    ctx.bank32(b, 512), ah, rl, start=False, stop=False), inc=[])
                mm = p.emit("tensor", lambda e, al=al, rh=rh, b=b, c=c: e.matmul(
                    ctx.bank32(b, 512), al, rh, start=False, stop=(c == 7)),
                    inc=[("ts", 1)])
        war_aor[par] = mm
        epi.tile(tt, ctx.bank32(0, 1024), mm)
        war_y = epi.war_s
    return epi


LN1E4 = math.log(10000.0)
_c64 = -LN1E4 / 512.0
C_HI = float(np.float32(_c64)); C_LO = float(np.float32(_c64 - C_HI))
_tp64 = 2.0 * math.pi
TP_HI = float(np.float32(_tp64)); TP_LO = float(np.float32(_tp64 - TP_HI))
INV2PI = float(np.float32(1.0 / _tp64))
_p264 = 0.5 * math.pi
P2_HI = float(np.float32(_p264)); P2_LO = float(np.float32(_p264 - P2_HI))
PI_CLAMP = float(np.nextafter(np.float32(math.pi), np.float32(0.0)))


def _pe_bits(chunk):
    is_cos = chunk >= 4
    fi0 = 128 * (chunk % 4)
    invf_max = math.exp(fi0 * _c64)
    tmax = 2047.0 * invf_max / _tp64 + (0.25 if is_cos else 0.0)
    if tmax < 0.45:
        return None
    b = 1
    while (1 << (b - 1)) <= tmax + 0.75:
        b += 1
    return b


def emit_pe_rt(ctx, wr_dram, rth_d, rtl_d):
    p = ctx.p
    tl = ctx.t
    a16 = A16(ctx)
    a32 = A32(ctx)
    pet_hi, pet_lo = a16.take(8 * T), a16.take(8 * T)
    scr = a16.take(8192).bitcast(F32)
    xsl = scr[:, 0:2048]
    nsl = scr[:, 2048:4096]
    tsl = a32.take(2048)
    rem = a32.take(2048)
    bit = a32.take(2048)
    cols = tl["cols"]
    fic = cols[:, 8:9]
    ivf = cols[:, 9:10]

    done = None
    for ch in range(8):
        is_cos = ch >= 4
        fi0 = 128 * (ch % 4)
        t = p.gp(lambda e, fi0=fi0: e.iota(fic, pattern=[[0, 1]], base=fi0,
                                           channel_multiplier=1,
                                           allow_small_or_imprecise_dtypes=True),
                 after=[done])
        t = p.vec(lambda e: e.tensor_single_scalar(ivf, fic, C_HI, A.mult), after=[t])
        t = p.vec(lambda e: e.scalar_tensor_tensor(ivf, fic, C_LO, ivf, A.mult, A.add), after=[t])
        t = p.act(lambda e: e.activation(ivf, ivf, AF.Exp), after=[t])
        t = p.gp(lambda e: e.iota(tsl, pattern=[[-1, 2048]], base=2047,
                                  channel_multiplier=0,
                                  allow_small_or_imprecise_dtypes=True), after=[t, done])
        t = p.vec(lambda e: e.tensor_single_scalar(xsl, tsl, ivf, A.mult), after=[t])
        if is_cos:
            t = p.vec(lambda e: e.tensor_scalar(xsl, xsl, P2_HI, P2_LO, A.add, A.add), after=[t])
        nb = _pe_bits(ch)
        if nb is not None:
            t = p.vec(lambda e: e.tensor_single_scalar(tsl, xsl, INV2PI, A.mult), after=[t])
            t = round_half_even(p, nsl, tsl, rem, bit, tsl, nbits=nb, after=[t])
            t = p.vec(lambda e: e.scalar_tensor_tensor(rem, nsl, -TP_HI, xsl, A.mult, A.add), after=[t])
            t = p.vec(lambda e: e.scalar_tensor_tensor(bit, nsl, -TP_LO, rem, A.mult, A.add), after=[t])
            t = p.vec(lambda e: e.tensor_scalar(bit, bit, -PI_CLAMP, PI_CLAMP,
                                                A.max, A.min), after=[t])
            ysl = bit
        else:
            ysl = xsl
        t = p.act(lambda e, ysl=ysl: e.activation(xsl if ysl is not xsl else tsl,
                                                  ysl, AF.Sin), after=[t])
        ssl = xsl if ysl is not xsl else tsl
        done = emit_wsplit(ctx, [(ssl, ft_chunk(pet_hi, ch), ft_chunk(pet_lo, ch), 1.0)],
                           after=[t])

    wst = a32[:, 0:128] if False else tsl[:, 0:128]
    wr_hi = rem[:, 0:512].bitcast(BF16)
    wr_lo = rem[:, 512:1024].bitcast(BF16)
    sth = bit[:, 0:256].bitcast(BF16)
    stl = bit[:, 256:512].bitcast(BF16)
    wdone = None
    prev_split = done
    for sl in range(8):
        sl_tok = None
        for c in range(8):
            src = AP(wr_dram.tensor, c * 128 * 1024 + 128 * sl, [[1024, 128], [1, 128]])
            ld = p.dma("sync", wst, src, after=[sl_tok if sl_tok else prev_split,
                                               wdone], sem="wsl")
            sl_tok = emit_wsplit(ctx, [(wst, wr_hi[:, c * 128:(c + 1) * 128],
                                        wr_lo[:, c * 128:(c + 1) * 128], 1.0)],
                                 after=[ld], war=sl_tok)
        for ts in range(4):
            b = ts % 2
            mm = None
            for c in range(8):
                lh = wr_hi[:, c * 128:(c + 1) * 128]
                ll = wr_lo[:, c * 128:(c + 1) * 128]
                rh = ft_chunk(pet_hi, c, ts * 512, 512)
                rl = ft_chunk(pet_lo, c, ts * 512, 512)
                aft = [sl_tok, wdone] if c == 0 else ()
                p.emit("tensor", lambda e, lh=lh, rh=rh, b=b, c=c: e.matmul(
                    ctx.bank32(b, 512), lh, rh, start=(c == 0), stop=False),
                    after=aft, inc=[])
                p.emit("tensor", lambda e, lh=lh, rl=rl, b=b: e.matmul(
                    ctx.bank32(b, 512), lh, rl, start=False, stop=False), inc=[])
                mm = p.emit("tensor", lambda e, ll=ll, rh=rh, b=b, c=c: e.matmul(
                    ctx.bank32(b, 512), ll, rh, start=False, stop=(c == 7)),
                    inc=[("ts", 1)])
            th = p.act(lambda e, b=b: e.activation(sth, ctx.bank32(b, 512),
                                                   AF.Copy, scale=0.125), after=[mm])
            t2 = p.vec(lambda e, b=b: e.scalar_tensor_tensor(
                stl, ctx.bank32(b, 512), 0.125, sth, A.mult, A.subtract), after=[th])
            o1 = p.dma("scalar", AP(rth_d, sl * 128 * 2048 + ts * 512,
                                    [[2048, 128], [1, 512]]), sth,
                       after=[t2], sem="rtw")
            o2 = p.dma("scalar", AP(rtl_d, sl * 128 * 2048 + ts * 512,
                                    [[2048, 128], [1, 512]]), stl,
                       after=[t2], sem="rtw")
            wdone = o2
        prev_split = sl_tok
    return p.barrier_tok("rtw"), p.barrier_tok("vs")

def emit_l1_ffn(ctx, ft, W, x_page, pages, out_page, start=None):
    p = ctx.p
    tl = ctx.t
    cols = tl["cols"]
    aoh, aol = ctx.dram["aoh"], ctx.dram["aol"]
    bndd = [ctx.dram["bnd0"], ctx.dram["bnd1"]]

    rt_done, pe_vs = emit_pe_rt(ctx, W["Wr"], pages["rth"], pages["rtl"])

    a32 = A32(ctx)
    bdsb = a32.take(2304)
    scsb = bdsb
    exps = a32.take(2048)
    wstage = a32.take(1024)
    gbg = a32.take(512)
    gbb = a32.take(256)
    tzb = p.vec(lambda e: e.memset(bdsb[:, 0:2176], 0.0), after=[pe_vs])
    zb0 = p.dma("scalar", AP(bndd[0], 0, [[2176, 128], [1, 2176]]),
                bdsb[:, 0:2176], after=[tzb], sem="bwA")
    zb1 = p.dma("scalar", AP(bndd[1], 0, [[2176, 128], [1, 2176]]),
                bdsb[:, 0:2176], after=[tzb], sem="bwB")

    a16 = A16(ctx)
    qu_hi, qu_lo = a16.take(T), a16.take(T)
    qv_hi, qv_lo = a16.take(T), a16.take(T)
    kt_hi, kt_lo = a16.take(T), a16.take(T)
    rt_hi, rt_lo = a16.take(T), a16.take(T)
    vtf32 = a16.take(4096).bitcast(F32)
    v32 = a16.take(4096).bitcast(F32)
    ats = a16.take(2048).bitcast(F32)
    aos = a16.take(512)
    aorh = [a16.take(1024) for _ in range(2)]
    aorl = [a16.take(1024) for _ in range(2)]
    wsl_hi = [a16.take(384) for _ in range(8)]
    wsl_lo = [a16.take(384) for _ in range(8)]
    AR16 = tl["AR16"]
    wbig_hi = [AR16[:, c * 1024:(c + 1) * 1024] for c in range(8)]
    wbig_lo = [AR16[:, 8192 + c * 1024:8192 + (c + 1) * 1024] for c in range(8)]

    rmaxs = (cols[:, 3:4], cols[:, 8:9])
    rsums = (cols[:, 4:5], cols[:, 9:10])
    rrecs = (cols[:, 5:6], cols[:, 12:13])
    ubc, vbc = cols[:, 10:11], cols[:, 11:12]
    bdsb2 = a16.take(4352).bitcast(F32)
    bdsbs = (bdsb, bdsb2)

    def qkv_slices(pr, prev):
        sl_tok = None
        free = prev
        for c in range(8):
            src = AP(W["Wqkv"].tensor, c * 128 * 3072 + 128 * pr,
                     [[3072, 128], [1024, 3], [1, 128]])
            ld = p.dma("sync", wstage[:, 0:384], src, after=[free], sem="wsl")
            sl_tok = emit_wsplit(
                ctx, [(wstage[:, 0:128], wsl_hi[c][:, 0:128], wsl_lo[c][:, 0:128], 1.0),
                      (wstage[:, 128:256], wsl_hi[c][:, 128:256], wsl_lo[c][:, 128:256], 0.125),
                      (wstage[:, 256:384], wsl_hi[c][:, 256:384], wsl_lo[c][:, 256:384], 1.0)],
                after=[ld], war=sl_tok)
            free = sl_tok
        return sl_tok

    def proj_q_biased(pr, after, war):
        ft_hi, ft_lo = ft
        ev = {0: None, 1: None}
        done = None
        for ts in range(4):
            b = ts % 2
            mm = None
            for c in range(8):
                lh = wsl_hi[c][:, 0:128]
                ll = wsl_lo[c][:, 0:128]
                rh = ft_hi[:, c * T + ts * 512: c * T + ts * 512 + 512]
                rl = ft_lo[:, c * T + ts * 512: c * T + ts * 512 + 512]
                aft = (list(after) + [war, ev[b]]) if c == 0 else ()
                p.emit("tensor", lambda e, lh=lh, rh=rh, b=b, c=c: e.matmul(
                    ctx.bank32(b, 512), lh, rh, start=(c == 0), stop=False),
                    after=aft, inc=[])
                p.emit("tensor", lambda e, lh=lh, rl=rl, b=b: e.matmul(
                    ctx.bank32(b, 512), lh, rl, start=False, stop=False), inc=[])
                mm = p.emit("tensor", lambda e, ll=ll, rh=rh, b=b, c=c: e.matmul(
                    ctx.bank32(b, 512), ll, rh, start=False, stop=(c == 7)),
                    inc=[("ts", 1)])
            ps = ctx.bank32(b, 512)
            sl_ = slice(ts * 512, ts * 512 + 512)
            t1 = p.act(lambda e, ps=ps, sl_=sl_: e.activation(
                qu_hi[:, sl_], ps, AF.Identity, bias=ubc), after=[mm])
            t2 = p.vec(lambda e, ps=ps, sl_=sl_: e.scalar_tensor_tensor(
                qu_lo[:, sl_], ps, ubc, qu_hi[:, sl_], A.add, A.subtract), after=[t1])
            t3 = p.act(lambda e, ps=ps, sl_=sl_: e.activation(
                qv_hi[:, sl_], ps, AF.Identity, bias=vbc), after=[t2])
            t4 = p.vec(lambda e, ps=ps, sl_=sl_: e.scalar_tensor_tensor(
                qv_lo[:, sl_], ps, vbc, qv_hi[:, sl_], A.add, A.subtract), after=[t3])
            ev[b] = t4
            done = t4
        return done

    def s1a(pr, qi, hx, slice_tok, war_hbuf):
        u = {"qi": qi, "hx": hx, "par": (qi * 2 + hx) % 2, "pr": pr}
        par = u["par"]
        kw = (qi + 1) * 128
        hbase = hx * 64
        hslice = slice(hbase, hbase + 64)
        u["hbase"] = hbase
        jlo = max(0, 1920 - 128 * qi)
        w = 2048 - jlo
        mm = None
        first = True
        for g0 in range(0, w, 512):
            gw = min(512, w - g0)
            bidx = g0 // 512
            lh = qv_hi[hslice, qi * 128:(qi + 1) * 128]
            ll = qv_lo[hslice, qi * 128:(qi + 1) * 128]
            rh = rt_hi[hslice, jlo + g0: jlo + g0 + gw]
            rl = rt_lo[hslice, jlo + g0: jlo + g0 + gw]
            aft = (list(slice_tok) + [S["bank03"]] + war_hbuf[par]) if first else ()
            p.emit("tensor", lambda e, lh=lh, rh=rh, bidx=bidx: e.matmul(
                ctx.bank32(bidx, rh.shape[-1]), lh, rh, start=True, stop=False),
                after=aft, inc=[])
            p.emit("tensor", lambda e, lh=lh, rl=rl, bidx=bidx: e.matmul(
                ctx.bank32(bidx, rl.shape[-1]), lh, rl, start=False, stop=False), inc=[])
            mm = p.emit("tensor", lambda e, ll=ll, rh=rh, bidx=bidx: e.matmul(
                ctx.bank32(bidx, rh.shape[-1]), ll, rh, start=False, stop=True),
                inc=[("ts", 1)])
            first = False
        tbd = p.act(lambda e, w=w, par=par: e.copy(bdsbs[par][:, 0:w],
                                                   ctx.bank32(0, w)),
                    after=[mm] + S["war_bd"][par])
        wbd = p.dma("scalar", AP(bndd[par], jlo, [[2176, 128], [1, w]]),
                    bdsbs[par][:, 0:w], after=[tbd] + war_hbuf[par],
                    sem="bwA" if par == 0 else "bwB")
        mm = None
        for g0 in range(0, kw, 512):
            gw = min(512, kw - g0)
            bidx = g0 // 512
            lh = qu_hi[hslice, qi * 128:(qi + 1) * 128]
            ll = qu_lo[hslice, qi * 128:(qi + 1) * 128]
            rh = kt_hi[hslice, g0:g0 + gw]
            rl = kt_lo[hslice, g0:g0 + gw]
            aft = [tbd] if g0 == 0 else ()
            p.emit("tensor", lambda e, lh=lh, rh=rh, bidx=bidx: e.matmul(
                ctx.bank32(bidx, rh.shape[-1]), lh, rh, start=True, stop=False),
                after=aft, inc=[])
            p.emit("tensor", lambda e, lh=lh, rl=rl, bidx=bidx: e.matmul(
                ctx.bank32(bidx, rl.shape[-1]), lh, rl, start=False, stop=False), inc=[])
            mm = p.emit("tensor", lambda e, ll=ll, rh=rh, bidx=bidx: e.matmul(
                ctx.bank32(bidx, rh.shape[-1]), ll, rh, start=False, stop=True),
                inc=[("ts", 1)])
        tsc = p.act(lambda e, kw=kw, par=par: e.copy(
            bdsbs[par][:, 0:kw], ctx.bank32(0, kw)), after=[mm, wbd])
        S["bank03"] = tsc
        base0 = 2047 - 128 * qi
        slab = AP(bndd[par], base0, [[2175, 128], [128, qi + 1], [1, 128]])
        gacc = p.dma("gpsimd", bdsbs[par][:, 0:kw], slab,
                     after=[wbd, tsc], sem="gacA" if par == 0 else "gacB",
                     accum_op=A.add)
        u["gacc"] = gacc
        u["wbd"] = wbd
        return u

    def s1b(u):
        qi, par = u["qi"], u["par"]
        kw = (qi + 1) * 128
        tmask = p.vec(lambda e, qi=qi, par=par: e.tensor_add(
            bdsbs[par][:, qi * 128:(qi + 1) * 128],
            bdsbs[par][:, qi * 128:(qi + 1) * 128],
            ctx.mdiag), after=[u["gacc"], ctx.mask_tok])
        tmx = p.vec(lambda e, kw=kw, par=par: e.reduce_max(
            rmaxs[par], bdsbs[par][:, 0:kw], axis=AX.X, negate=True),
            after=[tmask])
        texp = p.act(lambda e, kw=kw, par=par: e.activation(
            bdsbs[par][:, 0:kw], bdsbs[par][:, 0:kw], AF.Exp, bias=rmaxs[par],
            accum_out=rsums[par]), after=[tmx])
        trc = p.vec(lambda e, par=par: e.reciprocal(rrecs[par], rsums[par]),
                    after=[texp])
        tat = p.vec(lambda e, kw=kw, par=par: e.tensor_single_scalar(
            bdsbs[par][:, 0:kw], bdsbs[par][:, 0:kw], rrecs[par], A.mult),
            after=[trc])
        u["tat"] = tat

    def s2(u, war_aos, war_hbuf):
        qi, hx, par, hbase = u["qi"], u["hx"], u["par"], u["hbase"]
        war_hbuf[par] = [u["gacc"]]
        first_pv = True
        tpv = None
        last_t1 = None
        for gbase in range(0, qi + 1, 4):
            n = min(4, qi + 1 - gbase)
            gpar = (gbase // 4) % 2
            vb = (5, 6)[gpar]
            t1 = None
            for j in range(n):
                kc = gbase + j
                aft = [u["tat"]] + ([S["war_bank56"][gpar], ctx.ident_tok]
                                    if j == 0 else [])
                t1 = p.ten(lambda e, kc=kc, j=j, vb=vb, par=par: e.transpose(
                    ctx.bank32(vb, 128, j * 128),
                    bdsbs[par][:, kc * 128:(kc + 1) * 128], tl["idn32"][:, :]),
                    after=aft)
            last_t1 = t1
            te = p.act(lambda e, vb=vb, gpar=gpar, n=n: e.copy(
                ats[:, gpar * 512:gpar * 512 + n * 128],
                ctx.bank32(vb, n * 128)), after=[t1, S["war_ats"][gpar]])
            S["war_bank56"][gpar] = te
            for j in range(n):
                kc = gbase + j
                vsl = v32[:, kc * 128:(kc + 1) * 128][:, hbase:hbase + 64]
                ah = ats[:, gpar * 512 + j * 128:gpar * 512 + (j + 1) * 128]
                ops = ctx.bank32(4, 128)[hbase:hbase + 64, :]
                aft = [te] + ([S["war_pv"]] if (first_pv and hx == 0) else [])
                tpv = p.emit("tensor", lambda e, vsl=vsl, ah=ah, ops=ops,
                             st=first_pv, sp=(kc == qi), tp=(0, hbase): e.matmul(
                    ops, vsl, ah, start=st, stop=sp, tile_position=tp),
                    after=aft, inc=[("ts", 1)])
                first_pv = False
            S["war_ats"][gpar] = tpv
        S["war_bd"][par] = [last_t1]
        S["done_pv"] = tpv
        if hx == 1:
            slot = qi % 2
            sh = aos[:, slot * 256:slot * 256 + 128]
            sl = aos[:, slot * 256 + 128:slot * 256 + 256]
            th = p.act(lambda e, sh=sh: e.copy(sh, ctx.bank32(4, 128)),
                       after=[tpv, war_aos[slot]])
            tsub = p.vec(lambda e, sh=sh, sl=sl: e.tensor_sub(
                sl, ctx.bank32(4, 128), sh), after=[th])
            S["war_pv"] = tsub
            S["done_pv"] = tsub
            ro = qi * 1024 + u["pr"] * 128
            w1 = p.dma("gpsimd", AP(aoh, ro * 128, [[128, 128], [1, 128]]),
                       sh, after=[tsub], sem="aowA" if slot == 0 else "aowB")
            w2 = p.dma("gpsimd", AP(aol, ro * 128, [[128, 128], [1, 128]]),
                       sl, after=[tsub], sem="aowA" if slot == 0 else "aowB")
            war_aos[slot] = w2

    def attention(pr, slice_tok, war_ao, war_aos, war_hbuf):
        units = [(qi, hx) for qi in range(NTT) for hx in range(2)]
        ulist = [s1a(pr, 0, 0, slice_tok, war_hbuf)]
        s1b(ulist[0])
        for i in range(len(units)):
            if i + 1 < len(units):
                qi, hx = units[i + 1]
                ulist.append(s1a(pr, qi, hx, slice_tok, war_hbuf))
            s2(ulist[i], war_aos, war_hbuf)
            if i + 1 < len(units):
                s1b(ulist[i + 1])
        return S["done_pv"]

    slice_free = rt_done
    war_proj = (pe_vs, pe_vs)
    war_ao = None
    war_aos = {0: None, 1: None}
    war_hbuf = {0: [zb0, zb1], 1: [zb0, zb1]}
    S = {"bank03": None, "war_pv": None, "done_pv": None,
         "war_bd": {0: [zb0, zb1], 1: []},
         "war_ats": {0: None, 1: None},
         "war_bank56": {0: None, 1: None}}
    war_rt = None
    for pr in range(8):
        tub = p.dma("sync", ubc, AP(W["u_bias"].tensor, 128 * pr, [[1, 128], [1, 1]]),
                    after=[war_proj[0]], sem="ubl")
        tvb = p.dma("sync", vbc, AP(W["v_bias"].tensor, 128 * pr, [[1, 128], [1, 1]]),
                    after=[war_proj[0]], sem="vbl")
        sl_tok = qkv_slices(pr, slice_free)
        tq = proj_q_biased(pr, [sl_tok, tub, tvb], war_proj[0])
        tkk, _ = emit_projT(ctx, ft, wsl_hi, wsl_lo, 128, 128, kt_hi, kt_lo, (0, 1),
                            after=[sl_tok], war=tq)
        tv, _ = emit_projT(ctx, ft, wsl_hi, wsl_lo, 256, 128, None, None, (0, 1),
                           after=[sl_tok], war=tkk, dst32=vtf32)
        slice_free = tv
        lrh = p.dma("sync", rt_hi, AP(pages["rth"], pr * 128 * 2048, [[2048, 128], [1, 2048]]),
                    after=[rt_done, war_rt], sem="rtlA")
        lrl = p.dma("sync", rt_lo, AP(pages["rtl"], pr * 128 * 2048, [[2048, 128], [1, 2048]]),
                    after=[rt_done, war_rt], sem="rtlB")
        tev = None
        tev2 = {0: None, 1: None}
        for tt in range(NTT):
            slot = tt % 2
            vb = (5, 3)[slot]
            t1 = p.ten(lambda e, tt=tt, vb=vb: e.transpose(
                ctx.bank32(vb, 128), vtf32[:, tt * 128:(tt + 1) * 128],
                tl["idn32"][:, :]), after=[tv, tev2[slot], ctx.ident_tok])
            tev = p.act(lambda e, tt=tt, vb=vb: e.copy(
                v32[:, tt * 128:(tt + 1) * 128], ctx.bank32(vb, 128)),
                after=[t1])
            tev2[slot] = tev
        war_ao = attention(pr, [tev, lrh, lrl, tq], war_ao, war_aos, war_hbuf)
        war_rt = war_ao
        war_proj = (war_ao, war_ao)

    def stream_wbig(wdram, after_tok):
        wt, wf = None, None
        for c in range(8):
            for j in range(2):
                ld = p.dma("sync", wstage[:, 0:512],
                           wdram[c * 128:(c + 1) * 128, j * 512:(j + 1) * 512],
                           after=[wf, after_tok if (c == 0 and j == 0) else None],
                           sem="wsl")
                wt = emit_wsplit(ctx, [(wstage[:, 0:512],
                                        wbig_hi[c][:, j * 512:(j + 1) * 512],
                                        wbig_lo[c][:, j * 512:(j + 1) * 512], 1.0)],
                                 after=[ld, after_tok if (c == 0 and j == 0) else None],
                                 war=wt)
                wf = wt
        return wt

    gb_g = bdsb[:, 0:1024]
    gb_b = bdsb[:, 1024:2048]

    def load_gb(gname, bname, after_toks):
        tok = None
        for (nm, dst) in ((gname, gb_g), (bname, gb_b)):
            tg0 = p.dma("sync", wstage[0:1, 0:1024], W[nm],
                        after=list(after_toks) + [tok], sem="gbl")
            for j in range(2):
                tm = p.ten(lambda e, j=j: e.matmul(
                    ctx.bank32(4, 512), ctx.ones_row,
                    wstage[0:1, j * 512:(j + 1) * 512], start=True, stop=True),
                    after=[tg0, ctx.ones_tok, tok])
                tok = p.vec(lambda e, j=j, dst=dst: e.tensor_copy(
                    dst[:, j * 512:(j + 1) * 512], ctx.bank32(4, 512)), after=[tm])
        return tok

    def big_proj_epi(epi, rd_hi_fn, rd_lo_fn, wp_tok, extra=None):
        war_y = None
        for tt in range(NTT):
            lh_t, ll_t, ah_f, al_f = rd_hi_fn(tt)
            mm = None
            for osup in range(2):
                b = osup
                for c in range(8):
                    ah = ah_f(c)
                    al = al_f(c)
                    rh = wbig_hi[c][:, osup * 512:(osup + 1) * 512]
                    rl = wbig_lo[c][:, osup * 512:(osup + 1) * 512]
                    aft = [lh_t, ll_t, wp_tok, war_y] if (c == 0 and osup == 0) else ()
                    p.emit("tensor", lambda e, ah=ah, rh=rh, b=b, c=c: e.matmul(
                        ctx.bank32(b, 512), ah, rh, start=(c == 0), stop=False),
                        after=aft, inc=[])
                    p.emit("tensor", lambda e, ah=ah, rl=rl, b=b: e.matmul(
                        ctx.bank32(b, 512), ah, rl, start=False, stop=False), inc=[])
                    mm = p.emit("tensor", lambda e, al=al, rh=rh, b=b, c=c: e.matmul(
                        ctx.bank32(b, 512), al, rh, start=False, stop=(c == 7)),
                        inc=[("ts", 1)])
            if extra is not None:
                mm = extra(tt, mm)
            epi.tile(tt, ctx.bank32(0, 1024), mm)
            war_y = epi.war_s
        return epi

    wo_tok = stream_wbig(W["Wo"], war_ao)
    gb_tok = load_gb("ln1_g", "ln1_b", [war_ao, wo_tok])
    ao_barrier = (p.barrier_tok("aowA"), p.barrier_tok("aowB"))
    h1_page = AP(pages["h1"], 0, [[1024, 2048], [1, 1024]])
    epi1 = EpiPipe(ctx, x_page, h1_page, ft, banks=(6, 7), gb=(gb_g, gb_b))
    war_aor = {0: None, 1: None}

    def mk_rd(tt):
        par = tt % 2
        lh = p.dma("sync", aorh[par], AP(aoh, tt * 1024 * 128,
                                         [[128, 128], [16384, 8], [1, 128]]),
                   after=[war_aor[par], ao_barrier[0], ao_barrier[1]],
                   sem="aorA" if par == 0 else "aorB")
        ll = p.dma("scalar", aorl[par], AP(aol, tt * 1024 * 128,
                                           [[128, 128], [16384, 8], [1, 128]]),
                   after=[war_aor[par], ao_barrier[0], ao_barrier[1]],
                   sem="aosA" if par == 0 else "aosB")
        return (lh, ll,
                lambda c, par=par: aorh[par][:, c * 128:(c + 1) * 128],
                lambda c, par=par: aorl[par][:, c * 128:(c + 1) * 128])

    war_y = None
    for tt in range(NTT):
        par = tt % 2
        lh_t, ll_t, ah_f, al_f = mk_rd(tt)
        mm = None
        for osup in range(2):
            b = osup
            for c in range(8):
                ah = ah_f(c); al = al_f(c)
                rh = wbig_hi[c][:, osup * 512:(osup + 1) * 512]
                rl = wbig_lo[c][:, osup * 512:(osup + 1) * 512]
                aft = [lh_t, ll_t, wo_tok, gb_tok, war_y] if (c == 0 and osup == 0) else ()
                p.emit("tensor", lambda e, ah=ah, rh=rh, b=b, c=c: e.matmul(
                    ctx.bank32(b, 512), ah, rh, start=(c == 0), stop=False),
                    after=aft, inc=[])
                p.emit("tensor", lambda e, ah=ah, rl=rl, b=b: e.matmul(
                    ctx.bank32(b, 512), ah, rl, start=False, stop=False), inc=[])
                mm = p.emit("tensor", lambda e, al=al, rh=rh, b=b, c=c: e.matmul(
                    ctx.bank32(b, 512), al, rh, start=False, stop=(c == 7)),
                    inc=[("ts", 1)])
        war_aor[par] = mm
        epi1.tile(tt, ctx.bank32(0, 1024), mm)
        war_y = epi1.war_s
    h1_done = p.barrier_tok("dmo")
    ft_done = epi1.war_bank

    a16b = A16(ctx)
    f1h = [a16b.take(512) for _ in range(32)]
    f1l = [a16b.take(512) for _ in range(32)]
    wf1t = [(a16b.take(1024), a16b.take(1024)) for _ in range(2)]
    wf1h_t, wf1l_t = wf1t[0]
    wf1h = [wf1h_t[:, c * 128:(c + 1) * 128] for c in range(8)]
    wf1l = [wf1l_t[:, c * 128:(c + 1) * 128] for c in range(8)]
    wf2t = [(a16b.take(1024), a16b.take(1024)) for _ in range(2)]
    wf2h, wf2l = wf2t[0]
    bf2b = exps[:, 0:1024]
    relu_tmp = gbg
    gb2_tok = load_gb("ln2_g", "ln2_b", [h1_done, ft_done, gb_tok])
    bcol_tok = None
    for fs in range(32):
        bcol_tok = p.dma("sync", cols[:, 16 + fs:17 + fs],
                         AP(W["bff1"].tensor, 128 * fs, [[1, 128], [1, 1]]),
                         after=[bcol_tok, war_ao if fs == 0 else None], sem="bcl")
    tb2 = p.dma("sync", wstage[0:1, 0:1024], W["bff2"], after=[gb2_tok], sem="gbl")
    bf2b_tok = None
    for j in range(2):
        tm = p.ten(lambda e, j=j: e.matmul(ctx.bank32(0, 512), ctx.ones_row,
                                           wstage[0:1, j * 512:(j + 1) * 512],
                                           start=True, stop=True),
                   after=[tb2, ctx.ones_tok, bf2b_tok])
        bf2b_tok = p.vec(lambda e, j=j: e.tensor_copy(
            bf2b[:, j * 512:(j + 1) * 512], ctx.bank32(0, 512)), after=[tm])

    wf1c_h, wf1c_l = pages["wf1h"], pages["wf1l"]
    wf2c_h, wf2c_l = pages["wf2h"], pages["wf2l"]
    cache_toks = []
    sps = (wf1t[0], wf1t[1])
    prev_split = bf2b_tok
    store_war = {0: None, 1: None}
    kci = 0
    jobs = [("f1", dc, pc) for dc in range(8) for pc in range(4)] + \
           [("f2", fc, 0) for fc in range(32)]
    for knd, i0, i1 in jobs:
        par2 = kci % 2
        kci += 1
        sph_, spl_ = sps[par2]
        if knd == "f1":
            src = AP(W["Wff1"].tensor, i0 * 128 * 4096 + i1 * 1024,
                     [[4096, 128], [1, 1024]])
        else:
            src = AP(W["Wff2"].tensor, 128 * i0 * 1024, [[1024, 128], [1, 1024]])
        ld = p.dma("sync", wstage[:, 0:1024], src, after=[prev_split], sem="wsl")
        st = emit_wsplit(ctx, [(wstage[:, 0:1024], sph_, spl_, 1.0)],
                         after=[ld], war=store_war[par2])
        prev_split = st
        w = st
        if knd == "f1":
            dsts = ((sph_, wf1c_h), (spl_, wf1c_l))
            dap = lambda dst, i0=i0, i1=i1: AP(
                dst, i1 * 8 * 131072 + i0 * 128,
                [[1024, 128], [131072, 8], [1, 128]])
        else:
            dsts = ((sph_, wf2c_h), (spl_, wf2c_l))
            dap = lambda dst, i0=i0: AP(dst, i0 * 131072,
                                        [[1024, 128], [1, 1024]])
        for (buf, dst) in dsts:
            w = p.dma("scalar", dap(dst), buf, after=[w],
                      sem="wfcA" if par2 == 0 else "wfcB")
        cache_toks.append(w)
        store_war[par2] = w
    st = prev_split
    wsl_tok = prev_split
    cache_done = (p.barrier_tok("wfcA"), p.barrier_tok("wfcB"))

    h2_page = AP(pages["h2"], 0, [[1025, 2048], [1, 1024]])
    epi2 = EpiPipe(ctx, h1_page, h2_page, ft, banks=(6, 7), gb=(gb_g, gb_b))
    war_y = None
    war_wf1 = {0: None, 1: None}
    war_wf2 = {0: None, 1: None}
    war_f1 = None
    for sup2 in range(4):
        f1_done = None
        for fs in range(32):
            wpar = fs % 2
            wh_t, wl_t = wf1t[wpar]
            l1 = p.dma("sync", wh_t, AP(wf1c_h, fs * 131072, [[1024, 128], [1, 1024]]),
                       after=[cache_done[0], cache_done[1], war_wf1[wpar]],
                       sem="wl1A" if wpar == 0 else "wl1B")
            l2 = p.dma("scalar", wl_t,
                       AP(wf1c_l, fs * 131072, [[1024, 128], [1, 1024]]),
                       after=[cache_done[0], cache_done[1], war_wf1[wpar]],
                       sem="wl2A" if wpar == 0 else "wl2B")
            b = fs % 2
            mm = None
            for c in range(8):
                rh = tl["ft_hi"][:, c * T + sup2 * 512: c * T + sup2 * 512 + 512]
                rl = tl["ft_lo"][:, c * T + sup2 * 512: c * T + sup2 * 512 + 512]
                lh_ = wh_t[:, c * 128:(c + 1) * 128]
                ll_ = wl_t[:, c * 128:(c + 1) * 128]
                aft = [l1, l2, war_f1, bcol_tok, bf2b_tok] if c == 0 else ()
                p.emit("tensor", lambda e, c=c, rh=rh, b=b, lh_=lh_: e.matmul(
                    ctx.bank32(b, 512), lh_, rh, start=(c == 0), stop=False),
                    after=aft, inc=[])
                p.emit("tensor", lambda e, c=c, rl=rl, b=b, lh_=lh_: e.matmul(
                    ctx.bank32(b, 512), lh_, rl, start=False, stop=False), inc=[])
                mm = p.emit("tensor", lambda e, c=c, rh=rh, b=b, ll_=ll_: e.matmul(
                    ctx.bank32(b, 512), ll_, rh, start=False, stop=(c == 7)),
                    inc=[("ts", 1)])
            war_wf1[wpar] = mm
            bcol = cols[:, 16 + fs:17 + fs]
            th = p.act(lambda e, b=b, fs=fs, bcol=bcol: e.activation(
                f1h[fs], ctx.bank32(b, 512), AF.Relu, bias=bcol), after=[mm])
            tt2 = p.vec(lambda e, b=b, bcol=bcol: e.tensor_scalar(
                relu_tmp, ctx.bank32(b, 512), bcol, 0.0, A.add, A.max),
                after=[th, war_f1])
            tt3 = p.vec(lambda e, fs=fs: e.tensor_sub(f1l[fs], relu_tmp,
                                                      f1h[fs]), after=[tt2])
            war_f1 = tt3
            f1_done = tt3
        for half in range(2):
            mm_sub = {0: None, 1: None}
            for fs in range(32):
                wpar = fs % 2
                w2h, w2l = wf2t[wpar]
                l1 = p.dma("sync", w2h, AP(wf2c_h, fs * 131072, [[1024, 128], [1, 1024]]),
                           after=[cache_done[0], cache_done[1], war_wf2[wpar]],
                           sem="wm1A" if wpar == 0 else "wm1B")
                l2 = p.dma("scalar", w2l, AP(wf2c_l, fs * 131072, [[1024, 128], [1, 1024]]),
                           after=[cache_done[0], cache_done[1], war_wf2[wpar]],
                           sem="wm2A" if wpar == 0 else "wm2B")
                for sub in range(2):
                    tt = sup2 * 4 + half * 2 + sub
                    scol = (half * 2 + sub) * 128
                    for osup in range(2):
                        b2 = 2 + sub * 2 + osup
                        lh = f1h[fs][:, scol:scol + 128]
                        llx = f1l[fs][:, scol:scol + 128]
                        rh = w2h[:, osup * 512:(osup + 1) * 512]
                        rl = w2l[:, osup * 512:(osup + 1) * 512]
                        aft = [f1_done, l1, l2, war_y] if (fs == 0 and osup == 0) else ([l1, l2] if osup == 0 and sub == 0 else ())
                        p.emit("tensor", lambda e, lh=lh, rh=rh, b2=b2, fs=fs: e.matmul(
                            ctx.bank32(b2, 512), lh, rh, start=(fs == 0), stop=False),
                            after=aft, inc=[])
                        p.emit("tensor", lambda e, lh=lh, rl=rl, b2=b2: e.matmul(
                            ctx.bank32(b2, 512), lh, rl, start=False, stop=False), inc=[])
                        mm_sub[sub] = p.emit("tensor", lambda e, llx=llx, rh=rh, b2=b2,
                                             fs=fs: e.matmul(
                            ctx.bank32(b2, 512), llx, rh, start=False, stop=(fs == 31)),
                            inc=[("ts", 1)])
                war_wf2[wpar] = mm_sub[1]
            for sub in range(2):
                tt = sup2 * 4 + half * 2 + sub
                mmv = p.vec(lambda e, sub=sub: e.tensor_add(
                    tl["yst"][:, :], ctx.bank32(2 + sub * 2, 1024), bf2b),
                    after=[mm_sub[sub], bf2b_tok, war_y])
                epi2.tile(tt, tl["yst"][:, :], mmv)
                war_y = epi2.war_s
    h2_done = p.barrier_tok("dmo")
    ft2_done = epi2.war_bank
    wsl_tok = p.barrier_tok("wsl")

    wg_tok = st
    for c in range(8):
        ld = p.dma("sync", wstage[:, 0:8], W["Wg"][c * 128:(c + 1) * 128, :],
                   after=[wg_tok], sem="wsl")
        wg_tok = emit_wsplit(ctx, [(wstage[:, 0:8], wf1h[c][:, 0:8],
                                    wf1l[c][:, 0:8], 1.0)],
                             after=[ld] + ([war_wf1[0]] if c == 0 else []),
                             war=wg_tok)
    i8 = cols[:, 56:64]
    t_i8 = p.gp(lambda e: e.iota(i8, pattern=[[1, 8]], base=0, channel_multiplier=0,
                                 allow_small_or_imprecise_dtypes=True), after=[war_f1])
    on8 = cols[0:8, 65:66]
    t_o8 = p.vec(lambda e: e.memset(on8, 1.0), after=[war_f1])
    Isb = exps[0:8, 0:2048]
    Rsb = bdsb[0:8, 0:2048]
    lcp = cols[:, 48:56]
    ecol = cols[:, 64:65]
    gcol = cols[:, 66:67]
    discol = cols[:, 67:75]
    cumcol = cols[:, 75:83]
    negm = cols[:, 83:84]
    eblk = cols[:, 84:100]
    gs_done = None
    war_gb2 = epi2.war_s
    for tt in range(NTT):
        mm = None
        for c in range(8):
            rh = tl["ft_hi"][:, c * T + tt * 128: c * T + (tt + 1) * 128]
            rl = tl["ft_lo"][:, c * T + tt * 128: c * T + (tt + 1) * 128]
            aft = [wg_tok, ft2_done, gs_done, epi2.war_s] if c == 0 else ()
            p.emit("tensor", lambda e, c=c, rh=rh: e.matmul(
                ctx.bank32(4, 8), rh, wf1h[c][:, 0:8], start=(c == 0), stop=False),
                after=aft, inc=[])
            p.emit("tensor", lambda e, c=c, rl=rl: e.matmul(
                ctx.bank32(4, 8), rl, wf1h[c][:, 0:8], start=False, stop=False), inc=[])
            mm = p.emit("tensor", lambda e, c=c, rh=rh: e.matmul(
                ctx.bank32(4, 8), rh, wf1l[c][:, 0:8], start=False, stop=(c == 7)),
                inc=[("ts", 1)])
        t = p.vec(lambda e: e.tensor_copy(lcp, ctx.bank32(4, 8)), after=[mm])
        t = p.vec(lambda e: e.reduce_max(negm, lcp, axis=AX.X, negate=True), after=[t])
        t = p.vec(lambda e: e.tensor_scalar(discol, lcp, negm, 0.0, A.add, A.is_equal), after=[t])
        t = p.vec(lambda e: e.tensor_tensor_scan(cumcol, discol, discol, 0.0,
                                                 A.add, A.bypass), after=[t])
        t = p.vec(lambda e: e.tensor_single_scalar(cumcol, cumcol, 1.0, A.is_equal), after=[t])
        t = p.vec(lambda e: e.tensor_mul(discol, discol, cumcol), after=[t])
        tg = p.act(lambda e: e.activation(cumcol, lcp, AF.Exp, bias=negm,
                                          accum_out=gcol), after=[t])
        tg = p.vec(lambda e: e.reciprocal(gcol, gcol), after=[tg])
        t2 = p.vec(lambda e: e.tensor_mul(cumcol, discol, i8), after=[tg, t_i8])
        t2 = p.vec(lambda e, tt=tt: e.reduce_sum(eblk[:, tt:tt + 1], cumcol,
                                                 axis=AX.X), after=[t2])
        wgc = p.dma("scalar", AP(pages["h2"], tt * 128 * 1025 + 1024,
                                 [[1025, 128], [1, 1]]), gcol,
                    after=[tg, h2_done], sem="gwc",
                    allow_slow_non_contiguous=True)
        vb = (5, 3)[tt % 2]
        tr = p.ten(lambda e, vb=vb: e.transpose(
            ctx.bank32(vb, 128)[0:8, :], discol, tl["idn32"][:, :]),
            after=[t2, gs_done, ctx.ident_tok])
        te = p.vec(lambda e, tt=tt, vb=vb: e.tensor_copy(
            Isb[:, tt * 128:(tt + 1) * 128], ctx.bank32(vb, 128)[0:8, :]),
            after=[tr, war_gb2, wgc])
        gs_done = te
        war_gb2 = None
    t = p.vec(lambda e: e.tensor_tensor_scan(Rsb, Isb, Isb, 0.0, A.add, A.bypass),
              after=[gs_done])
    t = p.vec(lambda e: e.tensor_sub(Rsb, Rsb, Isb), after=[t])
    ncol = cols[0:8, 12:13]
    t = p.vec(lambda e: e.reduce_sum(ncol, Isb, axis=AX.X), after=[t])
    u1 = cols[0:8, 13:14]
    u2 = cols[0:8, 14:15]
    u3 = cols[0:8, 15:16]
    t = p.vec(lambda e: e.tensor_scalar(u1, ncol, 127.0, 1.0 / 128.0, A.add, A.mult), after=[t])
    t = p.vec(lambda e: e.tensor_copy(u2, u1), after=[t])
    for b in (16.0, 8.0, 4.0, 2.0, 1.0):
        t = p.vec(lambda e, b=b: e.tensor_single_scalar(u3, u2, b, A.is_ge), after=[t])
        t = p.vec(lambda e, b=b: e.scalar_tensor_tensor(u2, u3, -b, u2, A.mult, A.add), after=[t])
    t = p.vec(lambda e: e.tensor_sub(u1, u1, u2), after=[t])
    t = p.vec(lambda e: e.tensor_single_scalar(u1, u1, 128.0, A.mult), after=[t])
    tr = p.ten(lambda e: e.transpose(ctx.bank32(4, 8)[0:1, :], u1,
                                     tl["idn32"][0:8, 0:8]), after=[t, ctx.ident_tok])
    prow = cols[0:1, 228:236]
    t = p.vec(lambda e: e.tensor_copy(prow, ctx.bank32(4, 8)[0:1, :]), after=[tr])
    t = p.vec(lambda e: e.tensor_tensor_scan(prow, prow, prow, 0.0, A.add, A.bypass)
              if False else e.tensor_copy(cols[0:1, 236:244], prow), after=[t])
    t = p.vec(lambda e: e.tensor_tensor_scan(prow, cols[0:1, 236:244],
                                             cols[0:1, 236:244], 0.0, A.add, A.bypass), after=[t])
    t = p.vec(lambda e: e.memset(cols[0:1, 244:245], 0.0), after=[t])
    t = p.vec(lambda e: e.tensor_copy(cols[0:1, 245:252], prow[:, 0:7]), after=[t])
    tr = p.ten(lambda e: e.transpose(ctx.bank32(4, 8)[0:8, 0:1],
                                     cols[0:1, 244:252], tl["idn32"][0:1, 0:1]),
               after=[t, ctx.ident_tok])
    bcol8 = cols[0:8, 252:253]
    t = p.vec(lambda e: e.tensor_copy(bcol8, ctx.bank32(4, 8)[0:8, 0:1]), after=[tr])
    t = p.vec(lambda e: e.tensor_single_scalar(Rsb, Rsb, bcol8, A.add), after=[t])
    t = p.vec(lambda e: e.tensor_mul(Rsb, Rsb, Isb), after=[t])
    posrow = exps[0:1, 0:2048]
    mmp = None
    for g0 in range(0, 2048, 512):
        mmp = p.ten(lambda e, g0=g0: e.matmul(
            ctx.bank32(g0 // 512, 512)[0:1, :],
            on8, Rsb[:, g0:g0 + 512], start=True, stop=True),
            after=[t, t_o8, mmp])
    for g0 in range(0, 2048, 512):
        mmp = p.vec(lambda e, g0=g0: e.tensor_copy(
            posrow[:, g0:g0 + 512], ctx.bank32(g0 // 512, 512)[0:1, :]), after=[mmp])
    pf = tl["nh"][:, 0:23].bitcast(F32) if False else None
    t0 = p.vec(lambda e: e.memset(relu_tmp[:, 0:23], 2048.0), after=[war_f1])
    t0b = p.vec(lambda e: e.memset(relu_tmp[:, 23:46], 0.0), after=[t0])
    t0c = p.vec(lambda e: e.memset(wstage[0:1, 0:1024], 0.0), after=[t0b])
    w0 = p.dma("scalar", AP(pages["src"], 0, [[23, 128], [1, 23]]),
               relu_tmp[:, 0:23], after=[t0], sem="spf")
    w1 = p.dma("scalar", AP(pages["esrc"], 0, [[23, 128], [1, 23]]),
               relu_tmp[:, 23:46], after=[t0b], sem="spf")
    w2a = p.dma("scalar", AP(pages["h2"], 2048 * 1025, [[1024, 1], [1, 1024]]),
                wstage[0:1, 0:1024], after=[t0c, h2_done], sem="spf")
    w2b = p.dma("scalar", AP(pages["h2"], 2048 * 1025 + 1024, [[1, 1], [1, 1]]),
                wstage[0:1, 0:1], after=[t0c, h2_done], sem="spf")
    pf_done = p.barrier_tok("spf")
    sc_done = None
    icol = tl["icol"]
    poscol = cols[:, 253:254]
    tidcol = cols[:, 254:255]
    for tt in range(NTT):
        vb = (5, 3)[tt % 2]
        tr = p.ten(lambda e, tt=tt, vb=vb: e.transpose(
            ctx.bank32(vb, 1), posrow[:, tt * 128:(tt + 1) * 128],
            tl["idn32"][0:1, 0:1]), after=[mmp, sc_done, ctx.ident_tok])
        t = p.vec(lambda e, vb=vb: e.tensor_copy(poscol, ctx.bank32(vb, 1)),
                  after=[tr])
        t = p.vec(lambda e: e.tensor_copy(icol[:, 0:1], poscol), after=[t])
        tid = p.gp(lambda e, tt=tt: e.iota(tidcol, pattern=[[0, 1]], base=tt * 128,
                                           channel_multiplier=1,
                                           allow_small_or_imprecise_dtypes=True),
                   after=[sc_done])
        s1 = p.emit("gpsimd", lambda e: e.indirect_dma_start(
            AP(pages["src"], 0, [[1, 2944], [1, 1]]),
            IndirectOffsetOnAxis(icol[:, 0:1], 0), tidcol, None),
            after=[t, tid, pf_done], inc=[("sca", 16)])
        s2 = p.emit("gpsimd", lambda e, tt=tt: e.indirect_dma_start(
            AP(pages["esrc"], 0, [[1, 2944], [1, 1]]),
            IndirectOffsetOnAxis(icol[:, 0:1], 0), eblk[:, tt:tt + 1], None),
            after=[t, pf_done], inc=[("sca", 16)])
        sc_done = s2
    sort_done = p.barrier_tok("sca")

    a16c = A16(ctx)
    slotg = [a16c.take(2080).bitcast(F32) for _ in range(2)]
    h2Ts = a16c.take(2048).bitcast(F32)
    we1g = [a16c.take(2048).bitcast(F32) for _ in range(4)]
    f1rp = [a16c.take(2048).bitcast(F32) for _ in range(2)]
    ats2 = a16c.take(2048).bitcast(F32)
    we2g = [a16c.take(2048).bitcast(F32) for _ in range(4)]
    be1t = a16c.take(8192).bitcast(F32)
    be2t = a16c.take(2048).bitcast(F32)
    osb = a16c.take(2048).bitcast(F32)
    icp = a16c.take(46).bitcast(U32)
    icw1 = a16c.take(46).bitcast(U32)
    icw2 = a16c.take(46).bitcast(U32)
    icbe = a16c.take(46).bitcast(U32)
    pcol = cols[:, 239:240]
    t_pc = p.gp(lambda e: e.iota(pcol, pattern=[[0, 1]], base=0, channel_multiplier=1,
                                 allow_small_or_imprecise_dtypes=True), after=[sort_done])
    yst = tl["yst"]
    posf = yst[:, 0:23]
    w1f = yst[:, 32:55]
    w2f = yst[:, 64:87]
    e_ball = yst[:, 96:119]
    t_pos = p.dma("sync", posf, AP(pages["src"], 0, [[1, 128], [128, 23]]),
                  after=[sort_done], sem="pidx", allow_slow_non_contiguous=True)
    t_icp = p.vec(lambda e: e.tensor_copy(icp[:, :], posf), after=[t_pos])
    t_er = p.dma("sync", e_ball[0:1, :], AP(pages["esrc"], 0, [[1, 1], [128, 23]]),
                 after=[sort_done], sem="pidx2", allow_slow_non_contiguous=True)
    t_eb = p.ten(lambda e: e.matmul(ctx.bank32(4, 23), ctx.ones_row,
                                    e_ball[0:1, :], start=True, stop=True),
                 after=[t_er, ctx.ones_tok, sort_done], sem="ts")
    ebv = p.vec(lambda e: e.tensor_copy(e_ball, ctx.bank32(4, 23)), after=[t_eb])
    t_be = p.vec(lambda e: e.tensor_copy(icbe[:, :], e_ball), after=[ebv])
    t = p.vec(lambda e: e.tensor_single_scalar(w1f, e_ball, 1024.0, A.mult),
              after=[ebv, t_icp])
    t = p.vec(lambda e: e.tensor_scalar(w1f, w1f, pcol, 0.0, A.add, A.bypass),
              after=[t, t_pc])
    t_w1 = p.vec(lambda e: e.tensor_copy(icw1[:, :], w1f), after=[t])
    t = p.vec(lambda e: e.tensor_single_scalar(w2f, e_ball, 4096.0, A.mult),
              after=[ebv])
    t = p.vec(lambda e: e.tensor_scalar(w2f, w2f, pcol, 0.0, A.add, A.bypass),
              after=[t, t_pc])
    t_w2 = p.vec(lambda e: e.tensor_copy(icw2[:, :], w2f), after=[t])

    war_slot = {0: [], 1: []}
    war_h2T = None
    war_f1r = None
    war_o = None
    war_we1 = {0: None, 1: None, 2: None, 3: None}
    war_we2 = {0: None, 1: None, 2: None, 3: None}
    war_be = None
    h2flat = AP(pages["h2"], 0, [[1025, 2049], [1, 1025]])
    we1flat = AP(W["We1"].tensor, 0, [[4096, 8192], [1, 4096]])
    we2flat = AP(W["We2"].tensor, 0, [[1024, 32768], [1, 1024]])
    be1flat = AP(W["be1"].tensor, 0, [[4096, 8], [1, 4096]])
    be2flat = AP(W["be2"].tensor, 0, [[1024, 8], [1, 1024]])
    moeflat = AP(pages["moe"], 0, [[1024, 2049], [1, 1024]])
    w1sem = ("w1A", "w1B", "w1C", "w1D")
    w2sem = ("w2A", "w2B", "w2C", "w2D")
    pend_sc = None
    for sl in range(23):
        par = sl % 2
        g1 = p.emit("gpsimd", lambda e, par=par, sl=sl: e.indirect_dma_start(
            slotg[par][:, 0:1025], None, h2flat,
            IndirectOffsetOnAxis(icp[:, sl:sl + 1], 0)),
            after=[t_icp, h2_done] + war_slot[par],
            inc=[("sgA" if par == 0 else "sgB", 16)])
        be_tok = None
        th2 = None
        th2_slot = {0: war_h2T or ebv, 1: war_h2T or ebv}
        for c in range(8):
            vb = (4, 5)[c % 2]
            trp = p.ten(lambda e, c=c, vb=vb, par=par: e.transpose(
                ctx.bank32(vb, 128), slotg[par][:, c * 128:(c + 1) * 128],
                tl["idn32"][:, :]),
                after=[g1, th2_slot[c % 2], ctx.ident_tok])
            th2 = p.act(lambda e, c=c, vb=vb: e.copy(
                h2Ts[:, c * 128:(c + 1) * 128], ctx.bank32(vb, 128)), after=[trp])
            th2_slot[c % 2] = th2
        o_mm = None
        relu_prev = war_f1r
        tr_war = {0: war_h2T, 1: war_h2T}
        for fp_ in range(4):
            pb = (0, 2)[fp_ % 2]
            f1r = f1rp[fp_ % 2]
            mmf = None
            for dc in range(8):
                gidx = fp_ * 8 + dc
                wp = gidx % 4
                gw = p.emit("gpsimd", lambda e, dc=dc, fp_=fp_, wp=wp, sl=sl:
                            e.indirect_dma_start(
                    we1g[wp][:, 0:1024], None, we1flat,
                    IndirectOffsetOnAxis(icw1[:, sl:sl + 1], 0),
                    element_offset=dc * 524288 + fp_ * 1024),
                    after=[t_w1, war_we1[wp]], inc=[(w1sem[wp], 16)])
                lh = h2Ts[:, dc * 128:(dc + 1) * 128]
                for osup in range(2):
                    rh = we1g[wp][:, osup * 512:(osup + 1) * 512]
                    aft = ([gw] if osup == 0 else []) + \
                          ([th2, relu_prev] if (dc == 0 and osup == 0) else [])
                    mmf = p.emit("tensor", lambda e, lh=lh, rh=rh, pb=pb, osup=osup,
                                 dc=dc: e.matmul(
                        ctx.bank32(pb + osup, 512), lh, rh, start=(dc == 0),
                        stop=False), after=aft, inc=[("ts", 1)])
                war_we1[wp] = mmf
            if fp_ == 0:
                gb1 = p.emit("gpsimd", lambda e, sl=sl: e.indirect_dma_start(
                    be1t[0:2, 0:4096], None, be1flat,
                    IndirectOffsetOnAxis(icbe[0:2, sl:sl + 1], 0)),
                    after=[t_be, war_be], inc=[("sbe", 16)])
                gb2_ = p.emit("gpsimd", lambda e, sl=sl: e.indirect_dma_start(
                    be2t[0:2, 0:1024], None, be2flat,
                    IndirectOffsetOnAxis(icbe[0:2, sl:sl + 1], 0)),
                    after=[t_be, war_be], inc=[("sbe", 16)])
                be_tok = p.barrier_tok("sbe")
                if pend_sc is not None:
                    tev_p, sl_p = pend_sc
                    war_o = p.emit("gpsimd", lambda e, sl_p=sl_p:
                                   e.indirect_dma_start(
                        moeflat, IndirectOffsetOnAxis(icp[:, sl_p:sl_p + 1], 0),
                        osb, None), after=[tev_p], inc=[("mos", 16)])
                    pend_sc = None
            for osup in range(2):
                lb = be1t[0:1, fp_ * 1024 + osup * 512: fp_ * 1024 + (osup + 1) * 512]
                mmf = p.emit("tensor", lambda e, lb=lb, pb=pb, osup=osup: e.matmul(
                    ctx.bank32(pb + osup, 512), ctx.ones_row, lb, start=False,
                    stop=True), after=[be_tok] if osup == 0 else (), inc=[("ts", 1)])
            relu_done = None
            for osup in range(2):
                relu_done = p.act(lambda e, pb=pb, osup=osup, f1r=f1r: e.activation(
                    f1r[:, osup * 512:(osup + 1) * 512], ctx.bank32(pb + osup, 512),
                    AF.Relu), after=[mmf, relu_prev], sem="ss")
            relu_prev = relu_done
            for g4 in range(2):
                tb_ = (4, 5)[g4]
                tk = None
                for j in range(4):
                    fc = g4 * 4 + j
                    tk = p.ten(lambda e, fc=fc, j=j, tb_=tb_, f1r=f1r: e.transpose(
                        ctx.bank32(tb_, 128, j * 128),
                        f1r[:, fc * 128:(fc + 1) * 128], tl["idn32"][:, :]),
                        after=[relu_done, tr_war[g4], ctx.ident_tok] if j == 0 else ())
                te2 = p.act(lambda e, tb_=tb_, g4=g4: e.copy(
                    ats2[:, g4 * 512:(g4 + 1) * 512], ctx.bank32(tb_, 512)),
                    after=[tk], sem="ss")
                for j in range(4):
                    fc = g4 * 4 + j
                    gfc = fp_ * 8 + fc
                    wp2 = gfc % 4
                    gw2 = p.emit("gpsimd", lambda e, gfc=gfc, wp2=wp2, sl=sl:
                                 e.indirect_dma_start(
                        we2g[wp2][:, 0:1024], None, we2flat,
                        IndirectOffsetOnAxis(icw2[:, sl:sl + 1], 0),
                        element_offset=gfc * 128 * 1024),
                        after=[t_w2, war_we2[wp2]], inc=[(w2sem[wp2], 16)])
                    lh2 = ats2[:, g4 * 512 + j * 128:g4 * 512 + (j + 1) * 128]
                    for osup in range(2):
                        b2 = 6 + osup
                        rh2 = we2g[wp2][:, osup * 512:(osup + 1) * 512]
                        aft = [te2, gw2] + ([war_o] if (gfc == 0 and osup == 0) else [])
                        o_mm = p.emit("tensor", lambda e, lh2=lh2, rh2=rh2, b2=b2,
                                      gfc=gfc: e.matmul(
                            ctx.bank32(b2, 512), lh2, rh2, start=(gfc == 0),
                            stop=False), after=aft, inc=[("ts", 1)])
                    war_we2[wp2] = o_mm
                tr_war[g4] = o_mm
            war_f1r = o_mm
        for osup in range(2):
            b2 = 6 + osup
            lb = be2t[0:1, osup * 512:(osup + 1) * 512]
            o_mm = p.emit("tensor", lambda e, lb=lb, b2=b2: e.matmul(
                ctx.bank32(b2, 512)[0:128, :], ctx.ones_row, lb, start=False,
                stop=True), after=[be_tok], inc=[("ts", 1)])
        war_be = o_mm
        tev = p.act(lambda e, par=par: e.activation(
            osb, ctx.bank32(6, 1024), AF.Copy,
            scale=slotg[par][:, 1024:1025]),
            after=[o_mm, war_o], sem="ss")
        pend_sc = (tev, sl)
        war_h2T = th2
        war_slot[par] = [th2, tev]
    tev_p, sl_p = pend_sc
    moe_done = p.emit("gpsimd", lambda e, sl_p=sl_p: e.indirect_dma_start(
        moeflat, IndirectOffsetOnAxis(icp[:, sl_p:sl_p + 1], 0), osb, None),
        after=[tev_p], inc=[("mos", 16)])
    moe_barrier = p.barrier_tok("mos")

    gb3_tok = load_gb("ln3_g", "ln3_b", [mmp, sort_done])
    h3_page = AP(pages["h3"], 0, [[1024, 2048], [1, 1024]])
    epi3 = EpiPipe(ctx, h2_page, h3_page, ft, banks=(6, 7), gb=(gb_g, gb_b))
    epi3.war_bank = moe_barrier
    war_y2 = {0: None, 1: None}
    ysts = (tl["yst"][:, :], exps[:, 0:1024])
    for tt in range(NTT):
        pr6 = tt % 2
        ly = p.dma("sync", ysts[pr6],
                   AP(pages["moe"], tt * 128 * 1024, [[1024, 128], [1, 1024]]),
                   after=[moe_barrier, war_y2[pr6], gb3_tok],
                   sem="moelA" if pr6 == 0 else "moelB")
        epi3.tile(tt, ysts[pr6], ly)
        war_y2[pr6] = epi3.war_s
    h3_done = p.barrier_tok("dmo")
    ft3_done = epi3.war_bank

    wout_tok = stream_wbig(W["Wout"], moe_barrier)
    epi4 = EpiPipe(ctx, x_page, out_page, None, banks=(6, 7))
    war_y = None
    for tt in range(NTT):
        mm = None
        for osup in range(2):
            b = osup
            for c in range(8):
                lh = tl["ft_hi"][:, c * T + tt * 128: c * T + (tt + 1) * 128]
                ll = tl["ft_lo"][:, c * T + tt * 128: c * T + (tt + 1) * 128]
                rh = wbig_hi[c][:, osup * 512:(osup + 1) * 512]
                rl = wbig_lo[c][:, osup * 512:(osup + 1) * 512]
                aft = [ft3_done, h3_done, wout_tok, war_y] if (c == 0 and osup == 0) else ()
                p.emit("tensor", lambda e, lh=lh, rh=rh, b=b, c=c: e.matmul(
                    ctx.bank32(b, 512), lh, rh, start=(c == 0), stop=False),
                    after=aft, inc=[])
                p.emit("tensor", lambda e, lh=lh, rl=rl, b=b: e.matmul(
                    ctx.bank32(b, 512), lh, rl, start=False, stop=False), inc=[])
                mm = p.emit("tensor", lambda e, ll=ll, rh=rh, b=b, c=c: e.matmul(
                    ctx.bank32(b, 512), ll, rh, start=False, stop=(c == 7)),
                    inc=[("ts", 1)])
        epi4.tile(tt, ctx.bank32(0, 1024), mm)
        war_y = epi4.war_s
    return p.barrier_tok("dmo")


def emit_fence(p, sems=("dmo", "vs", "ss", "ts", "gs")):
    toks = [p.barrier_tok(s) for s in sems]
    for e in ENGINES:
        p.emit(e, lambda en: en.nop(), after=toks, inc=[("fnc", 1)])
    return toks


def build_full():
    nc = bass.Bass("TRN2", target_bir_lowering=False)
    F = K_INPUT_SHAPES
    ins = {}
    for name, shape in F.items():
        ins[name] = nc.dram_tensor(name, list(shape), F32, kind="ExternalInput").ap()
    out = nc.dram_tensor("out", [T, D], F32, kind="ExternalOutput").ap()

    dram = {"aoh": nc.dram_tensor("aoh", [16 * 8 * 128 * 128], BF16, kind="Internal"),
            "aol": nc.dram_tensor("aol", [16 * 8 * 128 * 128], BF16, kind="Internal"),
            "qp0": nc.dram_tensor("qp0", [128 * 1152], F32, kind="Internal"),
            "qp1": nc.dram_tensor("qp1", [128 * 1152], F32, kind="Internal"),
            "bnd0": nc.dram_tensor("bnd0", [128 * 2176], F32, kind="Internal"),
            "bnd1": nc.dram_tensor("bnd1", [128 * 2176], F32, kind="Internal"),
            "b80": nc.dram_tensor("b80", [128 * 256], F32, kind="Internal"),
            "b81": nc.dram_tensor("b81", [128 * 256], F32, kind="Internal")}
    pages = {"pg1": nc.dram_tensor("pg1", [T * D], F32, kind="Internal"),
             "pg2": nc.dram_tensor("pg2", [T * D], F32, kind="Internal"),
             "pg3": nc.dram_tensor("pg3", [T * D], F32, kind="Internal"),
             "h2": nc.dram_tensor("h2pg", [2049 * 1025 + 64], F32, kind="Internal"),
             "moe": nc.dram_tensor("moepg", [2049 * 1024], F32, kind="Internal"),
             "rth": nc.dram_tensor("rth", [1024 * 2048], BF16, kind="Internal"),
             "rtl": nc.dram_tensor("rtl", [1024 * 2048], BF16, kind="Internal"),
             "src": nc.dram_tensor("srcp", [2944], F32, kind="Internal"),
             "esrc": nc.dram_tensor("esrcp", [2944], F32, kind="Internal"),
             "wf1h": nc.dram_tensor("wf1ch", [1024 * 4096], BF16, kind="Internal"),
             "wf1l": nc.dram_tensor("wf1cl", [1024 * 4096], BF16, kind="Internal"),
             "wf2h": nc.dram_tensor("wf2ch", [4096 * 1024], BF16, kind="Internal"),
             "wf2l": nc.dram_tensor("wf2cl", [4096 * 1024], BF16, kind="Internal")}

    es = ExitStack()
    tiles = alloc_tiles(nc, es)
    psum = es.enter_context(nc.psum_tensor("ps", [128, 4096], F32))
    prog = Prog()
    ctx = Ctx(nc, prog, tiles, None, psum, dram)
    make_ident(ctx)
    emit_masks(ctx)
    ft = (tiles["ft_hi"][:, :], tiles["ft_lo"][:, :])
    pg1 = AP(pages["pg1"], 0, [[D, T], [1, D]])
    pg2 = AP(pages["pg2"], 0, [[D, T], [1, D]])
    pg3 = AP(pages["pg3"], 0, [[D, T], [1, D]])

    x0 = ins["L0_attn_x"]
    emit_page_to_ft(ctx, x0, ft)
    emit_fence(prog)
    emit_l0_attn(ctx, ft, ins["wqT"], ins["wkT"], ins["wvT"], ins["woT"],
                 ins["L0_attn_pek"], ins["L0_attn_pev"], x0, pg1, ft)
    emit_fence(prog)
    emit_l0_ffn(ctx, pg1, ins["l0fwT"], pg2, ft)
    emit_fence(prog)
    emit_l1_attn(ctx, ft, ins["L1_attn_w_qkv_full"], ins["L1_attn_w_qk_red"],
                 ins["L1_attn_w_v_red"], ins["L1_attn_w_proj"], pg2, pg3, ft)
    emit_fence(prog)
    Wd = {"Wqkv": ins["L1_ffn_Wqkv"], "Wo": ins["L1_ffn_Wo"], "Wr": ins["L1_ffn_Wr"],
          "u_bias": ins["u_bias"], "v_bias": ins["v_bias"],
          "ln1_g": ins["ln1_g"], "ln1_b": ins["ln1_b"],
          "Wff1": ins["L1_ffn_Wff1"], "bff1": ins["bff1"],
          "Wff2": ins["L1_ffn_Wff2"], "bff2": ins["bff2"],
          "ln2_g": ins["ln2_g"], "ln2_b": ins["ln2_b"],
          "Wg": ins["L1_ffn_Wg"], "We1": ins["We1"], "be1": ins["L1_ffn_be1"],
          "We2": ins["We2"], "be2": ins["L1_ffn_be2"],
          "ln3_g": ins["ln3_g"], "ln3_b": ins["ln3_b"], "Wout": ins["L1_ffn_Wout"]}
    l1f_pages = {"h1": pages["pg1"], "h2": pages["h2"], "h3": pages["pg2"],
                 "moe": pages["moe"], "rth": pages["rth"], "rtl": pages["rtl"],
                 "src": pages["src"], "esrc": pages["esrc"],
                 "wf1h": pages["wf1h"], "wf1l": pages["wf1l"],
                 "wf2h": pages["wf2h"], "wf2l": pages["wf2l"]}
    fin = emit_l1_ffn(ctx, ft, Wd, pg3, l1f_pages, out)
    prog.emit("sync", lambda e: e.nop(), after=[prog.barrier_tok("dmo")],
              inc=[("fnc", 1)])

    sems = {n: es.enter_context(nc.semaphore(n)) for n in prog.cnt.keys()}
    with nc.Block() as block:
        prog.replay(nc, block, sems)
    es.close()
    return nc


K_INPUT_SHAPES = {
    "L0_attn_x": (T, D),
    "wqT": (D, D), "wkT": (D, D), "wvT": (D, D), "woT": (D, D),
    "L0_attn_pek": (2049, 64), "L0_attn_pev": (2049, 64),
    "l0fwT": (D, D),
    "L1_attn_w_qkv_full": (D, 1536), "L1_attn_w_qk_red": (D, 512),
    "L1_attn_w_v_red": (D, 512), "L1_attn_w_proj": (D, D),
    "L1_ffn_Wqkv": (D, 3 * D), "L1_ffn_Wo": (D, D), "L1_ffn_Wr": (D, D),
    "u_bias": (1, 1024), "v_bias": (1, 1024),
    "ln1_g": (1, D), "ln1_b": (1, D),
    "L1_ffn_Wff1": (D, 4096), "bff1": (1, 4096),
    "L1_ffn_Wff2": (4096, D), "bff2": (1, D),
    "ln2_g": (1, D), "ln2_b": (1, D),
    "L1_ffn_Wg": (D, 8), "We1": (8 * D, 4096), "L1_ffn_be1": (8, 4096),
    "We2": (8 * 4096, D), "L1_ffn_be2": (8, D),
    "ln3_g": (1, D), "ln3_b": (1, D), "L1_ffn_Wout": (D, D),
}

_NC_CACHE = []


def kernel(**inputs):
    c = np.ascontiguousarray
    shared = {
        "wqT": c(inputs["L0_attn_Wq"].T), "wkT": c(inputs["L0_attn_Wk"].T),
        "wvT": c(inputs["L0_attn_Wv"].T), "woT": c(inputs["L0_attn_Wo"].T),
        "L0_attn_pek": c(inputs["L0_attn_pek"]),
        "L0_attn_pev": c(inputs["L0_attn_pev"]),
        "l0fwT": c(inputs["L0_ffn_weight"].T),
        "L1_attn_w_qkv_full": c(inputs["L1_attn_w_qkv_full"]),
        "L1_attn_w_qk_red": c(inputs["L1_attn_w_qk_red"]),
        "L1_attn_w_v_red": c(inputs["L1_attn_w_v_red"]),
        "L1_attn_w_proj": c(inputs["L1_attn_w_proj"]),
        "L1_ffn_Wqkv": c(inputs["L1_ffn_Wqkv"]), "L1_ffn_Wo": c(inputs["L1_ffn_Wo"]),
        "L1_ffn_Wr": c(inputs["L1_ffn_Wr"]),
        "u_bias": c(inputs["L1_ffn_u_bias"]).reshape(1, 1024),
        "v_bias": c(inputs["L1_ffn_v_bias"]).reshape(1, 1024),
        "ln1_g": c(inputs["L1_ffn_ln1_g"]).reshape(1, D),
        "ln1_b": c(inputs["L1_ffn_ln1_b"]).reshape(1, D),
        "L1_ffn_Wff1": c(inputs["L1_ffn_Wff1"]),
        "bff1": c(inputs["L1_ffn_bff1"]).reshape(1, 4096),
        "L1_ffn_Wff2": c(inputs["L1_ffn_Wff2"]),
        "bff2": c(inputs["L1_ffn_bff2"]).reshape(1, D),
        "ln2_g": c(inputs["L1_ffn_ln2_g"]).reshape(1, D),
        "ln2_b": c(inputs["L1_ffn_ln2_b"]).reshape(1, D),
        "L1_ffn_Wg": c(inputs["L1_ffn_Wg"]),
        "We1": c(inputs["L1_ffn_We1"]).reshape(8 * D, 4096),
        "L1_ffn_be1": c(inputs["L1_ffn_be1"]),
        "We2": c(inputs["L1_ffn_We2"]).reshape(8 * 4096, D),
        "L1_ffn_be2": c(inputs["L1_ffn_be2"]),
        "ln3_g": c(inputs["L1_ffn_ln3_g"]).reshape(1, D),
        "ln3_b": c(inputs["L1_ffn_ln3_b"]).reshape(1, D),
        "L1_ffn_Wout": c(inputs["L1_ffn_Wout"]),
    }
    x = np.ascontiguousarray(inputs["L0_attn_x"], dtype=np.float32)
    if not _NC_CACHE:
        _NC_CACHE.append(build_full())
    nc = _NC_CACHE[0]
    in_maps = []
    for core in range(8):
        m = dict(shared)
        m["L0_attn_x"] = c(x[core])
        in_maps.append(m)
    res = run_bass_kernel_spmd(nc, in_maps, core_ids=list(range(8)))
    return np.stack([r["out"] for r in res.results], axis=0)


# version 10
# speedup vs baseline: 1.0155x; 1.0155x over previous
import math
from contextlib import ExitStack

import numpy as np

import concourse.bass as bass
import concourse.mybir as mybir
from concourse.bass_utils import run_bass_kernel_spmd
from concourse.ap import AP
from concourse.bass_core_rust import IndirectOffsetOnAxis

F32 = mybir.dt.float32
BF16 = mybir.dt.bfloat16
U32 = mybir.dt.uint32
A = mybir.AluOpType
AF = mybir.ActivationFunctionType
AX = mybir.AxisListType

T = 2048
D = 1024
NTT = T // 128
NFC = D // 128
NEG = -1.0e30

ENGINES = ("sync", "scalar", "vector", "tensor", "gpsimd")


class Tok(tuple):
    __slots__ = ()


class Prog:

    def __init__(self):
        self.ops = []
        self.cnt = {}

    def emit(self, eng, fn, after=(), inc=()):
        waits = [t for t in after if t is not None]
        incs = list(inc)
        self.ops.append((eng, fn, waits, incs))
        tok = None
        for s, amt in incs:
            self.cnt[s] = self.cnt.get(s, 0) + amt
            if tok is None:
                tok = Tok((s, self.cnt[s]))
        return tok

    def dma(self, queue, out, in_, after=(), sem="dma", **kw):
        return self.emit(queue, lambda e: e.dma_start(out=out, in_=in_, **kw),
                         after=after, inc=[(sem, 16)])

    def ten(self, fn, after=(), sem="ts"):
        return self.emit("tensor", fn, after=after, inc=[(sem, 1)])

    def vec(self, fn, after=(), sem="vs"):
        return self.emit("vector", fn, after=after, inc=[(sem, 1)])

    def act(self, fn, after=(), sem="ss"):
        return self.emit("scalar", fn, after=after, inc=[(sem, 1)])

    def gp(self, fn, after=(), sem="gs"):
        return self.emit("gpsimd", fn, after=after, inc=[(sem, 1)])

    def barrier_tok(self, sem):
        return Tok((sem, self.cnt.get(sem, 0)))

    def replay(self, nc, block, sems):
        per = {e: [] for e in ENGINES}
        for op in self.ops:
            per[op[0]].append(op)

        def mk(ename):
            def body(eng):
                seen = {}
                for _, fn, waits, incs in per[ename]:
                    for s, v in waits:
                        if seen.get(s, -1) < v:
                            eng.wait_ge(sems[s], v)
                            seen[s] = v
                    inst = fn(eng)
                    for s, amt in incs:
                        inst = inst.then_inc(sems[s], amt)
            return body

        for ename in ENGINES:
            getattr(block, ename)(mk(ename))



def bf16x3_mm(p, psum_ap, lhs_hi, lhs_lo, rhs_hi, rhs_lo, after=(), start=True,
              stop=True, sem="ts", tile_position=None):
    p.emit("tensor", lambda e: e.matmul(psum_ap, lhs_hi, rhs_hi, start=start,
                                        stop=False, tile_position=tile_position),
           after=after, inc=[])
    p.emit("tensor", lambda e: e.matmul(psum_ap, lhs_hi, rhs_lo, start=False,
                                        stop=False, tile_position=tile_position),
           inc=[])
    return p.emit("tensor", lambda e: e.matmul(psum_ap, lhs_lo, rhs_hi,
                                               start=False, stop=stop,
                                               tile_position=tile_position),
                  inc=[(sem, 1)])


def round_half_even(p, v_out, v_in, tmp_rem, tmp_bit, tmp_par, nbits=8,
                    after=(), out_dtype_note=None):
    half = float(2 ** (nbits - 1))
    t = p.vec(lambda e: e.tensor_single_scalar(v_out, v_in, half, A.add), after=after)
    t = p.vec(lambda e: e.tensor_copy(tmp_rem, v_out), after=[t])
    for i in range(nbits):
        b = float(2 ** (nbits - 1 - i))
        t = p.vec(lambda e, b=b: e.tensor_single_scalar(tmp_bit, tmp_rem, b, A.is_ge), after=[t])
        t = p.vec(lambda e, b=b: e.scalar_tensor_tensor(tmp_rem, tmp_bit, -b, tmp_rem, A.mult, A.add), after=[t])
    t = p.vec(lambda e: e.scalar_tensor_tensor(tmp_par, tmp_rem, 0.5, tmp_bit, A.is_equal, A.mult), after=[t])
    t = p.vec(lambda e: e.tensor_single_scalar(tmp_bit, tmp_rem, 0.5, A.is_gt), after=[t])
    t = p.vec(lambda e: e.tensor_add(tmp_par, tmp_par, tmp_bit), after=[t])
    t = p.vec(lambda e: e.tensor_sub(v_out, v_out, tmp_rem), after=[t])
    t = p.vec(lambda e: e.tensor_add(v_out, v_out, tmp_par), after=[t])
    t = p.vec(lambda e: e.tensor_single_scalar(v_out, v_out, -half, A.add), after=[t])
    return t



SEM_NAMES = ("dma",
             "dmb",
             "dmg",
             "dmo",
             "ts",
             "vs",
             "ss",
             "gs",
             )


class Ctx:
    def __init__(self, nc, prog, tiles, sems, psum, dram=None):
        self.nc = nc
        self.p = prog
        self.t = tiles
        self.sems = sems
        self.psum = psum
        self.dram = dram or {}

    def bank32(self, b, cols=512, off=0):
        return self.psum[:, b * 512 + off:b * 512 + off + cols]

    def bank16(self, b, cols=1024, off=0):
        full = self.psum[:, b * 512:(b + 1) * 512].bitcast(BF16)
        return full[:, off:off + cols]


def alloc_tiles(nc, es):
    t = {}

    def sb(name, shape, dtype):
        t[name] = es.enter_context(nc.sbuf_tensor(name, shape, dtype))
        return t[name]

    sb("ft_hi", [128, 8 * T], BF16)
    sb("ft_lo", [128, 8 * T], BF16)
    sb("idn", [128, 128], BF16)
    sb("idn32", [128, 128], F32)
    sb("cols", [128, 512], F32)
    sb("icol", [128, 16], U32)
    sb("xa", [128, 1024], F32)
    sb("xb", [128, 1024], F32)
    sb("s0", [128, 1024], F32)
    sb("d0", [128, 1024], F32)
    sb("n0", [128, 1024], F32)
    sb("nh", [128, 1024], BF16)
    sb("nl", [128, 1024], BF16)
    sb("sqd", [128, 1024], F32)
    sb("yst", [128, 1024], F32)
    sb("AR32", [128, 6144], F32)
    sb("AR16", [128, 41984], BF16)
    return t


def tok_rows(page_ap, tt, w=1024):
    return page_ap[tt * 128:(tt + 1) * 128, :]


def ft_chunk(tile, c, c0=0, w=T):
    return tile[:, c * T + c0:c * T + c0 + w]


def make_ident(ctx):
    p = ctx.p
    for nm in ("idn", "idn32"):
        tile = ctx.t[nm]
        tk = p.gp(lambda e, tile=tile: e.memset(tile[:, :], 0.0))
        p.gp(lambda e, tile=tile: e.affine_select(
            out=tile[:, :], in_=tile[:, :], compare_op=A.not_equal, fill=1.0,
            base=0, pattern=[[-1, 128]], channel_multiplier=1), after=[tk])
    ctx.ones_row = ctx.t["cols"][0:1, 100:228]
    ctx.ones_row32 = ctx.t["cols"][32:33, 100:228]
    p.vec(lambda e: e.memset(ctx.ones_row, 1.0))
    p.vec(lambda e: e.memset(ctx.ones_row32, 1.0))
    ctx.ident_tok = p.barrier_tok("gs")
    ctx.ones_tok = p.barrier_tok("vs")
    return ctx.ident_tok



class EpiPipe:

    def __init__(self, ctx, x_page, out_page, ft_pair, banks=(6, 7), gb=None,
                 eps=1e-5, skip_ln=False):
        self.c = ctx
        self.x_page = x_page
        self.out_page = out_page
        self.ft = ft_pair
        self.banks = banks
        self.gb = gb
        self.eps = eps
        self.skip_ln = skip_ln
        self.war_x = {0: None, 1: None}
        self.war_s = None
        self.war_n = []
        self.war_split = None
        self.war_bank = None
        self.out_toks = []
        self.ft_toks = []

    def tile(self, tt, y_src, y_tok, y_is_add_of=None):
        c, p = self.c, self.c.p
        tl = c.t
        s0, d0, n0 = tl["s0"][:, :], tl["d0"][:, :], tl["n0"][:, :]
        cols = tl["cols"]
        m_col, v_col, r_col = cols[:, 0:1], cols[:, 1:2], cols[:, 2:3]
        par = tt % 2
        if y_is_add_of is None:
            xs = (tl["xa"] if par == 0 else tl["xb"])[:, :]
            lx = p.dma("sync", xs, self.x_page[tt * 128:(tt + 1) * 128, :],
                       after=[self.war_x[par]], sem="xlA" if par == 0 else "xlB")
            t = p.vec(lambda e: e.tensor_add(s0, y_src, xs),
                      after=[y_tok, lx, self.war_s])
            self.war_x[par] = t
        else:
            t = p.vec(lambda e: e.tensor_add(s0, y_src, y_is_add_of),
                      after=[y_tok, self.war_s])
            self.war_x[par] = t
        if self.skip_ln:
            tn = p.vec(lambda e: e.tensor_copy(n0, s0), after=[t] + self.war_n)
        else:
            t = p.vec(lambda e: e.reduce_sum(m_col, s0, axis=AX.X, negate=True), after=[t])
            t = p.vec(lambda e: e.tensor_single_scalar(m_col, m_col, 1.0 / 1024.0, A.mult), after=[t])
            t = p.vec(lambda e: e.tensor_single_scalar(d0, s0, m_col, A.add), after=[t])
            tsq = p.act(lambda e: e.activation(tl["sqd"][:, :], d0, AF.Square,
                                               accum_out=v_col), after=[t])
            tv = p.vec(lambda e: e.tensor_scalar(r_col, v_col, 1.0 / 1024.0,
                                                 self.eps, A.mult, A.add), after=[tsq])
            tv = p.act(lambda e: e.activation(r_col, r_col, AF.Sqrt), after=[tv])
            tv = p.vec(lambda e: e.reciprocal(r_col, r_col), after=[tv])
            if self.gb is None:
                tn = p.vec(lambda e: e.tensor_single_scalar(n0, d0, r_col, A.mult),
                           after=[tv] + self.war_n)
            else:
                g_b, b_b = self.gb
                tn = p.vec(lambda e: e.scalar_tensor_tensor(n0, d0, r_col, g_b,
                                                            A.mult, A.mult),
                           after=[tv] + self.war_n)
                tn = p.vec(lambda e: e.tensor_add(n0, n0, b_b), after=[tn])
        self.war_s = tn
        st = p.dma("scalar", self.out_page[tt * 128:(tt + 1) * 128, :], n0,
                   after=[tn], sem="dmo")
        self.out_toks.append(st)
        n_free = [st]
        if self.ft is not None:
            nh, nl = tl["nh"][:, :], tl["nl"][:, :]
            t_hi = p.vec(lambda e: e.tensor_copy(nh, n0), after=[tn, self.war_split])
            t_lo = p.vec(lambda e: e.tensor_sub(nl, n0, nh), after=[t_hi])
            n_free.append(t_lo)
            tks = []
            for (tile, bank, rdy) in ((nh, self.banks[0], t_hi),
                                      (nl, self.banks[1], t_lo)):
                tk = None
                for cc in range(8):
                    tk = p.ten(lambda e, tile=tile, cc=cc, bank=bank: e.transpose(
                        c.bank16(bank, 128, cc * 128),
                        tile[:, cc * 128:(cc + 1) * 128], tl["idn"][:, :]),
                        after=[rdy, self.war_bank, c.ident_tok] if cc == 0 else ())
                tks.append(tk)
            self.war_split = tks[1]
            evs = []
            for (bank, dst, tk) in ((self.banks[0], self.ft[0], tks[0]),
                                    (self.banks[1], self.ft[1], tks[1])):
                dst_ap = AP(dst.tensor, dst.offset + tt * 128,
                            [list(dst.ap[0]), [T, 8], [1, 128]])
                evs.append(p.act(lambda e, bank=bank, dst_ap=dst_ap: e.copy(
                    dst_ap, c.bank16(bank, 1024)), after=[tk]))
            self.war_bank = evs[-1]
            self.ft_toks.append(evs)
        self.war_n = n_free
        return st




def emit_bcast_row(ctx, dst_ap, src_row_ap, after=(), bank=4):
    p = ctx.p
    n = src_row_ap.shape[-1]
    tm = p.ten(lambda e: e.matmul(ctx.bank32(bank, n), ctx.ones_row,
                                  src_row_ap, start=True, stop=True),
               after=list(after) + [ctx.ones_tok])
    return p.vec(lambda e: e.tensor_copy(dst_ap, ctx.bank32(bank, n)), after=[tm])


def emit_clip(p, out_ap, in_ap, lo, hi, after=()):
    return p.vec(lambda e: e.tensor_scalar(out_ap, in_ap, float(lo), float(hi),
                                           A.max, A.min), after=after)



def emit_l0_ffn(ctx, x_page, wT_dram, out_page, ft_out, banks_tr=(6, 7)):
    p = ctx.p
    tl = ctx.t
    AR32, AR16 = tl["AR32"], tl["AR16"]
    cols = tl["cols"]
    wst = AR32[:, 0:1024]
    zt = AR32[:, 1024:2048]
    rt = AR32[:, 2048:3072]
    q0 = AR32[:, 3072:4096]
    q1 = AR32[:, 4096:5120]
    q2 = AR32[:, 5120:6144]
    wq = [AR16[:, c * 1024:(c + 1) * 1024] for c in range(8)]
    aq = AR16[:, 8192:9216]
    aqT = AR16[:, 9216:10240]

    tsum = None
    ld_prev = None
    for c in range(8):
        ld = p.dma("sync", wst, wT_dram[c * 128:(c + 1) * 128, :],
                   after=[tsum], sem="l0fw")
        tsum = p.vec(lambda e, c=c: e.reduce_sum(cols[:, 8 + c:9 + c], wst,
                                                 axis=AX.X, apply_absolute_value=True),
                     after=[ld, tsum])
    tsum = p.vec(lambda e: e.reduce_sum(cols[:, 16:17], cols[:, 8:16], axis=AX.X),
                 after=[tsum])
    ones_col = cols[:, 17:18]
    t1 = p.vec(lambda e: e.memset(ones_col, 1.0), after=[])
    tm = p.ten(lambda e: e.matmul(ctx.bank32(4, 1)[0:1, :], ones_col, cols[:, 16:17],
                                  start=True, stop=True), after=[tsum, t1])
    sc = cols[:, 18:19]
    tm = p.vec(lambda e: e.tensor_scalar(sc[0:1, :], ctx.bank32(4, 1)[0:1, :],
                                         1.0 / (1024.0 * 1024.0), 1e-8, A.mult, A.add),
               after=[tm])
    tws = p.vec(lambda e: e.tensor_copy(cols[0:1, 19:20], sc[0:1, :]), after=[tm])
    tm = p.vec(lambda e: e.reciprocal(sc[0:1, :], sc[0:1, :]), after=[tws])
    tb = emit_bcast_row(ctx, sc, sc[0:1, :], after=[tm])
    tbw = emit_bcast_row(ctx, cols[:, 19:20], cols[0:1, 19:20], after=[tws, tb])

    wq_tok = None
    for c in range(8):
        ld = p.dma("sync", wst, wT_dram[c * 128:(c + 1) * 128, :],
                   after=[wq_tok if wq_tok is not None else tsum], sem="l0fw")
        t = p.vec(lambda e, c=c: e.tensor_single_scalar(zt, wst, sc, A.mult),
                  after=[ld, tb, wq_tok])
        t = emit_clip(p, zt, zt, -2.0, 2.0, after=[t])
        t = round_half_even(p, rt, zt, q0, q1, q2, nbits=3, after=[t])
        t = p.vec(lambda e, c=c: e.tensor_scalar(wq[c], rt, -1.0, 1.0,
                                                 A.max, A.min), after=[t])
        wq_tok = t

    epi = EpiPipe(ctx, x_page, out_page, ft_out, banks=banks_tr)
    war_aq = None
    war_aqT = None
    war_x = None
    war_y = None
    for tt in range(NTT):
        xq = (tl["xa"] if tt % 2 == 0 else tl["xb"])[:, :]
        lx = p.dma("sync", xq, x_page[tt * 128:(tt + 1) * 128, :],
                   after=[epi.war_x[tt % 2], war_x], sem="xlA" if tt % 2 == 0 else "xlB")
        asc = cols[:, 20:21]
        t = p.vec(lambda e, xq=xq: e.reduce_max(asc, xq, axis=AX.X,
                                                apply_absolute_value=True), after=[lx])
        t = p.vec(lambda e: e.tensor_single_scalar(asc, asc, 1e-8, A.add), after=[t])
        tg = p.vec(lambda e: e.tensor_copy(cols[:, 21:22], asc), after=[t])
        t = p.vec(lambda e: e.reciprocal(asc, asc), after=[tg])
        t = p.vec(lambda e: e.tensor_single_scalar(asc, asc, 127.0, A.mult), after=[t])
        t = p.vec(lambda e, xq=xq: e.tensor_single_scalar(zt, xq, asc, A.mult), after=[t, war_aq, wq_tok])
        war_x = t
        t = emit_clip(p, zt, zt, -128.0, 127.0, after=[t])
        t = round_half_even(p, rt, zt, q0, q1, q2, nbits=8, after=[t])
        taq = p.vec(lambda e: e.tensor_copy(aq, rt), after=[t])
        tk = None
        for cc in range(8):
            tk = p.ten(lambda e, cc=cc: e.transpose(
                ctx.bank16(5, 128, cc * 128), aq[:, cc * 128:(cc + 1) * 128],
                tl["idn"][:, :]), after=[taq, war_aqT, tm, ctx.ident_tok] if cc == 0 else ())
        war_aq = tk
        tev = p.act(lambda e: e.copy(aqT, ctx.bank16(5, 1024)), after=[tk])
        war_aqT = tev
        mmt = None
        for osup in range(2):
            for cc in range(8):
                mmt = p.ten(lambda e, osup=osup, cc=cc: e.matmul(
                    ctx.bank32(osup, 512),
                    aqT[:, cc * 128:(cc + 1) * 128],
                    wq[cc][:, osup * 512:(osup + 1) * 512],
                    start=(cc == 0), stop=(cc == 7)),
                    after=[tev, wq_tok, tbw, war_y] if (cc == 0 and osup == 0) else ())
        scol = cols[:, 22:23]
        t = p.vec(lambda e: e.tensor_single_scalar(scol, cols[:, 21:22], cols[:, 19:20], A.mult), after=[tg, mmt, tbw])
        t = p.vec(lambda e: e.tensor_single_scalar(scol, scol, 1.0 / 127.0, A.mult), after=[t])
        ty = p.vec(lambda e: e.tensor_single_scalar(tl["yst"][:, :], ctx.bank32(0, 1024), scol, A.mult), after=[t])
        war_y = ty
        epi.tile(tt, tl["yst"][:, :], ty, y_is_add_of=xq)
    return epi


def emit_masks(ctx):
    p = ctx.p
    cols = ctx.t["cols"]
    md = cols[:, 256:384]
    me = cols[:, 384:512]
    t0 = p.gp(lambda e: e.memset(md, 0.0))
    t0 = p.gp(lambda e: e.affine_select(out=md, in_=md, compare_op=A.is_ge,
                                        fill=NEG, base=0, pattern=[[-1, 128]],
                                        channel_multiplier=1), after=[t0])
    t1 = p.gp(lambda e: e.memset(me, 0.0), after=[t0])
    t1 = p.gp(lambda e: e.affine_select(out=me, in_=me, compare_op=A.is_ge,
                                        fill=NEG, base=-1, pattern=[[1, 128]],
                                        channel_multiplier=-1), after=[t1])
    ctx.mdiag = md
    ctx.medge = me
    ctx.mask_tok = p.barrier_tok("gs")


def emit_wsplit(ctx, pieces, after=(), war=None):
    p = ctx.p
    last = war
    for (st, dh, dl, sc) in pieces:
        if sc == 1.0:
            t = p.vec(lambda e, st=st, dh=dh: e.tensor_copy(dh, st),
                      after=list(after) + [last])
            t = p.vec(lambda e, st=st, dh=dh, dl=dl: e.tensor_sub(dl, st, dh),
                      after=[t])
        else:
            t = p.vec(lambda e, st=st, dh=dh, sc=sc: e.tensor_single_scalar(
                dh, st, sc, A.mult), after=list(after) + [last])
            t = p.vec(lambda e, st=st, dh=dh, dl=dl, sc=sc: e.scalar_tensor_tensor(
                dl, st, sc, dh, A.mult, A.subtract), after=[t])
        last = t
    return last


class A16:
    def __init__(self, ctx, ncols=41984):
        self.t = ctx.t["AR16"]
        self.off = 0
        self.ncols = ncols

    def take(self, cols):
        assert self.off + cols <= self.ncols, (self.off, cols)
        ap = self.t[:, self.off:self.off + cols]
        self.off += cols
        return ap


class A32:
    def __init__(self, ctx, ncols=6144):
        self.t = ctx.t["AR32"]
        self.off = 0
        self.ncols = ncols

    def take(self, cols):
        assert self.off + cols <= self.ncols, (self.off, cols)
        ap = self.t[:, self.off:self.off + cols]
        self.off += cols
        return ap




def emit_projT(ctx, ft, wsl_hi, wsl_lo, wc0, fw, dst_hi, dst_lo, banks,
               after=(), war=None, evac_scale=None, dst32=None):
    p = ctx.p
    ft_hi, ft_lo = ft
    evac_tok = {0: None, 1: None}
    done = None
    for ts in range(4):
        b = banks[ts % 2]
        mm = None
        for c in range(8):
            lh = wsl_hi[c][:, wc0:wc0 + fw]
            ll = wsl_lo[c][:, wc0:wc0 + fw]
            rh = ft_hi[:, c * T + ts * 512: c * T + ts * 512 + 512]
            rl = ft_lo[:, c * T + ts * 512: c * T + ts * 512 + 512]
            aft = (list(after) + [war, evac_tok[ts % 2]]) if c == 0 else ()
            p.emit("tensor", lambda e, lh=lh, rh=rh, b=b, c=c: e.matmul(
                ctx.bank32(b, 512)[0:fw, :], lh, rh, start=(c == 0), stop=False),
                after=aft, inc=[])
            p.emit("tensor", lambda e, lh=lh, rl=rl, b=b: e.matmul(
                ctx.bank32(b, 512)[0:fw, :], lh, rl, start=False, stop=False), inc=[])
            mm = p.emit("tensor", lambda e, ll=ll, rh=rh, b=b, c=c: e.matmul(
                ctx.bank32(b, 512)[0:fw, :], ll, rh, start=False, stop=(c == 7)),
                inc=[("ts", 1)])
        ps = ctx.bank32(b, 512)[0:fw, :]
        if dst32 is not None:
            d32 = dst32[0:fw, ts * 512:(ts + 1) * 512]
            tok = p.act(lambda e, d32=d32, ps=ps: e.copy(d32, ps), after=[mm])
        else:
            dh = dst_hi[0:fw, ts * 512:(ts + 1) * 512]
            dl = dst_lo[0:fw, ts * 512:(ts + 1) * 512]
            th = p.act(lambda e, dh=dh, ps=ps: e.copy(dh, ps), after=[mm])
            tok = p.vec(lambda e, dl=dl, ps=ps, dh=dh: e.tensor_sub(dl, ps, dh),
                        after=[th])
        evac_tok[ts % 2] = tok
        done = tok
    return done, (evac_tok[0], evac_tok[1])


def emit_l1_attn(ctx, ft, wqkv_dram, wqkr_dram, wvr_dram, wproj_dram,
                 x_page, out_page, ft_out):
    p = ctx.p
    tl = ctx.t
    a16 = A16(ctx)
    a32 = A32(ctx)

    qt_hi, qt_lo = a16.take(T), a16.take(T)
    kt_hi, kt_lo = a16.take(T), a16.take(T)
    vt_hi, vt_lo = a16.take(T), a16.take(T)
    v_hi, v_lo = a16.take(T), a16.take(T)
    at_h2 = (a16.take(1280), a16.take(1280))
    at_l2 = (a16.take(1280), a16.take(1280))
    ats = a16.take(2048)
    aos = a16.take(512)
    aorh = [a16.take(1024) for _ in range(2)]
    aorl = [a16.take(1024) for _ in range(2)]
    wsl_hi = [a16.take(384) for _ in range(8)]
    wsl_lo = [a16.take(384) for _ in range(8)]
    AR16 = tl["AR16"]
    wp_hi = [AR16[:, c * 1024:(c + 1) * 1024] for c in range(8)]
    wp_lo = [AR16[:, 8192 + c * 1024:8192 + (c + 1) * 1024] for c in range(8)]
    wstage = a32.take(1024)
    exps2 = (a32.take(1280), a32.take(1280))

    cols = tl["cols"]
    rmaxs = (cols[:, 3:4], cols[:, 8:9])
    rsums = (cols[:, 4:5], cols[:, 9:10])
    rrecs = (cols[:, 5:6], cols[:, 12:13])
    aoh, aol = ctx.dram["aoh"], ctx.dram["aol"]

    def s1(pr, nk_win, hd_q, qi, hx, slice_tok):
        u = {"qi": qi, "hx": hx, "par": (qi * 2 + hx) % 2}
        par = u["par"]
        ks0 = max(0, qi - (nk_win - 1))
        nk = qi - ks0 + 1
        kw = nk * 128
        u["ks0"], u["nk"] = ks0, nk
        hbase = hd_q[hx]
        u["hbase"] = hbase
        hslice = slice(hbase[0], hbase[0] + hbase[1])
        sb = (0, 1, 2) if par == 0 else (5, 6, 7)
        u["sb"] = sb
        mm = None
        first = True
        for g0 in range(0, kw, 512):
            gw = min(512, kw - g0)
            bidx = sb[g0 // 512]
            lhsT_h = qt_hi[hslice, qi * 128:(qi + 1) * 128]
            lhsT_l = qt_lo[hslice, qi * 128:(qi + 1) * 128]
            rh = kt_hi[hslice, ks0 * 128 + g0: ks0 * 128 + g0 + gw]
            rl = kt_lo[hslice, ks0 * 128 + g0: ks0 * 128 + g0 + gw]
            aft = [slice_tok, S["trio"][par]] if first else ()
            tp = (hbase[0], 0) if hbase[0] == 96 else None
            p.emit("tensor", lambda e, a=lhsT_h, b=rh, bidx=bidx, tp=tp: e.matmul(
                ctx.bank32(bidx, b.shape[-1]), a, b, start=True, stop=False,
                tile_position=tp), after=aft, inc=[])
            p.emit("tensor", lambda e, a=lhsT_h, b=rl, bidx=bidx, tp=tp: e.matmul(
                ctx.bank32(bidx, b.shape[-1]), a, b, start=False, stop=False,
                tile_position=tp), inc=[])
            mm = p.emit("tensor", lambda e, a=lhsT_l, b=rh, bidx=bidx, tp=tp: e.matmul(
                ctx.bank32(bidx, b.shape[-1]), a, b, start=False, stop=True,
                tile_position=tp), inc=[("ts", 1)])
            first = False

        def bankslice(c0, w):
            b = sb[c0 // 512]
            return ctx.bank32(b, w, c0 % 512)
        tmask = p.vec(lambda e, apx=bankslice((nk - 1) * 128, 128):
                      e.tensor_add(apx, apx, ctx.mdiag), after=[mm, ctx.mask_tok])
        if qi >= nk_win - 1 and nk_win > 1:
            tmask = p.vec(lambda e, apx=bankslice(0, 128):
                          e.tensor_add(apx, apx, ctx.medge), after=[tmask])
        sc_all = ctx.bank32(sb[0], kw)
        tmx = p.vec(lambda e, sc_all=sc_all, par=par: e.reduce_max(
            rmaxs[par], sc_all, axis=AX.X, negate=True), after=[tmask])
        texp = p.act(lambda e, sc_all=sc_all, kw=kw, par=par: e.activation(
            exps2[par][:, 0:kw], sc_all, AF.Exp, bias=rmaxs[par],
            accum_out=rsums[par]), after=[tmx, S["war_exps"][par]])
        S["trio"][par] = texp
        trc = p.vec(lambda e, par=par: e.reciprocal(rrecs[par], rsums[par]),
                    after=[texp])
        th = p.act(lambda e, kw=kw, par=par: e.activation(
            at_h2[par][:, 0:kw], exps2[par][:, 0:kw], AF.Copy,
            scale=rrecs[par]), after=[trc, S["war_at"][par]])
        tl_ = p.vec(lambda e, kw=kw, par=par: e.scalar_tensor_tensor(
            at_l2[par][:, 0:kw], exps2[par][:, 0:kw], rrecs[par],
            at_h2[par][:, 0:kw], A.mult, A.subtract), after=[th])
        S["war_exps"][par] = tl_
        u["tl_"] = tl_
        return u

    def s2(u, ao_chunk, war_aos):
        qi, hx, par = u["qi"], u["hx"], u["par"]
        ks0, nk = u["ks0"], u["nk"]
        hbase = u["hbase"]
        hx64 = hx * 64
        first_pv = True
        tpv = None
        last_t = None
        for gbase in range(0, nk, 4):
            n = min(4, nk - gbase)
            gpar = (gbase // 4) % 2
            t2 = None
            for j in range(n):
                kc = gbase + j
                aft = [u["tl_"]] + ([S["war_b4"], ctx.ident_tok] if j == 0 else [])
                t1 = p.ten(lambda e, kc=kc, j=j, par=par: e.transpose(
                    ctx.bank16(4, 128, j * 256),
                    at_h2[par][:, kc * 128:(kc + 1) * 128], tl["idn"][:, :]),
                    after=aft)
                t2 = p.ten(lambda e, kc=kc, j=j, par=par: e.transpose(
                    ctx.bank16(4, 128, j * 256 + 128),
                    at_l2[par][:, kc * 128:(kc + 1) * 128], tl["idn"][:, :]))
            last_t = t2
            te = p.act(lambda e, gpar=gpar, n=n: e.copy(
                ats[:, gpar * 1024:gpar * 1024 + n * 256],
                ctx.bank16(4, n * 256)), after=[t2, S["war_ats"][gpar]])
            S["war_b4"] = te
            for j in range(n):
                kc = gbase + j
                kcol = (ks0 + kc) * 128
                vsel = slice(hx64, hx64 + 64)
                vh = v_hi[:, kcol:kcol + 128][:, vsel]
                vl = v_lo[:, kcol:kcol + 128][:, vsel]
                ah = ats[:, gpar * 1024 + j * 256:gpar * 1024 + j * 256 + 128]
                al = ats[:, gpar * 1024 + j * 256 + 128:gpar * 1024 + j * 256 + 256]
                pv_first = first_pv
                pv_last = (kc == nk - 1)
                tp = (0, hx64)
                ops = ctx.bank32(3, 128)[hx64:hx64 + 64, :]
                aft = [te] + ([S["war_pv"]] if (pv_first and hx == 0) else [])
                p.emit("tensor", lambda e, vh=vh, ah=ah, tp=tp, ops=ops,
                       st=pv_first: e.matmul(ops, vh, ah, start=st,
                                             stop=False, tile_position=tp),
                       after=aft, inc=[])
                p.emit("tensor", lambda e, vl=vl, ah=ah, tp=tp, ops=ops: e.matmul(
                    ops, vl, ah, start=False, stop=False, tile_position=tp), inc=[])
                tpv = p.emit("tensor", lambda e, vh=vh, al=al, tp=tp, ops=ops,
                             sp=pv_last: e.matmul(ops, vh, al, start=False,
                                                  stop=sp, tile_position=tp),
                             inc=[("ts", 1)])
                first_pv = False
            S["war_ats"][gpar] = tpv
        S["war_at"][par] = last_t
        S["done_pv"] = tpv
        if hx == 1:
            slot = qi % 2
            sh = aos[:, slot * 256:slot * 256 + 128]
            sl = aos[:, slot * 256 + 128:slot * 256 + 256]
            th = p.act(lambda e, sh=sh: e.copy(sh, ctx.bank32(3, 128)),
                       after=[tpv, war_aos[slot]])
            tsub = p.vec(lambda e, sh=sh, sl=sl: e.tensor_sub(
                sl, ctx.bank32(3, 128), sh), after=[th])
            S["war_pv"] = tsub
            S["done_pv"] = tsub
            ro = qi * 1024 + ao_chunk * 128
            w1 = p.dma("scalar", AP(aoh, ro * 128, [[128, 128], [1, 128]]),
                       sh, after=[tsub], sem="aowA" if slot == 0 else "aowB")
            w2 = p.dma("scalar", AP(aol, ro * 128, [[128, 128], [1, 128]]),
                       sl, after=[tsub], sem="aowA" if slot == 0 else "aowB")
            war_aos[slot] = w2

    def attention(pr, nk_win, hd_q, ao_chunk, slice_tok, war_ao, war_aos):
        units = [(qi, hx) for qi in range(NTT) for hx in range(2)]
        ulist = [s1(pr, nk_win, hd_q, 0, 0, slice_tok)]
        for i in range(len(units)):
            if i + 1 < len(units):
                qi, hx = units[i + 1]
                ulist.append(s1(pr, nk_win, hd_q, qi, hx, slice_tok))
            s2(ulist[i], ao_chunk, war_aos)
        return S["done_pv"]

    slice_free = None
    war_proj = (None, None)
    war_ao = None
    war_aos = {0: None, 1: None}
    S = {"trio": {0: None, 1: None}, "war_exps": {0: None, 1: None},
         "war_at": {0: None, 1: None}, "war_ats": {0: None, 1: None},
         "war_b4": None, "war_pv": None, "done_pv": None}
    for pr in range(4):
        sl_tok = None
        for c in range(8):
            src = AP(wqkv_dram.tensor, c * 128 * 1536 + 64 * 2 * pr,
                     [[1536, 128], [512, 3], [1, 128]])
            ld = p.dma("sync", wstage[:, 0:384], src, after=[slice_free], sem="wsl")
            sl_tok = emit_wsplit(
                ctx, [(wstage[:, 0:128], wsl_hi[c][:, 0:128], wsl_lo[c][:, 0:128], 1.0),
                      (wstage[:, 128:256], wsl_hi[c][:, 128:256], wsl_lo[c][:, 128:256], 0.125),
                      (wstage[:, 256:384], wsl_hi[c][:, 256:384], wsl_lo[c][:, 256:384], 1.0)],
                after=[ld], war=sl_tok)
            slice_free = sl_tok
        tq, _ = emit_projT(ctx, ft, wsl_hi, wsl_lo, 0, 128, qt_hi, qt_lo, (0, 1),
                           after=[sl_tok], war=war_proj[0])
        tk, _ = emit_projT(ctx, ft, wsl_hi, wsl_lo, 128, 128, kt_hi, kt_lo, (0, 1),
                           after=[sl_tok], war=tq)
        tv, _ = emit_projT(ctx, ft, wsl_hi, wsl_lo, 256, 128, vt_hi, vt_lo, (0, 1),
                           after=[sl_tok], war=tk)
        slice_free = tv
        tev = None
        for tt in range(NTT):
            t1 = p.ten(lambda e, tt=tt: e.transpose(
                ctx.bank16(4, 128, 0), vt_hi[:, tt * 128:(tt + 1) * 128],
                tl["idn"][:, :]), after=[tv, tev, ctx.ident_tok])
            t2 = p.ten(lambda e, tt=tt: e.transpose(
                ctx.bank16(4, 128, 128), vt_lo[:, tt * 128:(tt + 1) * 128],
                tl["idn"][:, :]))
            ta = p.act(lambda e, tt=tt: e.copy(v_hi[:, tt * 128:(tt + 1) * 128],
                                               ctx.bank16(4, 128, 0)), after=[t2])
            tev = p.act(lambda e, tt=tt: e.copy(v_lo[:, tt * 128:(tt + 1) * 128],
                                                ctx.bank16(4, 128, 128)), after=[ta])
        war_ao = attention(pr, 9, ((0, 64), (64, 64)), pr, tev, war_ao, war_aos)
        war_proj = (war_ao, war_ao)

    for quad in range(2):
        sl_tok = None
        for c in range(8):
            src = AP(wqkr_dram.tensor, c * 128 * 512 + 128 * quad,
                     [[512, 128], [256, 2], [1, 128]])
            ld = p.dma("sync", wstage[:, 0:256], src, after=[slice_free], sem="wsl")
            sl_tok = emit_wsplit(
                ctx, [(wstage[:, 0:128], wsl_hi[c][:, 0:128], wsl_lo[c][:, 0:128],
                       1.0 / math.sqrt(32.0)),
                      (wstage[:, 128:256], wsl_hi[c][:, 128:256], wsl_lo[c][:, 128:256], 1.0)],
                after=[ld], war=sl_tok)
            slice_free = sl_tok
        tq, _ = emit_projT(ctx, ft, wsl_hi, wsl_lo, 0, 128, qt_hi, qt_lo, (0, 1),
                           after=[sl_tok], war=war_proj[0])
        tkk, _ = emit_projT(ctx, ft, wsl_hi, wsl_lo, 128, 128, kt_hi, kt_lo, (0, 1),
                            after=[sl_tok], war=tq)
        slice_free = tkk
        war_proj = (tkk, tkk)
        for sub in range(2):
            pr = quad * 2 + sub
            svtok = None
            for c in range(8):
                src = AP(wvr_dram.tensor, c * 128 * 512 + 128 * pr,
                         [[512, 128], [1, 128]])
                ld = p.dma("sync", wstage[:, 0:128], src, after=[slice_free], sem="wsl")
                svtok = emit_wsplit(
                    ctx, [(wstage[:, 0:128], wsl_hi[c][:, 256:384],
                           wsl_lo[c][:, 256:384], 1.0)], after=[ld], war=svtok)
                slice_free = svtok
            tv, _ = emit_projT(ctx, ft, wsl_hi, wsl_lo, 256, 128, vt_hi, vt_lo,
                               (0, 1), after=[svtok], war=war_proj[0])
            slice_free = tv
            tev = None
            for tt in range(NTT):
                t1 = p.ten(lambda e, tt=tt: e.transpose(
                    ctx.bank16(4, 128, 0), vt_hi[:, tt * 128:(tt + 1) * 128],
                    tl["idn"][:, :]), after=[tv, tev, ctx.ident_tok])
                t2 = p.ten(lambda e, tt=tt: e.transpose(
                    ctx.bank16(4, 128, 128), vt_lo[:, tt * 128:(tt + 1) * 128],
                    tl["idn"][:, :]))
                ta = p.act(lambda e, tt=tt: e.copy(v_hi[:, tt * 128:(tt + 1) * 128],
                                                   ctx.bank16(4, 128, 0)), after=[t2])
                tev = p.act(lambda e, tt=tt: e.copy(v_lo[:, tt * 128:(tt + 1) * 128],
                                                    ctx.bank16(4, 128, 128)), after=[ta])
            hA = (sub * 64, 32)
            hB = (sub * 64 + 32, 32)
            war_ao = attention(pr, 3, (hA, hB), 4 + pr, tev, war_ao, war_aos)
            war_proj = (war_ao, war_proj[1])

    wp_tok, wst_free = None, None
    for c in range(8):
        ld = p.dma("sync", wstage, wproj_dram[c * 128:(c + 1) * 128, :],
                   after=[wst_free, war_ao if c == 0 else None], sem="wsl")
        wp_tok = emit_wsplit(ctx, [(wstage, wp_hi[c], wp_lo[c], 1.0)],
                             after=[ld, war_ao if c == 0 else None], war=wp_tok)
        wst_free = wp_tok

    ao_barrier = (p.barrier_tok("aowA"), p.barrier_tok("aowB"))
    epi = EpiPipe(ctx, x_page, out_page, ft_out, banks=(6, 7))
    wyq = [None, None]
    war_aor = {0: None, 1: None}
    for tt in range(NTT):
        par = tt % 2
        war_y = wyq[0]
        lh = p.dma("sync", aorh[par], AP(aoh,
                                         tt * 1024 * 128,
                                         [[128, 128], [16384, 8], [1, 128]]),
                   after=[war_aor[par], ao_barrier[0], ao_barrier[1]],
                   sem="aorA" if par == 0 else "aorB")
        ll = p.dma("scalar", aorl[par], AP(aol,
                                           tt * 1024 * 128,
                                           [[128, 128], [16384, 8], [1, 128]]),
                   after=[war_aor[par], ao_barrier[0], ao_barrier[1]],
                   sem="aosA" if par == 0 else "aosB")
        mm = None
        for osup in range(2):
            b = (tt % 2) * 2 + osup
            for c in range(8):
                ah = aorh[par][:, c * 128:(c + 1) * 128]
                al = aorl[par][:, c * 128:(c + 1) * 128]
                rh = wp_hi[c][:, osup * 512:(osup + 1) * 512]
                rl = wp_lo[c][:, osup * 512:(osup + 1) * 512]
                aft = [lh, ll, wp_tok, war_y] if (c == 0 and osup == 0) else ()
                p.emit("tensor", lambda e, ah=ah, rh=rh, b=b, c=c: e.matmul(
                    ctx.bank32(b, 512), ah, rh, start=(c == 0), stop=False),
                    after=aft, inc=[])
                p.emit("tensor", lambda e, ah=ah, rl=rl, b=b: e.matmul(
                    ctx.bank32(b, 512), ah, rl, start=False, stop=False), inc=[])
                mm = p.emit("tensor", lambda e, al=al, rh=rh, b=b, c=c: e.matmul(
                    ctx.bank32(b, 512), al, rh, start=False, stop=(c == 7)),
                    inc=[("ts", 1)])
        war_aor[par] = mm
        epi.tile(tt, ctx.bank32((tt % 2) * 2, 1024), mm)
        wyq = [wyq[1], epi.war_s]
    return epi


def emit_page_to_ft(ctx, page, ft_pair, banks=(6, 7)):
    p = ctx.p
    tl = ctx.t
    war = {0: None, 1: None}
    war_split = None
    war_bank = None
    done = None
    for tt in range(NTT):
        par = tt % 2
        xs = (tl["xa"] if par == 0 else tl["xb"])[:, :]
        lx = p.dma("sync", xs, page[tt * 128:(tt + 1) * 128, :],
                   after=[war[par]], sem="xlA" if par == 0 else "xlB")
        nh, nl = tl["nh"][:, :], tl["nl"][:, :]
        t_hi = p.vec(lambda e, xs=xs: e.tensor_copy(nh, xs), after=[lx, war_split])
        t_lo = p.vec(lambda e, xs=xs: e.tensor_sub(nl, xs, nh), after=[t_hi])
        war[par] = t_lo
        tks = []
        for (tile, bank, rdy) in ((nh, banks[0], t_hi), (nl, banks[1], t_lo)):
            tk = None
            for cc in range(8):
                tk = p.ten(lambda e, tile=tile, cc=cc, bank=bank: e.transpose(
                    ctx.bank16(bank, 128, cc * 128),
                    tile[:, cc * 128:(cc + 1) * 128], tl["idn"][:, :]),
                    after=[rdy, war_bank, ctx.ident_tok] if cc == 0 else ())
            tks.append(tk)
        war_split = tks[1]
        evs = []
        for (bank, dst, tk) in ((banks[0], ft_pair[0], tks[0]),
                                (banks[1], ft_pair[1], tks[1])):
            dst_ap = AP(dst.tensor, dst.offset + tt * 128,
                        [list(dst.ap[0]), [T, 8], [1, 128]])
            evs.append(p.act(lambda e, bank=bank, dst_ap=dst_ap: e.copy(
                dst_ap, ctx.bank16(bank, 1024)), after=[tk]))
        war_bank = evs[-1]
        done = evs[-1]
    return done


def emit_page_to_ft(ctx, page, ft_pair, banks=(6, 7)):
    p = ctx.p
    tl = ctx.t
    war = {0: None, 1: None}
    war_split = None
    war_bank = None
    done = None
    for tt in range(NTT):
        par = tt % 2
        xs = (tl["xa"] if par == 0 else tl["xb"])[:, :]
        lx = p.dma("sync", xs, page[tt * 128:(tt + 1) * 128, :],
                   after=[war[par]], sem="xlA" if par == 0 else "xlB")
        nh, nl = tl["nh"][:, :], tl["nl"][:, :]
        t_hi = p.vec(lambda e, xs=xs: e.tensor_copy(nh, xs), after=[lx, war_split])
        t_lo = p.vec(lambda e, xs=xs: e.tensor_sub(nl, xs, nh), after=[t_hi])
        war[par] = t_lo
        tks = []
        for (tile, bank, rdy) in ((nh, banks[0], t_hi), (nl, banks[1], t_lo)):
            tk = None
            for cc in range(8):
                tk = p.ten(lambda e, tile=tile, cc=cc, bank=bank: e.transpose(
                    ctx.bank16(bank, 128, cc * 128),
                    tile[:, cc * 128:(cc + 1) * 128], tl["idn"][:, :]),
                    after=[rdy, war_bank, ctx.ident_tok] if cc == 0 else ())
            tks.append(tk)
        war_split = tks[1]
        evs = []
        for (bank, dst, tk) in ((banks[0], ft_pair[0], tks[0]),
                                (banks[1], ft_pair[1], tks[1])):
            dst_ap = AP(dst.tensor, dst.offset + tt * 128,
                        [list(dst.ap[0]), [T, 8], [1, 128]])
            evs.append(p.act(lambda e, bank=bank, dst_ap=dst_ap: e.copy(
                dst_ap, ctx.bank16(bank, 1024)), after=[tk]))
        war_bank = evs[-1]
        done = evs[-1]
    return done


def emit_l0_attn(ctx, ft, wqT, wkT, wvT, woT, pek_dram, pev_dram,
                 x_page, out_page, ft_out):
    p = ctx.p
    tl = ctx.t
    a16 = A16(ctx)
    a32 = A32(ctx)

    qt_hi, qt_lo = a16.take(T), a16.take(T)
    kt_hi, kt_lo = a16.take(T), a16.take(T)
    vtf32 = a16.take(4096).bitcast(F32)
    v32 = a16.take(4096).bitcast(F32)
    ats = a16.take(2048).bitcast(F32)
    aos = a16.take(512)
    R = a16.take(4096)
    aorh = [R[:, 0:1024], R[:, 1024:2048]]
    aorl = [R[:, 2048:3072], R[:, 3072:4096]]
    scsb2 = R.bitcast(F32)
    qpsb2 = a16.take(2304).bitcast(F32)
    relmask = a16.take(512).bitcast(F32)
    A32t = a16.take(2304).bitcast(F32)
    wsl_hi = [a16.take(384) for _ in range(8)]
    wsl_lo = [a16.take(384) for _ in range(8)]
    pekT_hi, pekT_lo = a16.take(1152), a16.take(1152)
    pev_hi, pev_lo = a16.take(9 * 64), a16.take(9 * 64)
    AR16 = tl["AR16"]
    wo_hi = [AR16[:, c * 1024:(c + 1) * 1024] for c in range(8)]
    wo_lo = [AR16[:, 8192 + c * 1024:8192 + (c + 1) * 1024] for c in range(8)]

    scsb = a32.take(2048)
    exps = a32.take(2048)
    qpsb = a32.take(1152)
    b8sb = a32.take(256)
    wstage = a32.take(512)

    cols = tl["cols"]
    rmaxs = (cols[:, 3:4], cols[:, 8:9])
    rsums = (cols[:, 4:5], cols[:, 9:10])
    rrecs = (cols[:, 5:6], cols[:, 10:11])
    qp0s = (cols[:, 6:7], cols[:, 11:12])
    far = cols[:, 7:8]
    scsbs = (scsb, scsb2)
    qpsbs = (qpsb, qpsb2)
    aoh, aol = ctx.dram["aoh"], ctx.dram["aol"]
    qpd = [ctx.dram["qp0"], ctx.dram["qp1"]]
    bndd = [ctx.dram["bnd0"], ctx.dram["bnd1"]]
    b8d = [ctx.dram["b80"], ctx.dram["b81"]]
    mk = p.vec(lambda e: e.memset(relmask, 1.0))
    mk = p.gp(lambda e: e.affine_select(
        out=relmask[:, 0:128], in_=relmask[:, 0:128], compare_op=A.is_ge,
        fill=0.0, base=-128, pattern=[[1, 128]], channel_multiplier=1),
        after=[mk])
    mk_tok = p.gp(lambda e: e.affine_select(
        out=relmask[:, 128:256], in_=relmask[:, 128:256], compare_op=A.is_ge,
        fill=0.0, base=0, pattern=[[1, 128]], channel_multiplier=1),
        after=[mk])
    mask1 = relmask[:, 0:128]
    mask0 = relmask[:, 128:256]

    tzb = p.vec(lambda e: e.memset(scsb[:, 0:1152], 0.0))
    tza = p.vec(lambda e: e.memset(A32t[:, 1025:1152], 0.0))
    zb0 = p.dma("scalar", AP(bndd[0], 0, [[1152, 128], [1, 1152]]),
                scsb[:, 0:1152], after=[tzb], sem="bwA")
    zb1 = p.dma("scalar", AP(bndd[1], 0, [[1152, 128], [1, 1152]]),
                scsb[:, 0:1152], after=[tzb], sem="bwB")

    pvt = None
    pk_tok = None
    for cc in range(9):
        st = wstage[:, 0:64]
        ld = p.dma("sync", st, pek_dram[cc * 128:(cc + 1) * 128, :],
                   after=[pk_tok], sem="pkl")
        t1 = p.ten(lambda e, st=st: e.transpose(
            ctx.bank32(5, 128)[0:64, :], st, tl["idn32"][:, :]),
            after=[ld, pvt, ctx.ident_tok])
        t2 = p.vec(lambda e, cc=cc: e.tensor_copy(
            scsb[0:64, cc * 128:(cc + 1) * 128], ctx.bank32(5, 128)[0:64, :]),
            after=[t1, zb0, zb1])
        pvt = t2
        pk_tok = t2
    tdup = p.dma("sync", scsb[64:128, 0:1152], scsb[0:64, 0:1152],
                 after=[pvt], sem="pkd")
    tz = p.vec(lambda e: e.memset(scsb[:, 1025:1152], 0.0), after=[tdup])
    tk_pek = emit_wsplit(ctx, [(scsb[:, 0:1152], pekT_hi, pekT_lo, 0.125)],
                         after=[tz], war=None)
    pv_tok = tk_pek
    for cc in range(9):
        st = wstage[:, 0:64]
        if cc == 8:
            tz8 = p.vec(lambda e: e.memset(st, 0.0), after=[pv_tok])
            ld = p.dma("sync", st[0:1, :], pev_dram[1024:1025, :],
                       after=[tz8], sem="pkl")
        else:
            ld = p.dma("sync", st, pev_dram[cc * 128:(cc + 1) * 128, :],
                       after=[pv_tok], sem="pkl")
        pv_tok = emit_wsplit(ctx, [(st, pev_hi[:, cc * 64:(cc + 1) * 64],
                                    pev_lo[:, cc * 64:(cc + 1) * 64], 1.0)],
                             after=[ld], war=pv_tok)
    pev_tok = pv_tok
    pev32 = a16.take(1152).bitcast(F32)
    pev_tok = p.vec(lambda e: e.tensor_add(pev32, pev_hi, pev_lo), after=[pev_tok])

    def s1a(pr, qi, hx, slice_tok, war_hbuf):
        u = {"qi": qi, "hx": hx, "par": (qi * 2 + hx) % 2}
        par = u["par"]
        kw = (qi + 1) * 128
        gmax = min(qi, 7)
        hbase = hx * 64
        hslice = slice(hbase, hbase + 64)
        u["hbase"] = hbase
        mmq = None
        for g0 in range(0, 1152, 512):
            gw = min(512, 1152 - g0)
            bidx = g0 // 512
            lh = qt_hi[hslice, qi * 128:(qi + 1) * 128]
            llo = qt_lo[hslice, qi * 128:(qi + 1) * 128]
            rh = pekT_hi[hslice, g0:g0 + gw]
            rl = pekT_lo[hslice, g0:g0 + gw]
            aft = ([slice_tok, S["bank03"], tk_pek] + war_hbuf[par]) if g0 == 0 else ()
            p.emit("tensor", lambda e, lh=lh, rh=rh, bidx=bidx: e.matmul(
                ctx.bank32(bidx, rh.shape[-1]), lh, rh, start=True, stop=False),
                after=aft, inc=[])
            p.emit("tensor", lambda e, lh=lh, rl=rl, bidx=bidx: e.matmul(
                ctx.bank32(bidx, rl.shape[-1]), lh, rl, start=False, stop=False), inc=[])
            mmq = p.emit("tensor", lambda e, llo=llo, rh=rh, bidx=bidx: e.matmul(
                ctx.bank32(bidx, rh.shape[-1]), llo, rh, start=False, stop=True),
                inc=[("ts", 1)])
        tev = p.act(lambda e, par=par: e.copy(qpsbs[par], ctx.bank32(0, 1152)),
                    after=[mmq] + S["war_qpsb"][par])
        tq0 = p.vec(lambda e, par=par: e.tensor_copy(qp0s[par], qpsbs[par][:, 0:1]),
                    after=[tev])
        u["tq0"] = tq0
        wqp = p.dma("scalar", AP(qpd[par], 0, [[1152, 128], [1, 1152]]),
                    qpsbs[par], after=[tev] + war_hbuf[par],
                    sem="qpwA" if par == 0 else "qpwB")
        u["wqp"] = wqp
        mm = None
        for g0 in range(0, kw, 512):
            gw = min(512, kw - g0)
            bidx = g0 // 512
            lh = qt_hi[hslice, qi * 128:(qi + 1) * 128]
            llo = qt_lo[hslice, qi * 128:(qi + 1) * 128]
            rh = kt_hi[hslice, g0:g0 + gw]
            rl = kt_lo[hslice, g0:g0 + gw]
            aft = [tev] if g0 == 0 else ()
            p.emit("tensor", lambda e, lh=lh, rh=rh, bidx=bidx: e.matmul(
                ctx.bank32(bidx, rh.shape[-1]), lh, rh, start=True, stop=False),
                after=aft, inc=[])
            p.emit("tensor", lambda e, lh=lh, rl=rl, bidx=bidx: e.matmul(
                ctx.bank32(bidx, rl.shape[-1]), lh, rl, start=False, stop=False), inc=[])
            mm = p.emit("tensor", lambda e, llo=llo, rh=rh, bidx=bidx: e.matmul(
                ctx.bank32(bidx, rh.shape[-1]), llo, rh, start=False, stop=True),
                inc=[("ts", 1)])
        tsc = p.act(lambda e, kw=kw, par=par: e.copy(
            scsbs[par][:, 0:kw], ctx.bank32(0, kw)),
            after=[mm] + S["war_scsb"][par])
        S["bank03"] = tsc
        u["tsc"] = tsc
        base = 1024 - 128 * gmax
        slab = AP(qpd[par], base, [[1151, 128], [128, gmax + 1], [1, 128]])
        gacc = p.dma("gpsimd", scsbs[par][:, (qi - gmax) * 128:(qi + 1) * 128],
                     slab, after=[wqp, tsc],
                     sem="gacA" if par == 0 else "gacB", accum_op=A.add)
        u["toks_bias"] = [gacc]
        u["gacc"] = gacc
        u["bacc"] = None
        if qi >= 8:
            tb = p.vec(lambda e: e.memset(b8sb[:, 0:129], 0.0),
                       after=[tq0, S["war_b8sb"]] + war_hbuf[par])
            tb = p.vec(lambda e, par=par: e.tensor_single_scalar(
                b8sb[:, 0:129], b8sb[:, 0:129], qp0s[par], A.add), after=[tb])
            tb = p.vec(lambda e, par=par: e.tensor_copy(
                b8sb[:, 129:256], qpsbs[par][:, 1:128]), after=[tb, tev])
            u["tbcp"] = tb
            wb8 = p.dma("scalar", AP(b8d[par], 0, [[256, 128], [1, 256]]),
                        b8sb, after=[tb], sem="b8wA" if par == 0 else "b8wB")
            S["war_b8sb"] = wb8
            b8r = AP(b8d[par], 128, [[255, 128], [1, 128]])
            bacc = p.dma("gpsimd", scsbs[par][:, (qi - 8) * 128:(qi - 7) * 128],
                         b8r, after=[wb8, tsc],
                         sem="b8rA" if par == 0 else "b8rB", accum_op=A.add)
            u["toks_bias"].append(bacc)
            u["bacc"] = bacc
        return u

    def s1b(u, war_hbuf):
        qi, par = u["qi"], u["par"]
        kw = (qi + 1) * 128
        toks = list(u["toks_bias"])
        if qi >= 9:
            tfar = p.vec(lambda e, qi=qi, par=par: e.tensor_single_scalar(
                scsbs[par][:, 0:(qi - 8) * 128], scsbs[par][:, 0:(qi - 8) * 128],
                qp0s[par], A.add), after=[u["tq0"], u["tsc"]])
            toks.append(tfar)
        tmask = p.vec(lambda e, qi=qi, par=par: e.tensor_add(
            scsbs[par][:, qi * 128:(qi + 1) * 128],
            scsbs[par][:, qi * 128:(qi + 1) * 128],
            ctx.mdiag), after=toks + [ctx.mask_tok])
        tmx = p.vec(lambda e, kw=kw, par=par: e.reduce_max(
            rmaxs[par], scsbs[par][:, 0:kw], axis=AX.X, negate=True), after=[tmask])
        texp = p.act(lambda e, kw=kw, par=par: e.activation(
            exps[:, 0:kw], scsbs[par][:, 0:kw], AF.Exp, bias=rmaxs[par],
            accum_out=rsums[par]), after=[tmx])
        trc = p.vec(lambda e, par=par: e.reciprocal(rrecs[par], rsums[par]),
                    after=[texp])
        tat = p.vec(lambda e, kw=kw, par=par: e.tensor_single_scalar(
            scsbs[par][:, 0:kw], exps[:, 0:kw], rrecs[par], A.mult), after=[trc])
        u["tat"] = tat
        gmw = min(qi, 8)
        bw = (gmw + 1) * 128
        wofs = (8 - gmw) * 128
        wbd = p.dma("scalar",
                    AP(bndd[par], wofs, [[1152, 128], [1, bw]]),
                    scsbs[par][:, (qi - gmw) * 128:(qi + 1) * 128],
                    after=[tat] + war_hbuf[par],
                    sem="bwA" if par == 0 else "bwB")
        u["wbd"] = wbd

    def s2(u, war_aos, war_hbuf):
        qi, hx, par, hbase = u["qi"], u["hx"], u["par"], u["hbase"]
        ar = p.dma("sync", A32t[:, 1:1025],
                   AP(bndd[par], 1, [[1153, 128], [1, 1024]]),
                   after=[u["wbd"], S["war_A"]], sem="arA" if par == 0 else "arB")
        war_hbuf[par] = [u["gacc"], ar] + ([u["bacc"]] if u["bacc"] else [])
        tfs = p.vec(lambda e: e.reduce_sum(far, A32t[:, 1:1025], axis=AX.X,
                                           negate=True), after=[ar])
        tfs = p.vec(lambda e: e.tensor_single_scalar(far, far, 1.0, A.add),
                    after=[tfs])
        tfs = p.vec(lambda e: e.tensor_copy(A32t[:, 0:1], far), after=[tfs])
        for ucm, msk in ((7 - qi, mask1), (8 - qi, mask0)):
            if 0 <= ucm <= 8:
                tfs = p.vec(lambda e, ucm=ucm, msk=msk: e.tensor_mul(
                    A32t[:, ucm * 128:(ucm + 1) * 128],
                    A32t[:, ucm * 128:(ucm + 1) * 128], msk),
                    after=[tfs, mk_tok])
        uc0 = max(0, 7 - qi)
        chunks = [("s", kc) for kc in range(qi + 1)] + \
                 [("a", uc) for uc in range(uc0, 9)]
        first_pv = True
        tpv = None
        last_t1_attn = None
        for gbase in range(0, len(chunks), 4):
            grp = chunks[gbase:gbase + 4]
            gpar = (gbase // 4) % 2
            vb = (5, 6)[gpar]
            t1 = None
            for j, (knd, idx) in enumerate(grp):
                src = scsbs[par][:, idx * 128:(idx + 1) * 128] if knd == "s" \
                    else A32t[:, idx * 128:(idx + 1) * 128]
                stok = u["tat"] if knd == "s" else tfs
                aft = [stok] + ([S["war_bank56"][gpar], ctx.ident_tok] if j == 0 else [])
                t1 = p.ten(lambda e, src=src, j=j, vb=vb: e.transpose(
                    ctx.bank32(vb, 128, j * 128), src, tl["idn32"][:, :]),
                    after=aft)
                if knd == "s":
                    last_t1_attn = t1
            te = p.act(lambda e, vb=vb, gpar=gpar, n=len(grp): e.copy(
                ats[:, gpar * 512:gpar * 512 + n * 128],
                ctx.bank32(vb, n * 128)), after=[t1, S["war_ats"][gpar]])
            S["war_bank56"][gpar] = te
            for j, (knd, idx) in enumerate(grp):
                vsl = v32[:, idx * 128:(idx + 1) * 128][:, hbase:hbase + 64] \
                    if knd == "s" else pev32[:, idx * 64:(idx + 1) * 64]
                ah = ats[:, gpar * 512 + j * 128:gpar * 512 + (j + 1) * 128]
                ops = ctx.bank32(4, 128)[hbase:hbase + 64, :]
                last = (knd == "a" and idx == 8)
                aft = [te] + ([pev_tok] if knd == "a" else []) + \
                      ([S["war_pv"]] if (first_pv and hx == 0) else [])
                tpv = p.emit("tensor", lambda e, vsl=vsl, ah=ah, ops=ops,
                             st=first_pv, sp=last, tp=(0, hbase): e.matmul(
                    ops, vsl, ah, start=st, stop=sp, tile_position=tp),
                    after=aft, inc=[("ts", 1)])
                first_pv = False
            S["war_ats"][gpar] = tpv
            if any(k == "a" for k, _ in grp):
                S["war_A"] = t1
        S["war_scsb"][par] = [u["wbd"], last_t1_attn]
        S["war_qpsb"][par] = [u["wqp"]] + ([u["tbcp"]] if "tbcp" in u else [])
        S["done_pv"] = tpv
        if hx == 1:
            slot = qi % 2
            sh = aos[:, slot * 256:slot * 256 + 128]
            sl = aos[:, slot * 256 + 128:slot * 256 + 256]
            th = p.act(lambda e, sh=sh: e.copy(sh, ctx.bank32(4, 128)),
                       after=[tpv, war_aos[slot]])
            tsub = p.vec(lambda e, sh=sh, sl=sl: e.tensor_sub(
                sl, ctx.bank32(4, 128), sh), after=[th])
            S["war_pv"] = tsub
            S["done_pv"] = tsub
            ro = qi * 1024 + u["pr"] * 128
            w1 = p.dma("scalar", AP(aoh, ro * 128, [[128, 128], [1, 128]]),
                       sh, after=[tsub], sem="aowA" if slot == 0 else "aowB")
            w2 = p.dma("scalar", AP(aol, ro * 128, [[128, 128], [1, 128]]),
                       sl, after=[tsub], sem="aowA" if slot == 0 else "aowB")
            war_aos[slot] = w2

    def attention(pr, slice_tok, war_ao, war_aos, war_hbuf):
        units = [(qi, hx) for qi in range(NTT) for hx in range(2)]
        ulist = []
        u0 = s1a(pr, 0, 0, slice_tok, war_hbuf)
        u0["pr"] = pr
        s1b(u0, war_hbuf)
        ulist.append(u0)
        for i in range(len(units)):
            if i + 1 < len(units):
                qi, hx = units[i + 1]
                un = s1a(pr, qi, hx, slice_tok, war_hbuf)
                un["pr"] = pr
                ulist.append(un)
            s2(ulist[i], war_aos, war_hbuf)
            if i + 1 < len(units):
                s1b(ulist[i + 1], war_hbuf)
        return S["done_pv"]

    slice_free = pv_tok
    war_proj = (None, None)
    war_ao = None
    war_aos = {0: None, 1: None}
    war_hbuf = {0: [zb0], 1: [zb1]}
    S = {"bank03": None, "war_pv": None, "war_b8sb": None,
         "war_A": tza, "done_pv": None,
         "war_scsb": {0: [zb0, zb1, tk_pek], 1: []},
         "war_qpsb": {0: [], 1: []},
         "war_ats": {0: None, 1: None},
         "war_bank56": {0: None, 1: None}}
    for pr in range(8):
        sl_tok = None
        for c in range(8):
            for j, wsrc in enumerate((wqT, wkT, wvT)):
                src = AP(wsrc.tensor, c * 128 * 1024 + 128 * pr,
                         [[1024, 128], [1, 128]])
                ld = p.dma("sync", wstage[:, 0:128], src, after=[slice_free], sem="wsl")
                sl_tok = emit_wsplit(
                    ctx, [(wstage[:, 0:128], wsl_hi[c][:, j * 128:(j + 1) * 128],
                           wsl_lo[c][:, j * 128:(j + 1) * 128],
                           0.125 if j == 1 else 1.0)],
                    after=[ld], war=sl_tok)
                slice_free = sl_tok
        tq, _ = emit_projT(ctx, ft, wsl_hi, wsl_lo, 0, 128, qt_hi, qt_lo, (0, 1),
                           after=[sl_tok], war=war_proj[0])
        tkk, _ = emit_projT(ctx, ft, wsl_hi, wsl_lo, 128, 128, kt_hi, kt_lo, (0, 1),
                            after=[sl_tok], war=tq)
        tv, _ = emit_projT(ctx, ft, wsl_hi, wsl_lo, 256, 128, None, None, (0, 1),
                           after=[sl_tok], war=tkk, dst32=vtf32)
        slice_free = tv
        tev = None
        tev2 = {0: None, 1: None}
        for tt in range(NTT):
            slot = tt % 2
            vb = (5, 3)[slot]
            t1 = p.ten(lambda e, tt=tt, vb=vb: e.transpose(
                ctx.bank32(vb, 128), vtf32[:, tt * 128:(tt + 1) * 128],
                tl["idn32"][:, :]), after=[tv, tev2[slot], ctx.ident_tok])
            tev = p.act(lambda e, tt=tt, vb=vb: e.copy(
                v32[:, tt * 128:(tt + 1) * 128], ctx.bank32(vb, 128)),
                after=[t1])
            tev2[slot] = tev
        war_ao = attention(pr, tev, war_ao, war_aos, war_hbuf)
        war_proj = (war_ao, war_ao)

    wo_tok, wst_free = None, None
    for c in range(8):
        for j in range(2):
            ld = p.dma("sync", wstage[:, 0:512],
                       woT[c * 128:(c + 1) * 128, j * 512:(j + 1) * 512],
                       after=[wst_free, war_ao if (c == 0 and j == 0) else None],
                       sem="wsl")
            wo_tok = emit_wsplit(ctx, [(wstage[:, 0:512], wo_hi[c][:, j * 512:(j + 1) * 512],
                                        wo_lo[c][:, j * 512:(j + 1) * 512], 1.0)],
                                 after=[ld, war_ao if (c == 0 and j == 0) else None],
                                 war=wo_tok)
            wst_free = wo_tok

    ao_barrier = (p.barrier_tok("aowA"), p.barrier_tok("aowB"))
    epi = EpiPipe(ctx, x_page, out_page, ft_out, banks=(6, 7))
    wyq = [None, None]
    war_aor = {0: None, 1: None}
    for tt in range(NTT):
        par = tt % 2
        war_y = wyq[0]
        lh = p.dma("sync", aorh[par], AP(aoh, tt * 1024 * 128,
                                         [[128, 128], [16384, 8], [1, 128]]),
                   after=[war_aor[par], ao_barrier[0], ao_barrier[1]],
                   sem="aorA" if par == 0 else "aorB")
        ll = p.dma("scalar", aorl[par], AP(aol, tt * 1024 * 128,
                                           [[128, 128], [16384, 8], [1, 128]]),
                   after=[war_aor[par], ao_barrier[0], ao_barrier[1]],
                   sem="aosA" if par == 0 else "aosB")
        mm = None
        for osup in range(2):
            b = (tt % 2) * 2 + osup
            for c in range(8):
                ah = aorh[par][:, c * 128:(c + 1) * 128]
                al = aorl[par][:, c * 128:(c + 1) * 128]
                rh = wo_hi[c][:, osup * 512:(osup + 1) * 512]
                rl = wo_lo[c][:, osup * 512:(osup + 1) * 512]
                aft = [lh, ll, wo_tok, war_y] if (c == 0 and osup == 0) else ()
                p.emit("tensor", lambda e, ah=ah, rh=rh, b=b, c=c: e.matmul(
                    ctx.bank32(b, 512), ah, rh, start=(c == 0), stop=False),
                    after=aft, inc=[])
                p.emit("tensor", lambda e, ah=ah, rl=rl, b=b: e.matmul(
                    ctx.bank32(b, 512), ah, rl, start=False, stop=False), inc=[])
                mm = p.emit("tensor", lambda e, al=al, rh=rh, b=b, c=c: e.matmul(
                    ctx.bank32(b, 512), al, rh, start=False, stop=(c == 7)),
                    inc=[("ts", 1)])
        war_aor[par] = mm
        epi.tile(tt, ctx.bank32((tt % 2) * 2, 1024), mm)
        wyq = [wyq[1], epi.war_s]
    return epi


LN1E4 = math.log(10000.0)
_c64 = -LN1E4 / 512.0
C_HI = float(np.float32(_c64)); C_LO = float(np.float32(_c64 - C_HI))
_tp64 = 2.0 * math.pi
TP_HI = float(np.float32(_tp64)); TP_LO = float(np.float32(_tp64 - TP_HI))
INV2PI = float(np.float32(1.0 / _tp64))
_p264 = 0.5 * math.pi
P2_HI = float(np.float32(_p264)); P2_LO = float(np.float32(_p264 - P2_HI))
PI_CLAMP = float(np.nextafter(np.float32(math.pi), np.float32(0.0)))


def _pe_bits(chunk):
    is_cos = chunk >= 4
    fi0 = 128 * (chunk % 4)
    invf_max = math.exp(fi0 * _c64)
    tmax = 2047.0 * invf_max / _tp64 + (0.25 if is_cos else 0.0)
    if tmax < 0.45:
        return None
    b = 1
    while (1 << (b - 1)) <= tmax + 0.75:
        b += 1
    return b


def emit_pe_rt(ctx, wr_dram, rth_d, rtl_d):
    p = ctx.p
    tl = ctx.t
    a16 = A16(ctx)
    a32 = A32(ctx)
    pet_hi, pet_lo = a16.take(8 * T), a16.take(8 * T)
    scr = a16.take(8192).bitcast(F32)
    xsl = scr[:, 0:2048]
    nsl = scr[:, 2048:4096]
    tsl = a32.take(2048)
    rem = a32.take(2048)
    bit = a32.take(2048)
    cols = tl["cols"]
    fic = cols[:, 8:9]
    ivf = cols[:, 9:10]

    done = None
    for ch in range(8):
        is_cos = ch >= 4
        fi0 = 128 * (ch % 4)
        t = p.gp(lambda e, fi0=fi0: e.iota(fic, pattern=[[0, 1]], base=fi0,
                                           channel_multiplier=1,
                                           allow_small_or_imprecise_dtypes=True),
                 after=[done])
        t = p.vec(lambda e: e.tensor_single_scalar(ivf, fic, C_HI, A.mult), after=[t])
        t = p.vec(lambda e: e.scalar_tensor_tensor(ivf, fic, C_LO, ivf, A.mult, A.add), after=[t])
        t = p.act(lambda e: e.activation(ivf, ivf, AF.Exp), after=[t])
        t = p.gp(lambda e: e.iota(tsl, pattern=[[-1, 2048]], base=2047,
                                  channel_multiplier=0,
                                  allow_small_or_imprecise_dtypes=True), after=[t, done])
        t = p.vec(lambda e: e.tensor_single_scalar(xsl, tsl, ivf, A.mult), after=[t])
        if is_cos:
            t = p.vec(lambda e: e.tensor_scalar(xsl, xsl, P2_HI, P2_LO, A.add, A.add), after=[t])
        nb = _pe_bits(ch)
        if nb is not None:
            t = p.vec(lambda e: e.tensor_single_scalar(tsl, xsl, INV2PI, A.mult), after=[t])
            t = round_half_even(p, nsl, tsl, rem, bit, tsl, nbits=nb, after=[t])
            t = p.vec(lambda e: e.scalar_tensor_tensor(rem, nsl, -TP_HI, xsl, A.mult, A.add), after=[t])
            t = p.vec(lambda e: e.scalar_tensor_tensor(bit, nsl, -TP_LO, rem, A.mult, A.add), after=[t])
            t = p.vec(lambda e: e.tensor_scalar(bit, bit, -PI_CLAMP, PI_CLAMP,
                                                A.max, A.min), after=[t])
            ysl = bit
        else:
            ysl = xsl
        t = p.act(lambda e, ysl=ysl: e.activation(xsl if ysl is not xsl else tsl,
                                                  ysl, AF.Sin), after=[t])
        ssl = xsl if ysl is not xsl else tsl
        done = emit_wsplit(ctx, [(ssl, ft_chunk(pet_hi, ch), ft_chunk(pet_lo, ch), 1.0)],
                           after=[t])

    wst = a32[:, 0:128] if False else tsl[:, 0:128]
    wr_hi = rem[:, 0:512].bitcast(BF16)
    wr_lo = rem[:, 512:1024].bitcast(BF16)
    sth = bit[:, 0:256].bitcast(BF16)
    stl = bit[:, 256:512].bitcast(BF16)
    wdone = None
    prev_split = done
    for sl in range(8):
        sl_tok = None
        for c in range(8):
            src = AP(wr_dram.tensor, c * 128 * 1024 + 128 * sl, [[1024, 128], [1, 128]])
            ld = p.dma("sync", wst, src, after=[sl_tok if sl_tok else prev_split,
                                               wdone], sem="wsl")
            sl_tok = emit_wsplit(ctx, [(wst, wr_hi[:, c * 128:(c + 1) * 128],
                                        wr_lo[:, c * 128:(c + 1) * 128], 1.0)],
                                 after=[ld], war=sl_tok)
        for ts in range(4):
            b = ts % 2
            mm = None
            for c in range(8):
                lh = wr_hi[:, c * 128:(c + 1) * 128]
                ll = wr_lo[:, c * 128:(c + 1) * 128]
                rh = ft_chunk(pet_hi, c, ts * 512, 512)
                rl = ft_chunk(pet_lo, c, ts * 512, 512)
                aft = [sl_tok, wdone] if c == 0 else ()
                p.emit("tensor", lambda e, lh=lh, rh=rh, b=b, c=c: e.matmul(
                    ctx.bank32(b, 512), lh, rh, start=(c == 0), stop=False),
                    after=aft, inc=[])
                p.emit("tensor", lambda e, lh=lh, rl=rl, b=b: e.matmul(
                    ctx.bank32(b, 512), lh, rl, start=False, stop=False), inc=[])
                mm = p.emit("tensor", lambda e, ll=ll, rh=rh, b=b, c=c: e.matmul(
                    ctx.bank32(b, 512), ll, rh, start=False, stop=(c == 7)),
                    inc=[("ts", 1)])
            th = p.act(lambda e, b=b: e.activation(sth, ctx.bank32(b, 512),
                                                   AF.Copy, scale=0.125), after=[mm])
            t2 = p.vec(lambda e, b=b: e.scalar_tensor_tensor(
                stl, ctx.bank32(b, 512), 0.125, sth, A.mult, A.subtract), after=[th])
            o1 = p.dma("scalar", AP(rth_d, sl * 128 * 2048 + ts * 512,
                                    [[2048, 128], [1, 512]]), sth,
                       after=[t2], sem="rtw")
            o2 = p.dma("scalar", AP(rtl_d, sl * 128 * 2048 + ts * 512,
                                    [[2048, 128], [1, 512]]), stl,
                       after=[t2], sem="rtw")
            wdone = o2
        prev_split = sl_tok
    return p.barrier_tok("rtw"), p.barrier_tok("vs")

def emit_l1_ffn(ctx, ft, W, x_page, pages, out_page, start=None):
    p = ctx.p
    tl = ctx.t
    cols = tl["cols"]
    aoh, aol = ctx.dram["aoh"], ctx.dram["aol"]
    bndd = [ctx.dram["bnd0"], ctx.dram["bnd1"]]

    rt_done, pe_vs = emit_pe_rt(ctx, W["Wr"], pages["rth"], pages["rtl"])

    a32 = A32(ctx)
    bdsb = a32.take(2304)
    scsb = bdsb
    exps = a32.take(2048)
    wstage = a32.take(1024)
    gbg = a32.take(512)
    gbb = a32.take(256)
    tzb = p.vec(lambda e: e.memset(bdsb[:, 0:2176], 0.0), after=[pe_vs])
    zb0 = p.dma("scalar", AP(bndd[0], 0, [[2176, 128], [1, 2176]]),
                bdsb[:, 0:2176], after=[tzb], sem="bwA")
    zb1 = p.dma("scalar", AP(bndd[1], 0, [[2176, 128], [1, 2176]]),
                bdsb[:, 0:2176], after=[tzb], sem="bwB")

    a16 = A16(ctx)
    qu_hi, qu_lo = a16.take(T), a16.take(T)
    qv_hi, qv_lo = a16.take(T), a16.take(T)
    kt_hi, kt_lo = a16.take(T), a16.take(T)
    rt_hi, rt_lo = a16.take(T), a16.take(T)
    vtf32 = a16.take(4096).bitcast(F32)
    v32 = a16.take(4096).bitcast(F32)
    ats = a16.take(2048).bitcast(F32)
    aos = a16.take(512)
    aorh = [a16.take(1024) for _ in range(2)]
    aorl = [a16.take(1024) for _ in range(2)]
    wsl_hi = [a16.take(384) for _ in range(8)]
    wsl_lo = [a16.take(384) for _ in range(8)]
    AR16 = tl["AR16"]
    wbig_hi = [AR16[:, c * 1024:(c + 1) * 1024] for c in range(8)]
    wbig_lo = [AR16[:, 8192 + c * 1024:8192 + (c + 1) * 1024] for c in range(8)]

    rmaxs = (cols[:, 3:4], cols[:, 8:9])
    rsums = (cols[:, 4:5], cols[:, 9:10])
    rrecs = (cols[:, 5:6], cols[:, 12:13])
    ubc, vbc = cols[:, 10:11], cols[:, 11:12]
    bdsb2 = a16.take(4352).bitcast(F32)
    bdsbs = (bdsb, bdsb2)

    def qkv_slices(pr, prev):
        sl_tok = None
        free = prev
        for c in range(8):
            src = AP(W["Wqkv"].tensor, c * 128 * 3072 + 128 * pr,
                     [[3072, 128], [1024, 3], [1, 128]])
            ld = p.dma("sync", wstage[:, 0:384], src, after=[free], sem="wsl")
            sl_tok = emit_wsplit(
                ctx, [(wstage[:, 0:128], wsl_hi[c][:, 0:128], wsl_lo[c][:, 0:128], 1.0),
                      (wstage[:, 128:256], wsl_hi[c][:, 128:256], wsl_lo[c][:, 128:256], 0.125),
                      (wstage[:, 256:384], wsl_hi[c][:, 256:384], wsl_lo[c][:, 256:384], 1.0)],
                after=[ld], war=sl_tok)
            free = sl_tok
        return sl_tok

    def proj_q_biased(pr, after, war):
        ft_hi, ft_lo = ft
        ev = {0: None, 1: None}
        done = None
        for ts in range(4):
            b = ts % 2
            mm = None
            for c in range(8):
                lh = wsl_hi[c][:, 0:128]
                ll = wsl_lo[c][:, 0:128]
                rh = ft_hi[:, c * T + ts * 512: c * T + ts * 512 + 512]
                rl = ft_lo[:, c * T + ts * 512: c * T + ts * 512 + 512]
                aft = (list(after) + [war, ev[b]]) if c == 0 else ()
                p.emit("tensor", lambda e, lh=lh, rh=rh, b=b, c=c: e.matmul(
                    ctx.bank32(b, 512), lh, rh, start=(c == 0), stop=False),
                    after=aft, inc=[])
                p.emit("tensor", lambda e, lh=lh, rl=rl, b=b: e.matmul(
                    ctx.bank32(b, 512), lh, rl, start=False, stop=False), inc=[])
                mm = p.emit("tensor", lambda e, ll=ll, rh=rh, b=b, c=c: e.matmul(
                    ctx.bank32(b, 512), ll, rh, start=False, stop=(c == 7)),
                    inc=[("ts", 1)])
            ps = ctx.bank32(b, 512)
            sl_ = slice(ts * 512, ts * 512 + 512)
            t1 = p.act(lambda e, ps=ps, sl_=sl_: e.activation(
                qu_hi[:, sl_], ps, AF.Identity, bias=ubc), after=[mm])
            t2 = p.vec(lambda e, ps=ps, sl_=sl_: e.scalar_tensor_tensor(
                qu_lo[:, sl_], ps, ubc, qu_hi[:, sl_], A.add, A.subtract), after=[t1])
            t3 = p.act(lambda e, ps=ps, sl_=sl_: e.activation(
                qv_hi[:, sl_], ps, AF.Identity, bias=vbc), after=[t2])
            t4 = p.vec(lambda e, ps=ps, sl_=sl_: e.scalar_tensor_tensor(
                qv_lo[:, sl_], ps, vbc, qv_hi[:, sl_], A.add, A.subtract), after=[t3])
            ev[b] = t4
            done = t4
        return done

    def s1a(pr, qi, hx, slice_tok, war_hbuf):
        u = {"qi": qi, "hx": hx, "par": (qi * 2 + hx) % 2, "pr": pr}
        par = u["par"]
        kw = (qi + 1) * 128
        hbase = hx * 64
        hslice = slice(hbase, hbase + 64)
        u["hbase"] = hbase
        jlo = max(0, 1920 - 128 * qi)
        w = 2048 - jlo
        mm = None
        first = True
        for g0 in range(0, w, 512):
            gw = min(512, w - g0)
            bidx = g0 // 512
            lh = qv_hi[hslice, qi * 128:(qi + 1) * 128]
            ll = qv_lo[hslice, qi * 128:(qi + 1) * 128]
            rh = rt_hi[hslice, jlo + g0: jlo + g0 + gw]
            rl = rt_lo[hslice, jlo + g0: jlo + g0 + gw]
            aft = (list(slice_tok) + [S["bank03"]] + war_hbuf[par]) if first else ()
            p.emit("tensor", lambda e, lh=lh, rh=rh, bidx=bidx: e.matmul(
                ctx.bank32(bidx, rh.shape[-1]), lh, rh, start=True, stop=False),
                after=aft, inc=[])
            p.emit("tensor", lambda e, lh=lh, rl=rl, bidx=bidx: e.matmul(
                ctx.bank32(bidx, rl.shape[-1]), lh, rl, start=False, stop=False), inc=[])
            mm = p.emit("tensor", lambda e, ll=ll, rh=rh, bidx=bidx: e.matmul(
                ctx.bank32(bidx, rh.shape[-1]), ll, rh, start=False, stop=True),
                inc=[("ts", 1)])
            first = False
        tbd = p.act(lambda e, w=w, par=par: e.copy(bdsbs[par][:, 0:w],
                                                   ctx.bank32(0, w)),
                    after=[mm] + S["war_bd"][par])
        wbd = p.dma("scalar", AP(bndd[par], jlo, [[2176, 128], [1, w]]),
                    bdsbs[par][:, 0:w], after=[tbd] + war_hbuf[par],
                    sem="bwA" if par == 0 else "bwB")
        mm = None
        for g0 in range(0, kw, 512):
            gw = min(512, kw - g0)
            bidx = g0 // 512
            lh = qu_hi[hslice, qi * 128:(qi + 1) * 128]
            ll = qu_lo[hslice, qi * 128:(qi + 1) * 128]
            rh = kt_hi[hslice, g0:g0 + gw]
            rl = kt_lo[hslice, g0:g0 + gw]
            aft = [tbd] if g0 == 0 else ()
            p.emit("tensor", lambda e, lh=lh, rh=rh, bidx=bidx: e.matmul(
                ctx.bank32(bidx, rh.shape[-1]), lh, rh, start=True, stop=False),
                after=aft, inc=[])
            p.emit("tensor", lambda e, lh=lh, rl=rl, bidx=bidx: e.matmul(
                ctx.bank32(bidx, rl.shape[-1]), lh, rl, start=False, stop=False), inc=[])
            mm = p.emit("tensor", lambda e, ll=ll, rh=rh, bidx=bidx: e.matmul(
                ctx.bank32(bidx, rh.shape[-1]), ll, rh, start=False, stop=True),
                inc=[("ts", 1)])
        tsc = p.act(lambda e, kw=kw, par=par: e.copy(
            bdsbs[par][:, 0:kw], ctx.bank32(0, kw)), after=[mm, wbd])
        S["bank03"] = tsc
        base0 = 2047 - 128 * qi
        slab = AP(bndd[par], base0, [[2175, 128], [128, qi + 1], [1, 128]])
        gacc = p.dma("gpsimd", bdsbs[par][:, 0:kw], slab,
                     after=[wbd, tsc], sem="gacA" if par == 0 else "gacB",
                     accum_op=A.add)
        u["gacc"] = gacc
        u["wbd"] = wbd
        return u

    def s1b(u):
        qi, par = u["qi"], u["par"]
        kw = (qi + 1) * 128
        tmask = p.vec(lambda e, qi=qi, par=par: e.tensor_add(
            bdsbs[par][:, qi * 128:(qi + 1) * 128],
            bdsbs[par][:, qi * 128:(qi + 1) * 128],
            ctx.mdiag), after=[u["gacc"], ctx.mask_tok])
        tmx = p.vec(lambda e, kw=kw, par=par: e.reduce_max(
            rmaxs[par], bdsbs[par][:, 0:kw], axis=AX.X, negate=True),
            after=[tmask])
        texp = p.act(lambda e, kw=kw, par=par: e.activation(
            bdsbs[par][:, 0:kw], bdsbs[par][:, 0:kw], AF.Exp, bias=rmaxs[par],
            accum_out=rsums[par]), after=[tmx])
        trc = p.vec(lambda e, par=par: e.reciprocal(rrecs[par], rsums[par]),
                    after=[texp])
        tat = p.vec(lambda e, kw=kw, par=par: e.tensor_single_scalar(
            bdsbs[par][:, 0:kw], bdsbs[par][:, 0:kw], rrecs[par], A.mult),
            after=[trc])
        u["tat"] = tat

    def s2(u, war_aos, war_hbuf):
        qi, hx, par, hbase = u["qi"], u["hx"], u["par"], u["hbase"]
        war_hbuf[par] = [u["gacc"]]
        first_pv = True
        tpv = None
        last_t1 = None
        for gbase in range(0, qi + 1, 4):
            n = min(4, qi + 1 - gbase)
            gpar = (gbase // 4) % 2
            vb = (5, 6)[gpar]
            t1 = None
            for j in range(n):
                kc = gbase + j
                aft = [u["tat"]] + ([S["war_bank56"][gpar], ctx.ident_tok]
                                    if j == 0 else [])
                t1 = p.ten(lambda e, kc=kc, j=j, vb=vb, par=par: e.transpose(
                    ctx.bank32(vb, 128, j * 128),
                    bdsbs[par][:, kc * 128:(kc + 1) * 128], tl["idn32"][:, :]),
                    after=aft)
            last_t1 = t1
            te = p.act(lambda e, vb=vb, gpar=gpar, n=n: e.copy(
                ats[:, gpar * 512:gpar * 512 + n * 128],
                ctx.bank32(vb, n * 128)), after=[t1, S["war_ats"][gpar]])
            S["war_bank56"][gpar] = te
            for j in range(n):
                kc = gbase + j
                vsl = v32[:, kc * 128:(kc + 1) * 128][:, hbase:hbase + 64]
                ah = ats[:, gpar * 512 + j * 128:gpar * 512 + (j + 1) * 128]
                ops = ctx.bank32(4, 128)[hbase:hbase + 64, :]
                aft = [te] + ([S["war_pv"]] if (first_pv and hx == 0) else [])
                tpv = p.emit("tensor", lambda e, vsl=vsl, ah=ah, ops=ops,
                             st=first_pv, sp=(kc == qi), tp=(0, hbase): e.matmul(
                    ops, vsl, ah, start=st, stop=sp, tile_position=tp),
                    after=aft, inc=[("ts", 1)])
                first_pv = False
            S["war_ats"][gpar] = tpv
        S["war_bd"][par] = [last_t1]
        S["done_pv"] = tpv
        if hx == 1:
            slot = qi % 2
            sh = aos[:, slot * 256:slot * 256 + 128]
            sl = aos[:, slot * 256 + 128:slot * 256 + 256]
            th = p.act(lambda e, sh=sh: e.copy(sh, ctx.bank32(4, 128)),
                       after=[tpv, war_aos[slot]])
            tsub = p.vec(lambda e, sh=sh, sl=sl: e.tensor_sub(
                sl, ctx.bank32(4, 128), sh), after=[th])
            S["war_pv"] = tsub
            S["done_pv"] = tsub
            ro = qi * 1024 + u["pr"] * 128
            w1 = p.dma("scalar", AP(aoh, ro * 128, [[128, 128], [1, 128]]),
                       sh, after=[tsub], sem="aowA" if slot == 0 else "aowB")
            w2 = p.dma("scalar", AP(aol, ro * 128, [[128, 128], [1, 128]]),
                       sl, after=[tsub], sem="aowA" if slot == 0 else "aowB")
            war_aos[slot] = w2

    def attention(pr, slice_tok, war_ao, war_aos, war_hbuf):
        units = [(qi, hx) for qi in range(NTT) for hx in range(2)]
        ulist = [s1a(pr, 0, 0, slice_tok, war_hbuf)]
        s1b(ulist[0])
        for i in range(len(units)):
            if i + 1 < len(units):
                qi, hx = units[i + 1]
                ulist.append(s1a(pr, qi, hx, slice_tok, war_hbuf))
            s2(ulist[i], war_aos, war_hbuf)
            if i + 1 < len(units):
                s1b(ulist[i + 1])
        return S["done_pv"]

    slice_free = rt_done
    war_proj = (pe_vs, pe_vs)
    war_ao = None
    war_aos = {0: None, 1: None}
    war_hbuf = {0: [zb0, zb1], 1: [zb0, zb1]}
    S = {"bank03": None, "war_pv": None, "done_pv": None,
         "war_bd": {0: [zb0, zb1], 1: []},
         "war_ats": {0: None, 1: None},
         "war_bank56": {0: None, 1: None}}
    war_rt = None
    for pr in range(8):
        tub = p.dma("sync", ubc, AP(W["u_bias"].tensor, 128 * pr, [[1, 128], [1, 1]]),
                    after=[war_proj[0]], sem="ubl")
        tvb = p.dma("sync", vbc, AP(W["v_bias"].tensor, 128 * pr, [[1, 128], [1, 1]]),
                    after=[war_proj[0]], sem="vbl")
        sl_tok = qkv_slices(pr, slice_free)
        tq = proj_q_biased(pr, [sl_tok, tub, tvb], war_proj[0])
        tkk, _ = emit_projT(ctx, ft, wsl_hi, wsl_lo, 128, 128, kt_hi, kt_lo, (0, 1),
                            after=[sl_tok], war=tq)
        tv, _ = emit_projT(ctx, ft, wsl_hi, wsl_lo, 256, 128, None, None, (0, 1),
                           after=[sl_tok], war=tkk, dst32=vtf32)
        slice_free = tv
        lrh = p.dma("sync", rt_hi, AP(pages["rth"], pr * 128 * 2048, [[2048, 128], [1, 2048]]),
                    after=[rt_done, war_rt], sem="rtlA")
        lrl = p.dma("sync", rt_lo, AP(pages["rtl"], pr * 128 * 2048, [[2048, 128], [1, 2048]]),
                    after=[rt_done, war_rt], sem="rtlB")
        tev = None
        tev2 = {0: None, 1: None}
        for tt in range(NTT):
            slot = tt % 2
            vb = (5, 3)[slot]
            t1 = p.ten(lambda e, tt=tt, vb=vb: e.transpose(
                ctx.bank32(vb, 128), vtf32[:, tt * 128:(tt + 1) * 128],
                tl["idn32"][:, :]), after=[tv, tev2[slot], ctx.ident_tok])
            tev = p.act(lambda e, tt=tt, vb=vb: e.copy(
                v32[:, tt * 128:(tt + 1) * 128], ctx.bank32(vb, 128)),
                after=[t1])
            tev2[slot] = tev
        war_ao = attention(pr, [tev, lrh, lrl, tq], war_ao, war_aos, war_hbuf)
        war_rt = war_ao
        war_proj = (war_ao, war_ao)

    def stream_wbig(wdram, after_tok):
        wt, wf = None, None
        for c in range(8):
            for j in range(2):
                ld = p.dma("sync", wstage[:, 0:512],
                           wdram[c * 128:(c + 1) * 128, j * 512:(j + 1) * 512],
                           after=[wf, after_tok if (c == 0 and j == 0) else None],
                           sem="wsl")
                wt = emit_wsplit(ctx, [(wstage[:, 0:512],
                                        wbig_hi[c][:, j * 512:(j + 1) * 512],
                                        wbig_lo[c][:, j * 512:(j + 1) * 512], 1.0)],
                                 after=[ld, after_tok if (c == 0 and j == 0) else None],
                                 war=wt)
                wf = wt
        return wt

    gb_g = bdsb[:, 0:1024]
    gb_b = bdsb[:, 1024:2048]

    def load_gb(gname, bname, after_toks):
        tok = None
        for (nm, dst) in ((gname, gb_g), (bname, gb_b)):
            tg0 = p.dma("sync", wstage[0:1, 0:1024], W[nm],
                        after=list(after_toks) + [tok], sem="gbl")
            for j in range(2):
                tm = p.ten(lambda e, j=j: e.matmul(
                    ctx.bank32(4, 512), ctx.ones_row,
                    wstage[0:1, j * 512:(j + 1) * 512], start=True, stop=True),
                    after=[tg0, ctx.ones_tok, tok])
                tok = p.vec(lambda e, j=j, dst=dst: e.tensor_copy(
                    dst[:, j * 512:(j + 1) * 512], ctx.bank32(4, 512)), after=[tm])
        return tok

    def big_proj_epi(epi, rd_hi_fn, rd_lo_fn, wp_tok, extra=None):
        war_y = None
        for tt in range(NTT):
            lh_t, ll_t, ah_f, al_f = rd_hi_fn(tt)
            mm = None
            for osup in range(2):
                b = (tt % 2) * 2 + osup
                for c in range(8):
                    ah = ah_f(c)
                    al = al_f(c)
                    rh = wbig_hi[c][:, osup * 512:(osup + 1) * 512]
                    rl = wbig_lo[c][:, osup * 512:(osup + 1) * 512]
                    aft = [lh_t, ll_t, wp_tok, war_y] if (c == 0 and osup == 0) else ()
                    p.emit("tensor", lambda e, ah=ah, rh=rh, b=b, c=c: e.matmul(
                        ctx.bank32(b, 512), ah, rh, start=(c == 0), stop=False),
                        after=aft, inc=[])
                    p.emit("tensor", lambda e, ah=ah, rl=rl, b=b: e.matmul(
                        ctx.bank32(b, 512), ah, rl, start=False, stop=False), inc=[])
                    mm = p.emit("tensor", lambda e, al=al, rh=rh, b=b, c=c: e.matmul(
                        ctx.bank32(b, 512), al, rh, start=False, stop=(c == 7)),
                        inc=[("ts", 1)])
            if extra is not None:
                mm = extra(tt, mm)
            epi.tile(tt, ctx.bank32((tt % 2) * 2, 1024), mm)
            war_y = epi.war_s
        return epi

    wo_tok = stream_wbig(W["Wo"], war_ao)
    gb_tok = load_gb("ln1_g", "ln1_b", [war_ao, wo_tok])
    ao_barrier = (p.barrier_tok("aowA"), p.barrier_tok("aowB"))
    h1_page = AP(pages["h1"], 0, [[1024, 2048], [1, 1024]])
    epi1 = EpiPipe(ctx, x_page, h1_page, ft, banks=(6, 7), gb=(gb_g, gb_b))
    war_aor = {0: None, 1: None}

    def mk_rd(tt):
        par = tt % 2
        lh = p.dma("sync", aorh[par], AP(aoh, tt * 1024 * 128,
                                         [[128, 128], [16384, 8], [1, 128]]),
                   after=[war_aor[par], ao_barrier[0], ao_barrier[1]],
                   sem="aorA" if par == 0 else "aorB")
        ll = p.dma("scalar", aorl[par], AP(aol, tt * 1024 * 128,
                                           [[128, 128], [16384, 8], [1, 128]]),
                   after=[war_aor[par], ao_barrier[0], ao_barrier[1]],
                   sem="aosA" if par == 0 else "aosB")
        return (lh, ll,
                lambda c, par=par: aorh[par][:, c * 128:(c + 1) * 128],
                lambda c, par=par: aorl[par][:, c * 128:(c + 1) * 128])

    wyq = [None, None]
    for tt in range(NTT):
        par = tt % 2
        war_y = wyq[0]
        lh_t, ll_t, ah_f, al_f = mk_rd(tt)
        mm = None
        for osup in range(2):
            b = (tt % 2) * 2 + osup
            for c in range(8):
                ah = ah_f(c); al = al_f(c)
                rh = wbig_hi[c][:, osup * 512:(osup + 1) * 512]
                rl = wbig_lo[c][:, osup * 512:(osup + 1) * 512]
                aft = [lh_t, ll_t, wo_tok, gb_tok, war_y] if (c == 0 and osup == 0) else ()
                p.emit("tensor", lambda e, ah=ah, rh=rh, b=b, c=c: e.matmul(
                    ctx.bank32(b, 512), ah, rh, start=(c == 0), stop=False),
                    after=aft, inc=[])
                p.emit("tensor", lambda e, ah=ah, rl=rl, b=b: e.matmul(
                    ctx.bank32(b, 512), ah, rl, start=False, stop=False), inc=[])
                mm = p.emit("tensor", lambda e, al=al, rh=rh, b=b, c=c: e.matmul(
                    ctx.bank32(b, 512), al, rh, start=False, stop=(c == 7)),
                    inc=[("ts", 1)])
        war_aor[par] = mm
        epi1.tile(tt, ctx.bank32((tt % 2) * 2, 1024), mm)
        wyq = [wyq[1], epi1.war_s]
    h1_done = p.barrier_tok("dmo")
    ft_done = epi1.war_bank

    a16b = A16(ctx)
    f1h = [a16b.take(512) for _ in range(32)]
    f1l = [a16b.take(512) for _ in range(32)]
    wf1t = [(a16b.take(1024), a16b.take(1024)) for _ in range(2)]
    wf1h_t, wf1l_t = wf1t[0]
    wf1h = [wf1h_t[:, c * 128:(c + 1) * 128] for c in range(8)]
    wf1l = [wf1l_t[:, c * 128:(c + 1) * 128] for c in range(8)]
    wf2t = [(a16b.take(1024), a16b.take(1024)) for _ in range(2)]
    wf2h, wf2l = wf2t[0]
    bf2b = exps[:, 0:1024]
    relu_tmp = gbg
    gb2_tok = load_gb("ln2_g", "ln2_b", [h1_done, ft_done, gb_tok])
    bcol_tok = None
    for fs in range(32):
        bcol_tok = p.dma("sync", cols[:, 16 + fs:17 + fs],
                         AP(W["bff1"].tensor, 128 * fs, [[1, 128], [1, 1]]),
                         after=[bcol_tok, war_ao if fs == 0 else None], sem="bcl")
    tb2 = p.dma("sync", wstage[0:1, 0:1024], W["bff2"], after=[gb2_tok], sem="gbl")
    bf2b_tok = None
    for j in range(2):
        tm = p.ten(lambda e, j=j: e.matmul(ctx.bank32(0, 512), ctx.ones_row,
                                           wstage[0:1, j * 512:(j + 1) * 512],
                                           start=True, stop=True),
                   after=[tb2, ctx.ones_tok, bf2b_tok])
        bf2b_tok = p.vec(lambda e, j=j: e.tensor_copy(
            bf2b[:, j * 512:(j + 1) * 512], ctx.bank32(0, 512)), after=[tm])

    wf1c_h, wf1c_l = pages["wf1h"], pages["wf1l"]
    wf2c_h, wf2c_l = pages["wf2h"], pages["wf2l"]
    cache_toks = []
    sps = (wf1t[0], wf1t[1])
    prev_split = bf2b_tok
    store_war = {0: None, 1: None}
    kci = 0
    jobs = [("f1", dc, pc) for dc in range(8) for pc in range(4)] + \
           [("f2", fc, 0) for fc in range(32)]
    for knd, i0, i1 in jobs:
        par2 = kci % 2
        kci += 1
        sph_, spl_ = sps[par2]
        if knd == "f1":
            src = AP(W["Wff1"].tensor, i0 * 128 * 4096 + i1 * 1024,
                     [[4096, 128], [1, 1024]])
        else:
            src = AP(W["Wff2"].tensor, 128 * i0 * 1024, [[1024, 128], [1, 1024]])
        ld = p.dma("sync", wstage[:, 0:1024], src, after=[prev_split], sem="wsl")
        st = emit_wsplit(ctx, [(wstage[:, 0:1024], sph_, spl_, 1.0)],
                         after=[ld], war=store_war[par2])
        prev_split = st
        w = st
        if knd == "f1":
            dsts = ((sph_, wf1c_h), (spl_, wf1c_l))
            dap = lambda dst, i0=i0, i1=i1: AP(
                dst, i1 * 8 * 131072 + i0 * 128,
                [[1024, 128], [131072, 8], [1, 128]])
        else:
            dsts = ((sph_, wf2c_h), (spl_, wf2c_l))
            dap = lambda dst, i0=i0: AP(dst, i0 * 131072,
                                        [[1024, 128], [1, 1024]])
        for (buf, dst) in dsts:
            w = p.dma("scalar", dap(dst), buf, after=[w],
                      sem="wfcA" if par2 == 0 else "wfcB")
        cache_toks.append(w)
        store_war[par2] = w
    st = prev_split
    wsl_tok = prev_split
    cache_done = (p.barrier_tok("wfcA"), p.barrier_tok("wfcB"))

    h2_page = AP(pages["h2"], 0, [[1025, 2048], [1, 1024]])
    epi2 = EpiPipe(ctx, h1_page, h2_page, ft, banks=(6, 7), gb=(gb_g, gb_b))
    war_y = None
    war_wf1 = {0: None, 1: None}
    war_wf2 = {0: None, 1: None}
    war_f1 = None
    for sup2 in range(4):
        f1_done = None
        for fs in range(32):
            wpar = fs % 2
            wh_t, wl_t = wf1t[wpar]
            l1 = p.dma("sync", wh_t, AP(wf1c_h, fs * 131072, [[1024, 128], [1, 1024]]),
                       after=[cache_done[0], cache_done[1], war_wf1[wpar]],
                       sem="wl1A" if wpar == 0 else "wl1B")
            l2 = p.dma("scalar", wl_t,
                       AP(wf1c_l, fs * 131072, [[1024, 128], [1, 1024]]),
                       after=[cache_done[0], cache_done[1], war_wf1[wpar]],
                       sem="wl2A" if wpar == 0 else "wl2B")
            b = fs % 2
            mm = None
            for c in range(8):
                rh = tl["ft_hi"][:, c * T + sup2 * 512: c * T + sup2 * 512 + 512]
                rl = tl["ft_lo"][:, c * T + sup2 * 512: c * T + sup2 * 512 + 512]
                lh_ = wh_t[:, c * 128:(c + 1) * 128]
                ll_ = wl_t[:, c * 128:(c + 1) * 128]
                aft = [l1, l2, war_f1, bcol_tok, bf2b_tok] if c == 0 else ()
                p.emit("tensor", lambda e, c=c, rh=rh, b=b, lh_=lh_: e.matmul(
                    ctx.bank32(b, 512), lh_, rh, start=(c == 0), stop=False),
                    after=aft, inc=[])
                p.emit("tensor", lambda e, c=c, rl=rl, b=b, lh_=lh_: e.matmul(
                    ctx.bank32(b, 512), lh_, rl, start=False, stop=False), inc=[])
                mm = p.emit("tensor", lambda e, c=c, rh=rh, b=b, ll_=ll_: e.matmul(
                    ctx.bank32(b, 512), ll_, rh, start=False, stop=(c == 7)),
                    inc=[("ts", 1)])
            war_wf1[wpar] = mm
            bcol = cols[:, 16 + fs:17 + fs]
            th = p.act(lambda e, b=b, fs=fs, bcol=bcol: e.activation(
                f1h[fs], ctx.bank32(b, 512), AF.Relu, bias=bcol), after=[mm])
            tt2 = p.vec(lambda e, b=b, bcol=bcol: e.tensor_scalar(
                relu_tmp, ctx.bank32(b, 512), bcol, 0.0, A.add, A.max),
                after=[th, war_f1])
            tt3 = p.vec(lambda e, fs=fs: e.tensor_sub(f1l[fs], relu_tmp,
                                                      f1h[fs]), after=[tt2])
            war_f1 = tt3
            f1_done = tt3
        for half in range(2):
            mm_sub = {0: None, 1: None}
            for fs in range(32):
                wpar = fs % 2
                w2h, w2l = wf2t[wpar]
                l1 = p.dma("sync", w2h, AP(wf2c_h, fs * 131072, [[1024, 128], [1, 1024]]),
                           after=[cache_done[0], cache_done[1], war_wf2[wpar]],
                           sem="wm1A" if wpar == 0 else "wm1B")
                l2 = p.dma("scalar", w2l, AP(wf2c_l, fs * 131072, [[1024, 128], [1, 1024]]),
                           after=[cache_done[0], cache_done[1], war_wf2[wpar]],
                           sem="wm2A" if wpar == 0 else "wm2B")
                for sub in range(2):
                    tt = sup2 * 4 + half * 2 + sub
                    scol = (half * 2 + sub) * 128
                    for osup in range(2):
                        b2 = 2 + sub * 2 + osup
                        lh = f1h[fs][:, scol:scol + 128]
                        llx = f1l[fs][:, scol:scol + 128]
                        rh = w2h[:, osup * 512:(osup + 1) * 512]
                        rl = w2l[:, osup * 512:(osup + 1) * 512]
                        aft = [f1_done, l1, l2, war_y] if (fs == 0 and osup == 0) else ([l1, l2] if osup == 0 and sub == 0 else ())
                        p.emit("tensor", lambda e, lh=lh, rh=rh, b2=b2, fs=fs: e.matmul(
                            ctx.bank32(b2, 512), lh, rh, start=(fs == 0), stop=False),
                            after=aft, inc=[])
                        p.emit("tensor", lambda e, lh=lh, rl=rl, b2=b2: e.matmul(
                            ctx.bank32(b2, 512), lh, rl, start=False, stop=False), inc=[])
                        mm_sub[sub] = p.emit("tensor", lambda e, llx=llx, rh=rh, b2=b2,
                                             fs=fs: e.matmul(
                            ctx.bank32(b2, 512), llx, rh, start=False, stop=(fs == 31)),
                            inc=[("ts", 1)])
                war_wf2[wpar] = mm_sub[1]
            for sub in range(2):
                tt = sup2 * 4 + half * 2 + sub
                mmv = p.vec(lambda e, sub=sub: e.tensor_add(
                    tl["yst"][:, :], ctx.bank32(2 + sub * 2, 1024), bf2b),
                    after=[mm_sub[sub], bf2b_tok, war_y])
                epi2.tile(tt, tl["yst"][:, :], mmv)
                war_y = epi2.war_s
    h2_done = p.barrier_tok("dmo")
    ft2_done = epi2.war_bank
    wsl_tok = p.barrier_tok("wsl")

    wg_tok = st
    for c in range(8):
        ld = p.dma("sync", wstage[:, 0:8], W["Wg"][c * 128:(c + 1) * 128, :],
                   after=[wg_tok], sem="wsl")
        wg_tok = emit_wsplit(ctx, [(wstage[:, 0:8], wf1h[c][:, 0:8],
                                    wf1l[c][:, 0:8], 1.0)],
                             after=[ld] + ([war_wf1[0]] if c == 0 else []),
                             war=wg_tok)
    i8 = cols[:, 56:64]
    t_i8 = p.gp(lambda e: e.iota(i8, pattern=[[1, 8]], base=0, channel_multiplier=0,
                                 allow_small_or_imprecise_dtypes=True), after=[war_f1])
    on8 = cols[0:8, 65:66]
    t_o8 = p.vec(lambda e: e.memset(on8, 1.0), after=[war_f1])
    Isb = exps[0:8, 0:2048]
    Rsb = bdsb[0:8, 0:2048]
    lcp = cols[:, 48:56]
    ecol = cols[:, 64:65]
    gcol = cols[:, 66:67]
    discol = cols[:, 67:75]
    cumcol = cols[:, 75:83]
    negm = cols[:, 83:84]
    eblk = cols[:, 84:100]
    gs_done = None
    war_gb2 = epi2.war_s
    for tt in range(NTT):
        mm = None
        for c in range(8):
            rh = tl["ft_hi"][:, c * T + tt * 128: c * T + (tt + 1) * 128]
            rl = tl["ft_lo"][:, c * T + tt * 128: c * T + (tt + 1) * 128]
            aft = [wg_tok, ft2_done, gs_done, epi2.war_s] if c == 0 else ()
            p.emit("tensor", lambda e, c=c, rh=rh: e.matmul(
                ctx.bank32(4, 8), rh, wf1h[c][:, 0:8], start=(c == 0), stop=False),
                after=aft, inc=[])
            p.emit("tensor", lambda e, c=c, rl=rl: e.matmul(
                ctx.bank32(4, 8), rl, wf1h[c][:, 0:8], start=False, stop=False), inc=[])
            mm = p.emit("tensor", lambda e, c=c, rh=rh: e.matmul(
                ctx.bank32(4, 8), rh, wf1l[c][:, 0:8], start=False, stop=(c == 7)),
                inc=[("ts", 1)])
        t = p.vec(lambda e: e.tensor_copy(lcp, ctx.bank32(4, 8)), after=[mm])
        t = p.vec(lambda e: e.reduce_max(negm, lcp, axis=AX.X, negate=True), after=[t])
        t = p.vec(lambda e: e.tensor_scalar(discol, lcp, negm, 0.0, A.add, A.is_equal), after=[t])
        t = p.vec(lambda e: e.tensor_tensor_scan(cumcol, discol, discol, 0.0,
                                                 A.add, A.bypass), after=[t])
        t = p.vec(lambda e: e.tensor_single_scalar(cumcol, cumcol, 1.0, A.is_equal), after=[t])
        t = p.vec(lambda e: e.tensor_mul(discol, discol, cumcol), after=[t])
        tg = p.act(lambda e: e.activation(cumcol, lcp, AF.Exp, bias=negm,
                                          accum_out=gcol), after=[t])
        tg = p.vec(lambda e: e.reciprocal(gcol, gcol), after=[tg])
        t2 = p.vec(lambda e: e.tensor_mul(cumcol, discol, i8), after=[tg, t_i8])
        t2 = p.vec(lambda e, tt=tt: e.reduce_sum(eblk[:, tt:tt + 1], cumcol,
                                                 axis=AX.X), after=[t2])
        wgc = p.dma("scalar", AP(pages["h2"], tt * 128 * 1025 + 1024,
                                 [[1025, 128], [1, 1]]), gcol,
                    after=[tg, h2_done], sem="gwc",
                    allow_slow_non_contiguous=True)
        vb = (5, 3)[tt % 2]
        tr = p.ten(lambda e, vb=vb: e.transpose(
            ctx.bank32(vb, 128)[0:8, :], discol, tl["idn32"][:, :]),
            after=[t2, gs_done, ctx.ident_tok])
        te = p.vec(lambda e, tt=tt, vb=vb: e.tensor_copy(
            Isb[:, tt * 128:(tt + 1) * 128], ctx.bank32(vb, 128)[0:8, :]),
            after=[tr, war_gb2, wgc])
        gs_done = te
        war_gb2 = None
    t = p.vec(lambda e: e.tensor_tensor_scan(Rsb, Isb, Isb, 0.0, A.add, A.bypass),
              after=[gs_done])
    t = p.vec(lambda e: e.tensor_sub(Rsb, Rsb, Isb), after=[t])
    ncol = cols[0:8, 12:13]
    t = p.vec(lambda e: e.reduce_sum(ncol, Isb, axis=AX.X), after=[t])
    u1 = cols[0:8, 13:14]
    u2 = cols[0:8, 14:15]
    u3 = cols[0:8, 15:16]
    t = p.vec(lambda e: e.tensor_scalar(u1, ncol, 127.0, 1.0 / 128.0, A.add, A.mult), after=[t])
    t = p.vec(lambda e: e.tensor_copy(u2, u1), after=[t])
    for b in (16.0, 8.0, 4.0, 2.0, 1.0):
        t = p.vec(lambda e, b=b: e.tensor_single_scalar(u3, u2, b, A.is_ge), after=[t])
        t = p.vec(lambda e, b=b: e.scalar_tensor_tensor(u2, u3, -b, u2, A.mult, A.add), after=[t])
    t = p.vec(lambda e: e.tensor_sub(u1, u1, u2), after=[t])
    t = p.vec(lambda e: e.tensor_single_scalar(u1, u1, 128.0, A.mult), after=[t])
    tr = p.ten(lambda e: e.transpose(ctx.bank32(4, 8)[0:1, :], u1,
                                     tl["idn32"][0:8, 0:8]), after=[t, ctx.ident_tok])
    prow = cols[0:1, 228:236]
    t = p.vec(lambda e: e.tensor_copy(prow, ctx.bank32(4, 8)[0:1, :]), after=[tr])
    t = p.vec(lambda e: e.tensor_tensor_scan(prow, prow, prow, 0.0, A.add, A.bypass)
              if False else e.tensor_copy(cols[0:1, 236:244], prow), after=[t])
    t = p.vec(lambda e: e.tensor_tensor_scan(prow, cols[0:1, 236:244],
                                             cols[0:1, 236:244], 0.0, A.add, A.bypass), after=[t])
    t = p.vec(lambda e: e.memset(cols[0:1, 244:245], 0.0), after=[t])
    t = p.vec(lambda e: e.tensor_copy(cols[0:1, 245:252], prow[:, 0:7]), after=[t])
    tr = p.ten(lambda e: e.transpose(ctx.bank32(4, 8)[0:8, 0:1],
                                     cols[0:1, 244:252], tl["idn32"][0:1, 0:1]),
               after=[t, ctx.ident_tok])
    bcol8 = cols[0:8, 252:253]
    t = p.vec(lambda e: e.tensor_copy(bcol8, ctx.bank32(4, 8)[0:8, 0:1]), after=[tr])
    t = p.vec(lambda e: e.tensor_single_scalar(Rsb, Rsb, bcol8, A.add), after=[t])
    t = p.vec(lambda e: e.tensor_mul(Rsb, Rsb, Isb), after=[t])
    posrow = exps[0:1, 0:2048]
    mmp = None
    for g0 in range(0, 2048, 512):
        mmp = p.ten(lambda e, g0=g0: e.matmul(
            ctx.bank32(g0 // 512, 512)[0:1, :],
            on8, Rsb[:, g0:g0 + 512], start=True, stop=True),
            after=[t, t_o8, mmp])
    for g0 in range(0, 2048, 512):
        mmp = p.vec(lambda e, g0=g0: e.tensor_copy(
            posrow[:, g0:g0 + 512], ctx.bank32(g0 // 512, 512)[0:1, :]), after=[mmp])
    pf = tl["nh"][:, 0:23].bitcast(F32) if False else None
    t0 = p.vec(lambda e: e.memset(relu_tmp[:, 0:23], 2048.0), after=[war_f1])
    t0b = p.vec(lambda e: e.memset(relu_tmp[:, 23:46], 0.0), after=[t0])
    t0c = p.vec(lambda e: e.memset(wstage[0:1, 0:1024], 0.0), after=[t0b])
    w0 = p.dma("scalar", AP(pages["src"], 0, [[23, 128], [1, 23]]),
               relu_tmp[:, 0:23], after=[t0], sem="spf")
    w1 = p.dma("scalar", AP(pages["esrc"], 0, [[23, 128], [1, 23]]),
               relu_tmp[:, 23:46], after=[t0b], sem="spf")
    w2a = p.dma("scalar", AP(pages["h2"], 2048 * 1025, [[1024, 1], [1, 1024]]),
                wstage[0:1, 0:1024], after=[t0c, h2_done], sem="spf")
    w2b = p.dma("scalar", AP(pages["h2"], 2048 * 1025 + 1024, [[1, 1], [1, 1]]),
                wstage[0:1, 0:1], after=[t0c, h2_done], sem="spf")
    pf_done = p.barrier_tok("spf")
    sc_done = None
    icol = tl["icol"]
    poscol = cols[:, 253:254]
    tidcol = cols[:, 254:255]
    for tt in range(NTT):
        vb = (5, 3)[tt % 2]
        tr = p.ten(lambda e, tt=tt, vb=vb: e.transpose(
            ctx.bank32(vb, 1), posrow[:, tt * 128:(tt + 1) * 128],
            tl["idn32"][0:1, 0:1]), after=[mmp, sc_done, ctx.ident_tok])
        t = p.vec(lambda e, vb=vb: e.tensor_copy(poscol, ctx.bank32(vb, 1)),
                  after=[tr])
        t = p.vec(lambda e: e.tensor_copy(icol[:, 0:1], poscol), after=[t])
        tid = p.gp(lambda e, tt=tt: e.iota(tidcol, pattern=[[0, 1]], base=tt * 128,
                                           channel_multiplier=1,
                                           allow_small_or_imprecise_dtypes=True),
                   after=[sc_done])
        s1 = p.emit("gpsimd", lambda e: e.indirect_dma_start(
            AP(pages["src"], 0, [[1, 2944], [1, 1]]),
            IndirectOffsetOnAxis(icol[:, 0:1], 0), tidcol, None),
            after=[t, tid, pf_done], inc=[("sca", 16)])
        s2 = p.emit("gpsimd", lambda e, tt=tt: e.indirect_dma_start(
            AP(pages["esrc"], 0, [[1, 2944], [1, 1]]),
            IndirectOffsetOnAxis(icol[:, 0:1], 0), eblk[:, tt:tt + 1], None),
            after=[t, pf_done], inc=[("sca", 16)])
        sc_done = s2
    sort_done = p.barrier_tok("sca")

    a16c = A16(ctx)
    slotg = [a16c.take(2080).bitcast(F32) for _ in range(2)]
    h2Ts = a16c.take(2048).bitcast(F32)
    we1g = [a16c.take(2048).bitcast(F32) for _ in range(4)]
    f1rp = [a16c.take(2048).bitcast(F32) for _ in range(2)]
    ats2 = a16c.take(2048).bitcast(F32)
    we2g = [a16c.take(2048).bitcast(F32) for _ in range(4)]
    be1t = a16c.take(8192).bitcast(F32)
    be2t = a16c.take(2048).bitcast(F32)
    osb = a16c.take(2048).bitcast(F32)
    icp = a16c.take(46).bitcast(U32)
    icw1 = a16c.take(46).bitcast(U32)
    icw2 = a16c.take(46).bitcast(U32)
    icbe = a16c.take(46).bitcast(U32)
    pcol = cols[:, 239:240]
    t_pc = p.gp(lambda e: e.iota(pcol, pattern=[[0, 1]], base=0, channel_multiplier=1,
                                 allow_small_or_imprecise_dtypes=True), after=[sort_done])
    yst = tl["yst"]
    posf = yst[:, 0:23]
    w1f = yst[:, 32:55]
    w2f = yst[:, 64:87]
    e_ball = yst[:, 96:119]
    t_pos = p.dma("sync", posf, AP(pages["src"], 0, [[1, 128], [128, 23]]),
                  after=[sort_done], sem="pidx", allow_slow_non_contiguous=True)
    t_icp = p.vec(lambda e: e.tensor_copy(icp[:, :], posf), after=[t_pos])
    t_er = p.dma("sync", e_ball[0:1, :], AP(pages["esrc"], 0, [[1, 1], [128, 23]]),
                 after=[sort_done], sem="pidx2", allow_slow_non_contiguous=True)
    t_eb = p.ten(lambda e: e.matmul(ctx.bank32(4, 23), ctx.ones_row,
                                    e_ball[0:1, :], start=True, stop=True),
                 after=[t_er, ctx.ones_tok, sort_done], sem="ts")
    ebv = p.vec(lambda e: e.tensor_copy(e_ball, ctx.bank32(4, 23)), after=[t_eb])
    t_be = p.vec(lambda e: e.tensor_copy(icbe[:, :], e_ball), after=[ebv])
    t = p.vec(lambda e: e.tensor_single_scalar(w1f, e_ball, 1024.0, A.mult),
              after=[ebv, t_icp])
    t = p.vec(lambda e: e.tensor_scalar(w1f, w1f, pcol, 0.0, A.add, A.bypass),
              after=[t, t_pc])
    t_w1 = p.vec(lambda e: e.tensor_copy(icw1[:, :], w1f), after=[t])
    t = p.vec(lambda e: e.tensor_single_scalar(w2f, e_ball, 4096.0, A.mult),
              after=[ebv])
    t = p.vec(lambda e: e.tensor_scalar(w2f, w2f, pcol, 0.0, A.add, A.bypass),
              after=[t, t_pc])
    t_w2 = p.vec(lambda e: e.tensor_copy(icw2[:, :], w2f), after=[t])

    war_slot = {0: [], 1: []}
    war_h2T = None
    war_f1r = None
    war_o = None
    war_we1 = {0: None, 1: None, 2: None, 3: None}
    war_we2 = {0: None, 1: None, 2: None, 3: None}
    war_be = None
    h2flat = AP(pages["h2"], 0, [[1025, 2049], [1, 1025]])
    we1flat = AP(W["We1"].tensor, 0, [[4096, 8192], [1, 4096]])
    we2flat = AP(W["We2"].tensor, 0, [[1024, 32768], [1, 1024]])
    be1flat = AP(W["be1"].tensor, 0, [[4096, 8], [1, 4096]])
    be2flat = AP(W["be2"].tensor, 0, [[1024, 8], [1, 1024]])
    moeflat = AP(pages["moe"], 0, [[1024, 2049], [1, 1024]])
    w1sem = ("w1A", "w1B", "w1C", "w1D")
    w2sem = ("w2A", "w2B", "w2C", "w2D")
    pend_sc = None
    for sl in range(23):
        par = sl % 2
        g1 = p.emit("gpsimd", lambda e, par=par, sl=sl: e.indirect_dma_start(
            slotg[par][:, 0:1025], None, h2flat,
            IndirectOffsetOnAxis(icp[:, sl:sl + 1], 0)),
            after=[t_icp, h2_done] + war_slot[par],
            inc=[("sgA" if par == 0 else "sgB", 16)])
        be_tok = None
        th2 = None
        th2_slot = {0: war_h2T or ebv, 1: war_h2T or ebv}
        for c in range(8):
            vb = (4, 5)[c % 2]
            trp = p.ten(lambda e, c=c, vb=vb, par=par: e.transpose(
                ctx.bank32(vb, 128), slotg[par][:, c * 128:(c + 1) * 128],
                tl["idn32"][:, :]),
                after=[g1, th2_slot[c % 2], ctx.ident_tok])
            th2 = p.act(lambda e, c=c, vb=vb: e.copy(
                h2Ts[:, c * 128:(c + 1) * 128], ctx.bank32(vb, 128)), after=[trp])
            th2_slot[c % 2] = th2
        o_mm = None
        relu_prev = war_f1r
        tr_war = {0: war_h2T, 1: war_h2T}
        for fp_ in range(4):
            pb = (0, 2)[fp_ % 2]
            f1r = f1rp[fp_ % 2]
            mmf = None
            for dc in range(8):
                gidx = fp_ * 8 + dc
                wp = gidx % 4
                gw = p.emit("gpsimd", lambda e, dc=dc, fp_=fp_, wp=wp, sl=sl:
                            e.indirect_dma_start(
                    we1g[wp][:, 0:1024], None, we1flat,
                    IndirectOffsetOnAxis(icw1[:, sl:sl + 1], 0),
                    element_offset=dc * 524288 + fp_ * 1024),
                    after=[t_w1, war_we1[wp]], inc=[(w1sem[wp], 16)])
                lh = h2Ts[:, dc * 128:(dc + 1) * 128]
                for osup in range(2):
                    rh = we1g[wp][:, osup * 512:(osup + 1) * 512]
                    aft = ([gw] if osup == 0 else []) + \
                          ([th2, relu_prev] if (dc == 0 and osup == 0) else [])
                    mmf = p.emit("tensor", lambda e, lh=lh, rh=rh, pb=pb, osup=osup,
                                 dc=dc: e.matmul(
                        ctx.bank32(pb + osup, 512), lh, rh, start=(dc == 0),
                        stop=False), after=aft, inc=[("ts", 1)])
                war_we1[wp] = mmf
            if fp_ == 0:
                gb1 = p.emit("gpsimd", lambda e, sl=sl: e.indirect_dma_start(
                    be1t[0:2, 0:4096], None, be1flat,
                    IndirectOffsetOnAxis(icbe[0:2, sl:sl + 1], 0)),
                    after=[t_be, war_be], inc=[("sbe", 16)])
                gb2_ = p.emit("gpsimd", lambda e, sl=sl: e.indirect_dma_start(
                    be2t[0:2, 0:1024], None, be2flat,
                    IndirectOffsetOnAxis(icbe[0:2, sl:sl + 1], 0)),
                    after=[t_be, war_be], inc=[("sbe", 16)])
                be_tok = p.barrier_tok("sbe")
                if pend_sc is not None:
                    tev_p, sl_p = pend_sc
                    war_o = p.emit("gpsimd", lambda e, sl_p=sl_p:
                                   e.indirect_dma_start(
                        moeflat, IndirectOffsetOnAxis(icp[:, sl_p:sl_p + 1], 0),
                        osb, None), after=[tev_p], inc=[("mos", 16)])
                    pend_sc = None
            for osup in range(2):
                lb = be1t[0:1, fp_ * 1024 + osup * 512: fp_ * 1024 + (osup + 1) * 512]
                mmf = p.emit("tensor", lambda e, lb=lb, pb=pb, osup=osup: e.matmul(
                    ctx.bank32(pb + osup, 512), ctx.ones_row, lb, start=False,
                    stop=True), after=[be_tok] if osup == 0 else (), inc=[("ts", 1)])
            relu_done = None
            for osup in range(2):
                relu_done = p.act(lambda e, pb=pb, osup=osup, f1r=f1r: e.activation(
                    f1r[:, osup * 512:(osup + 1) * 512], ctx.bank32(pb + osup, 512),
                    AF.Relu), after=[mmf, relu_prev], sem="ss")
            relu_prev = relu_done
            for g4 in range(2):
                tb_ = (4, 5)[g4]
                tk = None
                for j in range(4):
                    fc = g4 * 4 + j
                    tk = p.ten(lambda e, fc=fc, j=j, tb_=tb_, f1r=f1r: e.transpose(
                        ctx.bank32(tb_, 128, j * 128),
                        f1r[:, fc * 128:(fc + 1) * 128], tl["idn32"][:, :]),
                        after=[relu_done, tr_war[g4], ctx.ident_tok] if j == 0 else ())
                te2 = p.act(lambda e, tb_=tb_, g4=g4: e.copy(
                    ats2[:, g4 * 512:(g4 + 1) * 512], ctx.bank32(tb_, 512)),
                    after=[tk], sem="ss")
                for j in range(4):
                    fc = g4 * 4 + j
                    gfc = fp_ * 8 + fc
                    wp2 = gfc % 4
                    gw2 = p.emit("gpsimd", lambda e, gfc=gfc, wp2=wp2, sl=sl:
                                 e.indirect_dma_start(
                        we2g[wp2][:, 0:1024], None, we2flat,
                        IndirectOffsetOnAxis(icw2[:, sl:sl + 1], 0),
                        element_offset=gfc * 128 * 1024),
                        after=[t_w2, war_we2[wp2]], inc=[(w2sem[wp2], 16)])
                    lh2 = ats2[:, g4 * 512 + j * 128:g4 * 512 + (j + 1) * 128]
                    for osup in range(2):
                        b2 = 6 + osup
                        rh2 = we2g[wp2][:, osup * 512:(osup + 1) * 512]
                        aft = [te2, gw2] + ([war_o] if (gfc == 0 and osup == 0) else [])
                        o_mm = p.emit("tensor", lambda e, lh2=lh2, rh2=rh2, b2=b2,
                                      gfc=gfc: e.matmul(
                            ctx.bank32(b2, 512), lh2, rh2, start=(gfc == 0),
                            stop=False), after=aft, inc=[("ts", 1)])
                    war_we2[wp2] = o_mm
                tr_war[g4] = o_mm
            war_f1r = o_mm
        for osup in range(2):
            b2 = 6 + osup
            lb = be2t[0:1, osup * 512:(osup + 1) * 512]
            o_mm = p.emit("tensor", lambda e, lb=lb, b2=b2: e.matmul(
                ctx.bank32(b2, 512)[0:128, :], ctx.ones_row, lb, start=False,
                stop=True), after=[be_tok], inc=[("ts", 1)])
        war_be = o_mm
        tev = p.act(lambda e, par=par: e.activation(
            osb, ctx.bank32(6, 1024), AF.Copy,
            scale=slotg[par][:, 1024:1025]),
            after=[o_mm, war_o], sem="ss")
        pend_sc = (tev, sl)
        war_h2T = th2
        war_slot[par] = [th2, tev]
    tev_p, sl_p = pend_sc
    moe_done = p.emit("gpsimd", lambda e, sl_p=sl_p: e.indirect_dma_start(
        moeflat, IndirectOffsetOnAxis(icp[:, sl_p:sl_p + 1], 0), osb, None),
        after=[tev_p], inc=[("mos", 16)])
    moe_barrier = p.barrier_tok("mos")

    gb3_tok = load_gb("ln3_g", "ln3_b", [mmp, sort_done])
    h3_page = AP(pages["h3"], 0, [[1024, 2048], [1, 1024]])
    epi3 = EpiPipe(ctx, h2_page, h3_page, ft, banks=(6, 7), gb=(gb_g, gb_b))
    epi3.war_bank = moe_barrier
    war_y2 = {0: None, 1: None}
    ysts = (tl["yst"][:, :], exps[:, 0:1024])
    for tt in range(NTT):
        pr6 = tt % 2
        ly = p.dma("sync", ysts[pr6],
                   AP(pages["moe"], tt * 128 * 1024, [[1024, 128], [1, 1024]]),
                   after=[moe_barrier, war_y2[pr6], gb3_tok],
                   sem="moelA" if pr6 == 0 else "moelB")
        epi3.tile(tt, ysts[pr6], ly)
        war_y2[pr6] = epi3.war_s
    h3_done = p.barrier_tok("dmo")
    ft3_done = epi3.war_bank

    wout_tok = stream_wbig(W["Wout"], moe_barrier)
    epi4 = EpiPipe(ctx, x_page, out_page, None, banks=(6, 7))
    wyq = [None, None]
    for tt in range(NTT):
        war_y = wyq[0]
        mm = None
        for osup in range(2):
            b = (tt % 2) * 2 + osup
            for c in range(8):
                lh = tl["ft_hi"][:, c * T + tt * 128: c * T + (tt + 1) * 128]
                ll = tl["ft_lo"][:, c * T + tt * 128: c * T + (tt + 1) * 128]
                rh = wbig_hi[c][:, osup * 512:(osup + 1) * 512]
                rl = wbig_lo[c][:, osup * 512:(osup + 1) * 512]
                aft = [ft3_done, h3_done, wout_tok, war_y] if (c == 0 and osup == 0) else ()
                p.emit("tensor", lambda e, lh=lh, rh=rh, b=b, c=c: e.matmul(
                    ctx.bank32(b, 512), lh, rh, start=(c == 0), stop=False),
                    after=aft, inc=[])
                p.emit("tensor", lambda e, lh=lh, rl=rl, b=b: e.matmul(
                    ctx.bank32(b, 512), lh, rl, start=False, stop=False), inc=[])
                mm = p.emit("tensor", lambda e, ll=ll, rh=rh, b=b, c=c: e.matmul(
                    ctx.bank32(b, 512), ll, rh, start=False, stop=(c == 7)),
                    inc=[("ts", 1)])
        epi4.tile(tt, ctx.bank32((tt % 2) * 2, 1024), mm)
        wyq = [wyq[1], epi4.war_s]
    return p.barrier_tok("dmo")


def emit_fence(p, sems=("dmo", "vs", "ss", "ts", "gs")):
    toks = [p.barrier_tok(s) for s in sems]
    for e in ENGINES:
        p.emit(e, lambda en: en.nop(), after=toks, inc=[("fnc", 1)])
    return toks


def build_full():
    nc = bass.Bass("TRN2", target_bir_lowering=False)
    F = K_INPUT_SHAPES
    ins = {}
    for name, shape in F.items():
        ins[name] = nc.dram_tensor(name, list(shape), F32, kind="ExternalInput").ap()
    out = nc.dram_tensor("out", [T, D], F32, kind="ExternalOutput").ap()

    dram = {"aoh": nc.dram_tensor("aoh", [16 * 8 * 128 * 128], BF16, kind="Internal"),
            "aol": nc.dram_tensor("aol", [16 * 8 * 128 * 128], BF16, kind="Internal"),
            "qp0": nc.dram_tensor("qp0", [128 * 1152], F32, kind="Internal"),
            "qp1": nc.dram_tensor("qp1", [128 * 1152], F32, kind="Internal"),
            "bnd0": nc.dram_tensor("bnd0", [128 * 2176], F32, kind="Internal"),
            "bnd1": nc.dram_tensor("bnd1", [128 * 2176], F32, kind="Internal"),
            "b80": nc.dram_tensor("b80", [128 * 256], F32, kind="Internal"),
            "b81": nc.dram_tensor("b81", [128 * 256], F32, kind="Internal")}
    pages = {"pg1": nc.dram_tensor("pg1", [T * D], F32, kind="Internal"),
             "pg2": nc.dram_tensor("pg2", [T * D], F32, kind="Internal"),
             "pg3": nc.dram_tensor("pg3", [T * D], F32, kind="Internal"),
             "h2": nc.dram_tensor("h2pg", [2049 * 1025 + 64], F32, kind="Internal"),
             "moe": nc.dram_tensor("moepg", [2049 * 1024], F32, kind="Internal"),
             "rth": nc.dram_tensor("rth", [1024 * 2048], BF16, kind="Internal"),
             "rtl": nc.dram_tensor("rtl", [1024 * 2048], BF16, kind="Internal"),
             "src": nc.dram_tensor("srcp", [2944], F32, kind="Internal"),
             "esrc": nc.dram_tensor("esrcp", [2944], F32, kind="Internal"),
             "wf1h": nc.dram_tensor("wf1ch", [1024 * 4096], BF16, kind="Internal"),
             "wf1l": nc.dram_tensor("wf1cl", [1024 * 4096], BF16, kind="Internal"),
             "wf2h": nc.dram_tensor("wf2ch", [4096 * 1024], BF16, kind="Internal"),
             "wf2l": nc.dram_tensor("wf2cl", [4096 * 1024], BF16, kind="Internal")}

    es = ExitStack()
    tiles = alloc_tiles(nc, es)
    psum = es.enter_context(nc.psum_tensor("ps", [128, 4096], F32))
    prog = Prog()
    ctx = Ctx(nc, prog, tiles, None, psum, dram)
    make_ident(ctx)
    emit_masks(ctx)
    ft = (tiles["ft_hi"][:, :], tiles["ft_lo"][:, :])
    pg1 = AP(pages["pg1"], 0, [[D, T], [1, D]])
    pg2 = AP(pages["pg2"], 0, [[D, T], [1, D]])
    pg3 = AP(pages["pg3"], 0, [[D, T], [1, D]])

    x0 = ins["L0_attn_x"]
    emit_page_to_ft(ctx, x0, ft)
    emit_fence(prog)
    emit_l0_attn(ctx, ft, ins["wqT"], ins["wkT"], ins["wvT"], ins["woT"],
                 ins["L0_attn_pek"], ins["L0_attn_pev"], x0, pg1, ft)
    emit_fence(prog)
    emit_l0_ffn(ctx, pg1, ins["l0fwT"], pg2, ft)
    emit_fence(prog)
    emit_l1_attn(ctx, ft, ins["L1_attn_w_qkv_full"], ins["L1_attn_w_qk_red"],
                 ins["L1_attn_w_v_red"], ins["L1_attn_w_proj"], pg2, pg3, ft)
    emit_fence(prog)
    Wd = {"Wqkv": ins["L1_ffn_Wqkv"], "Wo": ins["L1_ffn_Wo"], "Wr": ins["L1_ffn_Wr"],
          "u_bias": ins["u_bias"], "v_bias": ins["v_bias"],
          "ln1_g": ins["ln1_g"], "ln1_b": ins["ln1_b"],
          "Wff1": ins["L1_ffn_Wff1"], "bff1": ins["bff1"],
          "Wff2": ins["L1_ffn_Wff2"], "bff2": ins["bff2"],
          "ln2_g": ins["ln2_g"], "ln2_b": ins["ln2_b"],
          "Wg": ins["L1_ffn_Wg"], "We1": ins["We1"], "be1": ins["L1_ffn_be1"],
          "We2": ins["We2"], "be2": ins["L1_ffn_be2"],
          "ln3_g": ins["ln3_g"], "ln3_b": ins["ln3_b"], "Wout": ins["L1_ffn_Wout"]}
    l1f_pages = {"h1": pages["pg1"], "h2": pages["h2"], "h3": pages["pg2"],
                 "moe": pages["moe"], "rth": pages["rth"], "rtl": pages["rtl"],
                 "src": pages["src"], "esrc": pages["esrc"],
                 "wf1h": pages["wf1h"], "wf1l": pages["wf1l"],
                 "wf2h": pages["wf2h"], "wf2l": pages["wf2l"]}
    fin = emit_l1_ffn(ctx, ft, Wd, pg3, l1f_pages, out)
    prog.emit("sync", lambda e: e.nop(), after=[prog.barrier_tok("dmo")],
              inc=[("fnc", 1)])

    sems = {n: es.enter_context(nc.semaphore(n)) for n in prog.cnt.keys()}
    with nc.Block() as block:
        prog.replay(nc, block, sems)
    es.close()
    return nc


K_INPUT_SHAPES = {
    "L0_attn_x": (T, D),
    "wqT": (D, D), "wkT": (D, D), "wvT": (D, D), "woT": (D, D),
    "L0_attn_pek": (2049, 64), "L0_attn_pev": (2049, 64),
    "l0fwT": (D, D),
    "L1_attn_w_qkv_full": (D, 1536), "L1_attn_w_qk_red": (D, 512),
    "L1_attn_w_v_red": (D, 512), "L1_attn_w_proj": (D, D),
    "L1_ffn_Wqkv": (D, 3 * D), "L1_ffn_Wo": (D, D), "L1_ffn_Wr": (D, D),
    "u_bias": (1, 1024), "v_bias": (1, 1024),
    "ln1_g": (1, D), "ln1_b": (1, D),
    "L1_ffn_Wff1": (D, 4096), "bff1": (1, 4096),
    "L1_ffn_Wff2": (4096, D), "bff2": (1, D),
    "ln2_g": (1, D), "ln2_b": (1, D),
    "L1_ffn_Wg": (D, 8), "We1": (8 * D, 4096), "L1_ffn_be1": (8, 4096),
    "We2": (8 * 4096, D), "L1_ffn_be2": (8, D),
    "ln3_g": (1, D), "ln3_b": (1, D), "L1_ffn_Wout": (D, D),
}

_NC_CACHE = []


def kernel(**inputs):
    c = np.ascontiguousarray
    shared = {
        "wqT": c(inputs["L0_attn_Wq"].T), "wkT": c(inputs["L0_attn_Wk"].T),
        "wvT": c(inputs["L0_attn_Wv"].T), "woT": c(inputs["L0_attn_Wo"].T),
        "L0_attn_pek": c(inputs["L0_attn_pek"]),
        "L0_attn_pev": c(inputs["L0_attn_pev"]),
        "l0fwT": c(inputs["L0_ffn_weight"].T),
        "L1_attn_w_qkv_full": c(inputs["L1_attn_w_qkv_full"]),
        "L1_attn_w_qk_red": c(inputs["L1_attn_w_qk_red"]),
        "L1_attn_w_v_red": c(inputs["L1_attn_w_v_red"]),
        "L1_attn_w_proj": c(inputs["L1_attn_w_proj"]),
        "L1_ffn_Wqkv": c(inputs["L1_ffn_Wqkv"]), "L1_ffn_Wo": c(inputs["L1_ffn_Wo"]),
        "L1_ffn_Wr": c(inputs["L1_ffn_Wr"]),
        "u_bias": c(inputs["L1_ffn_u_bias"]).reshape(1, 1024),
        "v_bias": c(inputs["L1_ffn_v_bias"]).reshape(1, 1024),
        "ln1_g": c(inputs["L1_ffn_ln1_g"]).reshape(1, D),
        "ln1_b": c(inputs["L1_ffn_ln1_b"]).reshape(1, D),
        "L1_ffn_Wff1": c(inputs["L1_ffn_Wff1"]),
        "bff1": c(inputs["L1_ffn_bff1"]).reshape(1, 4096),
        "L1_ffn_Wff2": c(inputs["L1_ffn_Wff2"]),
        "bff2": c(inputs["L1_ffn_bff2"]).reshape(1, D),
        "ln2_g": c(inputs["L1_ffn_ln2_g"]).reshape(1, D),
        "ln2_b": c(inputs["L1_ffn_ln2_b"]).reshape(1, D),
        "L1_ffn_Wg": c(inputs["L1_ffn_Wg"]),
        "We1": c(inputs["L1_ffn_We1"]).reshape(8 * D, 4096),
        "L1_ffn_be1": c(inputs["L1_ffn_be1"]),
        "We2": c(inputs["L1_ffn_We2"]).reshape(8 * 4096, D),
        "L1_ffn_be2": c(inputs["L1_ffn_be2"]),
        "ln3_g": c(inputs["L1_ffn_ln3_g"]).reshape(1, D),
        "ln3_b": c(inputs["L1_ffn_ln3_b"]).reshape(1, D),
        "L1_ffn_Wout": c(inputs["L1_ffn_Wout"]),
    }
    x = np.ascontiguousarray(inputs["L0_attn_x"], dtype=np.float32)
    if not _NC_CACHE:
        _NC_CACHE.append(build_full())
    nc = _NC_CACHE[0]
    in_maps = []
    for core in range(8):
        m = dict(shared)
        m["L0_attn_x"] = c(x[core])
        in_maps.append(m)
    res = run_bass_kernel_spmd(nc, in_maps, core_ids=list(range(8)))
    return np.stack([r["out"] for r in res.results], axis=0)
